# version 10
# baseline (speedup 1.0000x reference)
import numpy as np
import concourse.bass as bass
import concourse.mybir as mybir
from concourse.bass import AP
from concourse.bass_utils import run_bass_kernel_spmd

F32 = mybir.dt.float32
BF16 = mybir.dt.bfloat16
I16 = mybir.dt.int16
I32 = mybir.dt.int32
ALU = mybir.AluOpType
AF = mybir.ActivationFunctionType
AX = mybir.AxisListType

D = 1024
T = 2048
NCH = 8
HEADS = 16
DH = 64
IFF0 = 3072
DFF1 = 4096
NSSM = 512
MAGIC = 12582912.0

VW = 4095
PITCH = 4352

ENGINES = ("tensor", "vector", "scalar", "gpsimd", "sync")
SEMS = ("T", "V", "S", "G", "Dy", "Dg", "Ds")
ENG_SEM = {"tensor": "T", "vector": "V", "scalar": "S", "gpsimd": "G"}
DMA_SEM = {"sync": "Dy", "gpsimd": "Dg", "scalar": "Ds"}


class Prog:

    def __init__(self):
        self.q = {e: [] for e in ENGINES}
        self.cnt = {s: 0 for s in SEMS}

    AUTO = ("vector", "scalar", "gpsimd")

    def do(self, eng, fn, wait=(), sig=False, dma=False):
        waits = []
        own = ENG_SEM.get(eng)
        for tok in wait:
            if tok is None:
                continue
            sn, val = tok
            if sn == own:
                assert val <= self.cnt[sn], "forward wait on own engine"
                continue
            waits.append((sn, val))
        sig_spec = None
        tok = None
        auto = (eng in Prog.AUTO) and (fn is not None) and not dma
        if auto:
            sn = ENG_SEM[eng]
            waits.append((sn, self.cnt[sn]))
            self.cnt[sn] += 1
            sig_spec = (sn, 1)
            tok = (sn, self.cnt[sn])
        elif sig:
            sn = DMA_SEM[eng] if dma else ENG_SEM[eng]
            amt = 16 if dma else 1
            self.cnt[sn] += amt
            sig_spec = (sn, amt)
            tok = (sn, self.cnt[sn])
        self.q[eng].append({"waits": waits, "fn": fn, "sig": sig_spec})
        return tok

    def barrier_tokens(self):
        return [(s, self.cnt[s]) for s in SEMS if self.cnt[s] > 0]

    def barrier(self, toks=None):
        toks = self.barrier_tokens() if toks is None else [t for t in toks if t]
        for e in ENGINES:
            own = ENG_SEM.get(e)
            need = [(sn, v) for (sn, v) in toks if sn != own]
            if need:
                self.q[e].append({"waits": need, "fn": None, "sig": None})

    def check(self):
        ptr = {e: 0 for e in ENGINES}
        cnt = {s: 0 for s in SEMS}
        progress = True
        while progress:
            progress = False
            for e in ENGINES:
                while ptr[e] < len(self.q[e]):
                    it = self.q[e][ptr[e]]
                    if all(cnt[sn] >= v for sn, v in it["waits"]):
                        if it["sig"]:
                            sn, amt = it["sig"]
                            cnt[sn] += amt
                        ptr[e] += 1
                        progress = True
                    else:
                        break
        stuck = {e: (ptr[e], len(self.q[e])) for e in ENGINES if ptr[e] < len(self.q[e])}
        if stuck:
            detail = {}
            for e in stuck:
                it = self.q[e][ptr[e]]
                detail[e] = (ptr[e], it["waits"], {s: cnt[s] for s in SEMS})
            raise RuntimeError(f"program deadlock: {detail}")

    def emit(self, nc, block, handles):
        for ename in ENGINES:
            items = self.q[ename]

            def body(eng, items=items):
                for it in items:
                    seen = {}
                    for sn, v in it["waits"]:
                        seen[sn] = max(seen.get(sn, 0), v)
                    for sn, v in seen.items():
                        eng.wait_ge(handles[sn], v)
                    ins = None
                    if it["fn"] is not None:
                        ins = it["fn"](eng)
                    if it["sig"] is not None:
                        sn, amt = it["sig"]
                        ins.then_inc(handles[sn], amt)

            getattr(block, ename)(body)


def host_bucket_onehot():
    t = np.arange(VW)
    n = (t - 2047).astype(np.int64)
    nf = np.maximum(n, 1).astype(np.float32)
    val = np.int64(16) + (
        np.log(nf / np.float32(16.0)).astype(np.float32)
        / np.float32(np.log(np.float64(8.0)))
        * np.float32(16.0)
    ).astype(np.int32)
    val = np.minimum(val, 31).astype(np.int64)
    ret = np.where(n < 16, n, val)
    bucket = np.maximum(ret, 0).astype(np.int64)
    oh = np.zeros((32, PITCH), np.float32)
    oh[bucket, t] = 1.0
    return oh


class K:
    def __init__(self, n_cores=8, dbg=()):
        self.nc = bass.Bass("TRN2", target_bir_lowering=False, debug=False)
        self.p = Prog()
        self.n_cores = n_cores
        self.dbg_names = set(dbg)
        self.dbg_outs = {}
        self.inputs = {}
        self.guards = []
        self.sb = {}

    def din(self, name, shape, dtype=F32):
        ap = self.nc.dram_tensor(name, list(shape), dtype, kind="ExternalInput").ap()
        self.inputs[name] = ap
        return ap

    def dscratch(self, name, shape, dtype=F32):
        return self.nc.dram_tensor(name, list(shape), dtype, kind="Internal").ap()

    def dout(self, name, shape, dtype=F32):
        return self.nc.dram_tensor(name, list(shape), dtype, kind="ExternalOutput").ap()

    def dbg(self, name, shape, dtype=F32):
        if name not in self.dbg_names:
            return None
        t = self.dout("dbg_" + name, shape, dtype)
        self.dbg_outs[name] = t
        return t

    _uid = 0

    def _uname(self, name):
        K._uid += 1
        return f"{name}__u{K._uid}"

    def sbuf(self, name, shape, dtype=F32):
        g = self.nc.sbuf_tensor(self._uname(name), list(shape), dtype)
        h = g.__enter__()
        self.guards.append((name, g))
        self.sb[name] = h
        return h

    def psum(self, name, shape=(128, 512), dtype=F32):
        g = self.nc.psum_tensor(self._uname(name), list(shape), dtype)
        h = g.__enter__()
        self.guards.append((name, g))
        self.sb[name] = h
        return h

    def free(self, *names):
        names = list(names)
        while names:
            nm, g = self.guards[-1]
            assert nm in names, f"free order: top is {nm}, wanted {names}"
            g.__exit__(None, None, None)
            self.guards.pop()
            self.sb.pop(nm)
            names.remove(nm)

    def dma(self, eng, out, in_, wait=()):
        return self.p.do(eng, lambda e: e.dma_start(out=out, in_=in_),
                         wait=wait, sig=True, dma=True)

    def tens(self, fn, wait=(), sig=False):
        return self.p.do("tensor", fn, wait=wait, sig=sig)

    def vec(self, fn, wait=(), sig=False):
        return self.p.do("vector", fn, wait=wait, sig=sig)

    def sca(self, fn, wait=(), sig=False):
        return self.p.do("scalar", fn, wait=wait, sig=sig)

    def gps(self, fn, wait=(), sig=False):
        return self.p.do("gpsimd", fn, wait=wait, sig=sig)

    def split_evac(self, ps, hi, lo, scale=1.0, wait=()):
        th = self.sca(lambda e, ps=ps, hi=hi, scale=scale: e.activation(
            out=hi, in_=ps, func=AF.Copy, bias=0.0, scale=float(scale)),
            wait=wait, sig=True)
        tl = self.vec(lambda e, ps=ps, hi=hi, lo=lo, scale=scale:
                      e.scalar_tensor_tensor(out=lo, in0=ps, scalar=float(scale),
                                             in1=hi, op0=ALU.mult, op1=ALU.subtract),
                      wait=list(wait) + [th], sig=True)
        return tl




def build_kernel(n_cores=8, dbg=(), phases=None):
    k = K(n_cores=n_cores, dbg=dbg)
    nc, p = k.nc, k.p
    bar = p.barrier

    x_in = k.din("x", [D, T])
    wqB = k.din("wqB", [NCH, D, 128])
    wkB = k.din("wkB", [NCH, D, 128])
    wvB = k.din("wvB", [NCH, D, 128])
    woT = k.din("woT", [D, D])
    bq = k.din("bq", [1, D]); bk = k.din("bk", [1, D])
    bv = k.din("bv", [1, D]); bo = k.din("bo", [1, D])
    relb = k.din("relb", [32, 16])
    onehotT = k.din("onehotT", [32, PITCH])
    identc = k.din("identc", [128, 128], BF16)
    ones_r = k.din("ones_r", [1, 128])
    onesall = k.din("onesall", [128, 128])
    ones_c = k.din("ones_c", [128, 1])
    halfA_c = k.din("halfA_c", [128, 1])
    halfB_c = k.din("halfB_c", [128, 1])
    onesrow_b = k.din("onesrow_b", [1, 512], BF16)
    bqcol = k.din("bqcol", [128, 8]); bkcol = k.din("bkcol", [128, 8])
    bvcol = k.din("bvcol", [128, 8]); bocol = k.din("bocol", [128, 8])
    b2col = k.din("b2col", [128, 8])
    wup = k.din("wup", [IFF0, D]); wgate = k.din("wgate", [IFF0, D])
    wdown = k.din("wdown", [D, IFF0])
    lam_re = k.din("lam_re", [128, 4]); lam_im = k.din("lam_im", [128, 4])
    stepv = k.din("stepv", [128, 4])
    CreT = k.din("CreT", [NSSM, D]); CimT = k.din("CimT", [NSSM, D])
    Dvec = k.din("Dvec", [1, D])
    w1T = k.din("w1T", [D, DFF1]); b1 = k.din("b1", [1, DFF1])
    w2T = k.din("w2T", [DFF1, D]); b2 = k.din("b2", [1, D])
    out = k.dout("out", [D, T])

    vdram = k.dscratch("vdram", [16, PITCH], BF16)
    mbias = k.dscratch("mbias", [16 * 128 * PITCH], BF16)
    sp_up = k.dscratch("sp_up", [IFF0, T])
    sp_gate = k.dscratch("sp_gate", [IFF0, T])
    sp_q = k.dscratch("sp_q", [IFF0, T])
    sp_g2k = k.dscratch("sp_g2k", [IFF0, T], I16)
    sp_upk = k.dscratch("sp_upk", [IFF0, T], I16)
    sp_oh = k.dscratch("sp_oh", [IFF0, T], BF16)
    sp_ol = k.dscratch("sp_ol", [IFF0, T], BF16)
    w1q_d = k.dscratch("w1q_d", [D, DFF1], BF16)
    w2q_d = k.dscratch("w2q_d", [DFF1, D], BF16)
    sp_bur = k.dscratch("sp_bur", [NSSM, T]); sp_bui = k.dscratch("sp_bui", [NSSM, T])
    sp_du = k.dscratch("sp_du", [D, T])
    csp_d = {nm: k.dscratch("c_" + nm, [NSSM, D], BF16)
             for nm in ("reh", "rel", "imh", "iml")}
    cc_in = k.dscratch("cc_in", [1, 16])
    cc_out = k.dscratch("cc_out", [1, 16])

    X = k.sbuf("X", [128, NCH * T])
    XH = k.sbuf("XH", [128, NCH * T], BF16)
    XL = k.sbuf("XL", [128, NCH * T], BF16)
    ident = k.sbuf("ident", [128, 128], BF16)
    onesr = k.sbuf("onesr", [1, 128])
    onesa = k.sbuf("onesa", [128, 128])
    onesc = k.sbuf("onesc", [128, 1])
    halfA = k.sbuf("halfA", [128, 1]); halfB = k.sbuf("halfB", [128, 1])
    bcols = k.sbuf("bcols", [128, 40])
    panel = k.sbuf("panel", [1, 64])
    pbc = k.sbuf("pbc", [128, 16])
    runmax = k.sbuf("runmax", [128, 4])
    ccst = k.sbuf("ccst", [1, 16])

    for c in range(NCH):
        k.dma("sync", X[:, c * T:(c + 1) * T], x_in[c * 128:(c + 1) * 128, :])
    k.dma("sync", ident[:], identc[:])
    k.dma("sync", onesr[:], ones_r[:])
    k.dma("sync", onesa[:], onesall[:])
    k.dma("sync", onesc[:], ones_c[:])
    k.dma("sync", halfA[:], halfA_c[:])
    k.dma("sync", halfB[:], halfB_c[:])
    for j, bc in enumerate((bqcol, bkcol, bvcol, bocol, b2col)):
        k.dma("sync", bcols[:, j * 8:(j + 1) * 8], bc[:])
    bar()

    S_XG, S_SX, S_FX, S_UG, S_GG, S_SU, S_FU, S_SG, S_FG, S_QG, S_SQ2, S_FQ2, \
        S_PG, S_SO, S_FO, S_G1, S_G2, S_T0, S_T1, S_T2 = range(20)

    def ps_slot(i):
        return panel[0:1, i:i + 1]

    def bcast_panel(slot, col, extra_wait=()):
        psb = k.psum("psb_bc", [128, 1])
        t1 = k.tens(lambda e: e.matmul(psb[:], onesr[:], ps_slot(slot),
                                       start=True, stop=True), wait=extra_wait, sig=True)
        t2 = k.vec(lambda e, col=col: e.tensor_copy(out=pbc[:, col:col + 1], in_=psb[:]),
                   wait=[t1], sig=True)
        self_bar = k.p.barrier()
        k.free("psb_bc")
        return t2

    def emit_ln():
        mrow = k.sbuf("mrow", [1, T]); qrow = k.sbuf("qrow", [1, T])
        inv = k.sbuf("inv", [1, T])
        mb = k.sbuf("mb", [128, T]); ib = k.sbuf("ib", [128, T])
        sqb = k.sbuf("sqb", [128, 1024])
        t2b = k.sbuf("t2b", [128, T])
        psS = [k.psum("psS0", [1, 512]), k.psum("psS1", [1, 512])]
        psQ = [k.psum("psQ0", [1, 512]), k.psum("psQ1", [1, 512])]
        psB = [k.psum("psB0", [128, 512]), k.psum("psB1", [128, 512])]
        bar()
        k.vec(lambda e: e.memset(runmax[:, 0:1], 0.0), sig=True)
        t_cs = [None] * 6; t_cq = [None] * 6
        prev_mm = [None, None]
        for tc in range(4):
            pS = psS[tc % 2]; pQ = psQ[tc % 2]
            t_smm = None
            for c in range(NCH):
                xs = X[:, c * T + tc * 512: c * T + tc * 512 + 512]
                sq = sqb[:, (c % 2) * 512:(c % 2) * 512 + 512]
                tq = k.sca(lambda e, xs=xs, sq=sq: e.activation(
                    out=sq, in_=xs, func=AF.Square, bias=0.0, scale=1.0),
                    wait=[prev_mm[c % 2]], sig=True)
                t_smm = k.tens(lambda e, xs=xs, c=c, pS=pS: e.matmul(
                    pS[:], onesc[:], xs, start=(c == 0), stop=(c == NCH - 1)),
                    wait=[t_cs[tc - 2] if (c == 0 and tc >= 2) else None],
                    sig=(c == NCH - 1))
                prev_mm[c % 2] = k.tens(lambda e, sq=sq, c=c, pQ=pQ: e.matmul(
                    pQ[:], onesc[:], sq, start=(c == 0), stop=(c == NCH - 1)),
                    wait=[tq, t_cq[tc - 2] if (c == 0 and tc >= 2) else None],
                    sig=True)
            t_cs[tc] = k.vec(lambda e, tc=tc, pS=pS: e.tensor_copy(
                out=mrow[0:1, tc * 512:(tc + 1) * 512], in_=pS[:]),
                wait=[t_smm], sig=True)
            t_cq[tc] = k.vec(lambda e, tc=tc, pQ=pQ: e.tensor_copy(
                out=qrow[0:1, tc * 512:(tc + 1) * 512], in_=pQ[:]),
                wait=[prev_mm[(NCH - 1) % 2]], sig=True)
        t_m = k.vec(lambda e: e.tensor_scalar_mul(out=mrow[:], in0=mrow[:],
                                                  scalar1=1.0 / D), sig=True)
        k.vec(lambda e: e.tensor_scalar_mul(out=qrow[:], in0=qrow[:],
                                            scalar1=1.0 / D), sig=True)
        t_i = k.sca(lambda e: e.activation(out=inv[:], in_=mrow[:], func=AF.Square,
                                           bias=0.0, scale=1.0), wait=[t_m], sig=True)
        t_q2 = k.vec(lambda e: e.tensor_sub(out=qrow[:], in0=qrow[:], in1=inv[:]),
                     wait=[t_i], sig=True)
        t_s = k.sca(lambda e: e.activation(out=qrow[:], in_=qrow[:], func=AF.Sqrt,
                                           bias=cstv[0:1, 0:1], scale=1.0),
                    wait=[t_q2], sig=True)
        t_r = k.vec(lambda e: e.reciprocal(out=inv[:], in_=qrow[:]),
                    wait=[t_s], sig=True)
        cp_tok = [None] * 10
        jobs = [(mrow, mb, tc, t_m) for tc in range(4)] + \
               [(inv, ib, tc, t_r) for tc in range(4)]
        for j, (srow, dst, tc, trdy) in enumerate(jobs):
            pB = psB[j % 2]
            tmm = k.tens(lambda e, srow=srow, tc=tc, pB=pB: e.matmul(
                pB[:], onesr[:], srow[0:1, tc * 512:(tc + 1) * 512],
                start=True, stop=True),
                wait=[trdy, cp_tok[j - 2] if j >= 2 else None], sig=True)
            cp_tok[j] = k.vec(lambda e, dst=dst, tc=tc, pB=pB: e.tensor_copy(
                out=dst[:, tc * 512:(tc + 1) * 512], in_=pB[:]),
                wait=[tmm], sig=True)
        th = None
        tmax = None
        for c in range(NCH):
            sl = slice(c * T, (c + 1) * T)
            ta = k.vec(lambda e, sl=sl: e.tensor_sub(out=t2b[:], in0=X[:, sl], in1=mb[:]),
                       wait=[th], sig=True)
            tb_ = k.vec(lambda e: e.tensor_mul(out=t2b[:], in0=t2b[:], in1=ib[:]),
                        wait=[ta], sig=True)
            tr = k.vec(lambda e: e.reduce_max(out=runmax[:, 1:2], in_=t2b[:], axis=AX.X,
                                              apply_absolute_value=True), wait=[tb_], sig=True)
            tmax = k.vec(lambda e: e.tensor_max(out=runmax[:, 0:1], in0=runmax[:, 0:1],
                                                in1=runmax[:, 1:2]), wait=[tr], sig=True)
            th = k.sca(lambda e, sl=sl: e.activation(out=XH[:, sl], in_=t2b[:],
                                                     func=AF.Copy, bias=0.0, scale=1.0),
                       wait=[tb_], sig=True)
            k.vec(lambda e, sl=sl: e.scalar_tensor_tensor(
                out=XL[:, sl], in0=t2b[:], scalar=1.0, in1=XH[:, sl],
                op0=ALU.mult, op1=ALU.subtract), wait=[th], sig=True)
        psT = k.psum("psT", [1, 128])
        tt = k.tens(lambda e: e.transpose(psT[:], runmax[:, 0:1], identfs[:]),
                    wait=[tmax], sig=True)
        k.vec(lambda e: e.reduce_max(out=ps_slot(S_XG), in_=psT[:], axis=AX.X),
              wait=[tt], sig=True)
        bar()
        k.free("psT")
        k.free("psB1", "psB0", "psQ1", "psQ0", "psS1", "psS0")
        k.free("t2b", "sqb", "ib", "mb", "inv", "qrow", "mrow")
    def emit_allreduce_max(slots):
        k.vec(lambda e: e.memset(ccst[:], 0.0), sig=True)
        bar()
        for j, s in enumerate(slots):
            k.vec(lambda e, j=j, s=s: e.tensor_copy(out=ccst[0:1, j:j + 1],
                                                    in_=ps_slot(s)), sig=True)
        bar()
        k.dma("sync", cc_in[:], ccst[:])
        bar()
        k.gps(lambda e: e.collective_compute(
            "AllReduce", ALU.max, replica_groups=[list(range(n_cores))],
            ins=[cc_in[:]], outs=[cc_out[:]]), sig=True)
        bar()
        k.dma("sync", ccst[:], cc_out[:])
        bar()
        for j, s in enumerate(slots):
            k.vec(lambda e, j=j, s=s: e.tensor_copy(out=ps_slot(s),
                                                    in_=ccst[0:1, j:j + 1]), sig=True)
        bar()

    bmax_row = k.sbuf("bmax_row", [1, 16])

    def emit_bias_prep():
        relf = k.sbuf("relf", [32, 16]); relbb = k.sbuf("relbb", [32, 16], BF16)
        ohf = k.sbuf("ohf", [32, 512]); ohb = k.sbuf("ohb", [32, 512], BF16)
        vsb = k.sbuf("vsb", [16, PITCH], BF16)
        bm16 = k.sbuf("bm16", [16, 1], BF16)
        psV = k.psum("psV", [16, 512])
        psT2 = k.psum("psT2", [1, 16], BF16)
        k.dma("sync", relf[:], relb[:])
        bar()
        k.vec(lambda e: e.tensor_copy(out=relbb[:], in_=relf[:]), sig=True)
        bar()
        for j in range(0, PITCH, 512):
            w = min(512, PITCH - j)
            k.dma("sync", ohf[0:32, 0:w], onehotT[0:32, j:j + w])
            bar()
            k.vec(lambda e, w=w: e.tensor_copy(out=ohb[0:32, 0:w], in_=ohf[0:32, 0:w]), sig=True)
            bar()
            k.tens(lambda e, w=w: e.matmul(psV[0:16, 0:w], relbb[:], ohb[0:32, 0:w],
                                           start=True, stop=True), sig=True)
            bar()
            k.sca(lambda e, j=j, w=w: e.activation(out=vsb[0:16, j:j + w],
                                                   in_=psV[0:16, 0:w], func=AF.Copy,
                                                   bias=0.0, scale=1.0), sig=True)
            bar()
        k.vec(lambda e: e.reduce_max(out=bm16[:], in_=vsb[0:16, 0:VW], axis=AX.X), sig=True)
        bar()
        k.tens(lambda e: e.transpose(psT2[:], bm16[:], ident[0:16, 0:16]), sig=True)
        bar()
        k.vec(lambda e: e.tensor_copy(out=bmax_row[:], in_=psT2[:]), sig=True)
        k.dma("sync", vdram[:], vsb[:])
        bar()
        k.dma("sync",
              AP(mbias.tensor, 0, [[128 * PITCH, 16], [PITCH + 1, 128], [1, VW]]),
              AP(vdram.tensor, 0, [[PITCH, 16], [0, 128], [1, VW]]))
        bar()
        k.free("psT2", "psV", "bm16", "vsb", "ohb", "ohf", "relbb", "relf")

    def emit_attention():
        wsp = k.sbuf("wsp", [128, 1024])
        wsh = k.sbuf("wsh", [128, 1024], BF16)
        wsl = k.sbuf("wsl", [128, 1024], BF16)
        Pht = k.sbuf("Pht", [128, 512], BF16); Plt = k.sbuf("Plt", [128, 512], BF16)
        P32t = k.sbuf("P32t", [128, 512])
        vh_t = Pht; vl_t = Plt
        qh = k.sbuf("qh", [128, T], BF16); ql = k.sbuf("ql", [128, T], BF16)
        kh_ = k.sbuf("kh_", [128, T], BF16); kl_ = k.sbuf("kl_", [128, T], BF16)
        vTh = k.sbuf("vTh", [128, 16 * 130], BF16)
        vTl = k.sbuf("vTl", [128, 16 * 130], BF16)
        VN = k.sbuf("VN", [128, 4096], BF16)
        attnh = k.sbuf("attnh", [64, 2 * T], BF16); attnl = k.sbuf("attnl", [64, 2 * T], BF16)
        rs_sb = k.sbuf("rs_sb", [128, 512])
        sqt = rs_sb
        tmp64 = P32t[0:64, :]
        negc_sb = k.sbuf("negc_sb", [128, 2])
        psc = [k.psum("psc0"), k.psum("psc1")]
        psPV = [k.psum("psPV0"), k.psum("psPV1")]
        psrb = k.psum("psrb")
        psqkv = [k.psum("psqkv0"), k.psum("psqkv1")]
        psTT = [k.psum("psTT0", [128, 128], BF16)] * 2
        psNA = psrb; psNB = psrb
        S_T = 20

        def load_wslice_blocked(wB, pr):
            src = AP(wB.tensor, pr * (D * 128),
                     [[128, 128], [128 * 128, 8], [1, 128]])
            dst = wsp[:].rearrange("p (a b) -> p a b", b=128)
            k.dma("sync", dst, src)

        def split_w(width=1024):
            bar()
            k.sca(lambda e: e.activation(out=wsh[:, 0:width], in_=wsp[:, 0:width],
                                         func=AF.Copy, bias=0.0, scale=1.0), sig=True)
            bar()
            k.vec(lambda e: e.scalar_tensor_tensor(out=wsl[:, 0:width], in0=wsp[:, 0:width],
                                                   scalar=1.0, in1=wsh[:, 0:width],
                                                   op0=ALU.mult, op1=ALU.subtract), sig=True)
            bar()

        for dh in range(NCH):
            k.vec(lambda e, dh=dh: e.tensor_scalar(
                out=X[:, dh * T:(dh + 1) * T], in0=X[:, dh * T:(dh + 1) * T],
                scalar1=bcols[:, 24 + dh: 24 + dh + 1], scalar2=None,
                op0=ALU.add), sig=True)
        bar()
        for pr in range(NCH):
            k.vec(lambda e: e.memset(panel[0:1, S_T:S_T + 8], 0.0), sig=True)
            bar()
            for pj, (wB, bn, scale) in enumerate((
                    (wqB, "bq", 0.125), (wkB, "bk", 1.0), (wvB, "bv", 1.0))):
                load_wslice_blocked(wB, pr)
                split_w()
                if pj == 2:
                    k.vec(lambda e: e.memset(vTh[:], 1.0), sig=True)
                    k.vec(lambda e: e.memset(vTl[:], 0.0), sig=True)
                t_grp = [None, None]
                t_bias = [None] * 6
                t_done = [None] * 6
                bslot = {"bq": 0, "bk": 1, "bv": 2}[bn]
                for tc in range(4):
                    ps = psqkv[tc % 2]
                    tsl = slice(tc * 512, (tc + 1) * 512)
                    for c in range(NCH):
                        lh = wsh[:, c * 128:(c + 1) * 128]
                        ll = wsl[:, c * 128:(c + 1) * 128]
                        rh = XH[:, c * T + tc * 512: c * T + tc * 512 + 512]
                        rl = XL[:, c * T + tc * 512: c * T + tc * 512 + 512]
                        k.tens(lambda e, ps=ps, lh=lh, rh=rh, first=(c == 0): e.matmul(
                            ps[:], lh, rh, start=first, stop=False),
                            wait=[t_done[tc - 2]] if c == 0 else [])
                        k.tens(lambda e, ps=ps, lh=lh, rl=rl: e.matmul(
                            ps[:], lh, rl, start=False, stop=False))
                        tg = k.tens(lambda e, ps=ps, ll=ll, rh=rh, last=(c == NCH - 1):
                                    e.matmul(ps[:], ll, rh, start=False, stop=last),
                                    sig=(c == NCH - 1))
                    t_grp[tc % 2] = tg
                    tb = k.vec(lambda e, ps=ps: e.tensor_scalar(
                        out=P32t[:], in0=ps[:],
                        scalar1=bcols[:, bslot * 8 + pr: bslot * 8 + pr + 1],
                        scalar2=None, op0=ALU.add), wait=[tg], sig=True)
                    t_bias[tc] = tb
                    if pj < 2:
                        dsth = (qh, kh_)[pj]; dstl = (ql, kl_)[pj]
                        th = k.sca(lambda e, dsth=dsth, tsl=tsl, scale=scale: e.activation(
                            out=dsth[:, tsl], in_=P32t[:], func=AF.Copy,
                            bias=0.0, scale=scale), wait=[tb], sig=True)
                        tlo = k.vec(lambda e, dsth=dsth, dstl=dstl, tsl=tsl, scale=scale:
                                    e.scalar_tensor_tensor(out=dstl[:, tsl], in0=P32t[:],
                                                           scalar=scale, in1=dsth[:, tsl],
                                                           op0=ALU.mult, op1=ALU.subtract),
                                    wait=[th], sig=True)
                        tsq = k.sca(lambda e, scale=scale: e.activation(
                            out=sqt[:], in_=P32t[:], func=AF.Square,
                            bias=0.0, scale=scale), wait=[tb], sig=True)
                        nsl = S_T + pj * 2
                        tA = k.tens(lambda e: e.matmul(psNA[0:1, :], halfA[:], sqt[:],
                                                       start=True, stop=True),
                                    wait=[tsq], sig=True)
                        trA = k.vec(lambda e: e.reduce_max(out=ps_slot(S_T + 6),
                                                           in_=psNA[0:1, :], axis=AX.X),
                                    wait=[tA], sig=True)
                        k.vec(lambda e, nsl=nsl: e.tensor_max(
                            out=ps_slot(nsl), in0=ps_slot(nsl),
                            in1=ps_slot(S_T + 6)), sig=True)
                        tB = k.tens(lambda e: e.matmul(psNB[0:1, :], halfB[:], sqt[:],
                                                       start=True, stop=True),
                                    wait=[trA], sig=True)
                        trB = k.vec(lambda e: e.reduce_max(out=ps_slot(S_T + 6),
                                                           in_=psNB[0:1, :], axis=AX.X),
                                    wait=[tB], sig=True)
                        tmx = k.vec(lambda e, nsl=nsl: e.tensor_max(
                            out=ps_slot(nsl + 1), in0=ps_slot(nsl + 1),
                            in1=ps_slot(S_T + 6)), sig=True)
                        t_done[tc] = tmx
                    else:
                        th = k.sca(lambda e: e.activation(out=vh_t[:], in_=P32t[:],
                                                          func=AF.Copy, bias=0.0,
                                                          scale=1.0), wait=[tb], sig=True)
                        tlo = k.vec(lambda e: e.scalar_tensor_tensor(
                            out=vl_t[:], in0=P32t[:], scalar=1.0, in1=vh_t[:],
                            op0=ALU.mult, op1=ALU.subtract), wait=[th], sig=True)
                        t_cp = None
                        for src_t, dst in ((vh_t, vTh), (vl_t, vTl)):
                            for tb_ in range(4):
                                blk = tc * 4 + tb_
                                pT = psTT[blk % 2]
                                t_t = k.tens(lambda e, src_t=src_t, tb_=tb_, pT=pT:
                                             e.transpose(pT[:],
                                                         src_t[:, tb_ * 128:(tb_ + 1) * 128],
                                                         ident[:]),
                                             wait=[tlo if src_t is vl_t else th,
                                                   t_cp], sig=True)
                                t_c1 = k.vec(lambda e, dst=dst, blk=blk, pT=pT: e.tensor_copy(
                                    out=dst[:, blk * 130: blk * 130 + 64],
                                    in_=pT[:, 0:64]), wait=[t_t], sig=True)
                                t_cp = k.vec(lambda e, dst=dst, blk=blk, pT=pT: e.tensor_copy(
                                    out=dst[:, blk * 130 + 65: blk * 130 + 129],
                                    in_=pT[:, 64:128]), wait=[t_c1], sig=True)
                        t_done[tc] = t_cp
                bar()
            k.sca(lambda e: e.activation(out=panel[0:1, S_T:S_T + 4],
                                         in_=panel[0:1, S_T:S_T + 4],
                                         func=AF.Sqrt, bias=0.0, scale=1.0), sig=True)
            bar()
            for hp in range(2):
                hh = 2 * pr + hp
                k.vec(lambda e, hp=hp: e.tensor_mul(
                    out=ps_slot(S_T + 4 + hp), in0=ps_slot(S_T + hp),
                    in1=ps_slot(S_T + 2 + hp)), sig=True)
                bar()
                k.vec(lambda e, hp=hp, hh=hh: e.tensor_add(
                    out=ps_slot(S_T + 4 + hp), in0=ps_slot(S_T + 4 + hp),
                    in1=bmax_row[0:1, hh:hh + 1]), sig=True)
                bar()
                k.vec(lambda e, hp=hp: e.tensor_scalar(
                    out=ps_slot(S_T + 4 + hp), in0=ps_slot(S_T + 4 + hp),
                    scalar1=0.05, scalar2=-1.0, op0=ALU.add, op1=ALU.mult), sig=True)
                bar()
            k.tens(lambda e: e.matmul(psrb[:, 0:2], onesr[:],
                                      panel[0:1, S_T + 4:S_T + 6],
                                      start=True, stop=True), sig=True)
            bar()
            k.vec(lambda e: e.tensor_copy(out=negc_sb[:], in_=psrb[:, 0:2]), sig=True)
            bar()
            for hp in range(2):
                hh = 2 * pr + hp
                k.dma("sync", VN[:],
                      AP(mbias.tensor, hh * 128 * PITCH, [[PITCH, 128], [1, 4096]]))
                bar()
                hsl = slice(hp * 64, (hp + 1) * 64)
                t_ep = [None, None]
                t_pv_last = [None] * 65
                te1s = [None] * 65; te2s = [None] * 65; tpls = [None] * 65
                tscs = [None] * 65

                def emit_pv(gp, hp=hp):
                    Ip, Jp = divmod(gp, 16)
                    pvp = psPV[Ip % 2]
                    lh = vTh[:, Jp * 130 + hp * 65: Jp * 130 + hp * 65 + 65]
                    ll = vTl[:, Jp * 130 + hp * 65: Jp * 130 + hp * 65 + 65]
                    k.tens(lambda e, pvp=pvp, lh=lh, first=(Jp == 0): e.matmul(
                        pvp[0:65, :], lh, Pht[:], start=first, stop=False),
                        wait=[te1s[gp]])
                    k.tens(lambda e, pvp=pvp, lh=lh: e.matmul(
                        pvp[0:65, :], lh, Plt[:], start=False, stop=False),
                        wait=[tpls[gp]])
                    t_pv_last[gp] = k.tens(lambda e, pvp=pvp, ll=ll,
                                           last=(Jp == 15): e.matmul(
                        pvp[0:65, :], ll, Pht[:], start=False, stop=last), sig=True)

                def emit_epilogue(I, hp=hp):
                    pv = psPV[I % 2]
                    asl = slice(hp * T + I * 512, hp * T + I * 512 + 512)
                    tpv_done = t_pv_last[I * 16 + 15]
                    tr1 = k.vec(lambda e, pv=pv: e.reciprocal(out=rs_sb[64:65, :],
                                                              in_=pv[64:65, :]),
                                wait=[tpv_done, t_ep[I % 2]], sig=True)
                    tr2 = k.tens(lambda e: e.matmul(psrb[0:65, :], onesa[64:65, 0:65],
                                                    rs_sb[64:65, :], start=True,
                                                    stop=True), wait=[tr1], sig=True)
                    tr3 = k.vec(lambda e: e.tensor_copy(out=rs_sb[0:64, :],
                                                        in_=psrb[0:64, :]),
                                wait=[tr2], sig=True)
                    tr4 = k.vec(lambda e, pv=pv: e.tensor_mul(out=tmp64, in0=pv[0:64, :],
                                                              in1=rs_sb[0:64, :]),
                                wait=[tr3], sig=True)
                    tr5 = k.sca(lambda e, asl=asl: e.activation(
                        out=attnh[0:64, asl], in_=tmp64, func=AF.Copy,
                        bias=0.0, scale=1.0), wait=[tr4], sig=True)
                    t_ep[I % 2] = k.vec(lambda e, asl=asl: e.scalar_tensor_tensor(
                        out=attnl[0:64, asl], in0=tmp64, scalar=1.0,
                        in1=attnh[0:64, asl], op0=ALU.mult, op1=ALU.subtract),
                        wait=[tr5], sig=True)

                for g in range(64):
                    I, J = divmod(g, 16)
                    ps = psc[g % 2]
                    isl = slice(I * 512, (I + 1) * 512)
                    c0 = 2047 + I * 512 - J * 128
                    w0 = ([te2s[g - 2]] if g >= 2 else [])
                    if J == 0:
                        w0 = w0 + [t_ep[I % 2]]
                    k.tens(lambda e, ps=ps, J=J, hsl=hsl, isl=isl: e.matmul(
                        ps[:], kh_[hsl, J * 128:(J + 1) * 128], qh[hsl, isl],
                        start=True, stop=False), wait=w0)
                    k.tens(lambda e, ps=ps, J=J, hsl=hsl, isl=isl: e.matmul(
                        ps[:], kh_[hsl, J * 128:(J + 1) * 128], ql[hsl, isl],
                        start=False, stop=False))
                    k.tens(lambda e, ps=ps, J=J, hsl=hsl, isl=isl: e.matmul(
                        ps[:], kl_[hsl, J * 128:(J + 1) * 128], qh[hsl, isl],
                        start=False, stop=False))
                    tscs[g] = k.tens(lambda e, ps=ps, hp=hp, c0=c0: e.matmul(
                        ps[:], ident[:], VN[:, c0:c0 + 512],
                        start=False, stop=True), sig=True)
                    if g >= 1:
                        emit_pv(g - 1)
                        if g % 16 == 0:
                            emit_epilogue(I - 1)
                    te1s[g] = k.sca(lambda e, ps=ps, hp=hp: e.activation(
                        out=Pht[:], in_=ps[:], func=AF.Exp,
                        bias=negc_sb[:, hp:hp + 1], scale=1.0),
                        wait=[tscs[g], t_pv_last[g - 1]], sig=True)
                    te2s[g] = k.sca(lambda e, ps=ps, hp=hp: e.activation(
                        out=P32t[:], in_=ps[:], func=AF.Exp,
                        bias=negc_sb[:, hp:hp + 1], scale=1.0),
                        wait=[tpls[g - 1]], sig=True)
                    tpls[g] = k.vec(lambda e: e.tensor_sub(out=Plt[:], in0=P32t[:],
                                                           in1=Pht[:]),
                                    wait=[te1s[g], te2s[g], t_pv_last[g - 1]], sig=True)
                emit_pv(63)
                emit_epilogue(3)
                bar()
            for hp in range(2):
                k.dma("sync", wsp[0:64, 0:1024],
                      woT[pr * 128 + hp * 64: pr * 128 + hp * 64 + 64, :])
                bar()
                k.sca(lambda e: e.activation(out=wsh[0:64, 0:1024], in_=wsp[0:64, 0:1024],
                                             func=AF.Copy, bias=0.0, scale=1.0), sig=True)
                bar()
                k.vec(lambda e: e.scalar_tensor_tensor(
                    out=wsl[0:64, 0:1024], in0=wsp[0:64, 0:1024], scalar=1.0,
                    in1=wsh[0:64, 0:1024], op0=ALU.mult, op1=ALU.subtract), sig=True)
                bar()
                t_add = [None, None]
                i = 0
                for dh in range(NCH):
                    lh = wsh[0:64, dh * 128:(dh + 1) * 128]
                    ll = wsl[0:64, dh * 128:(dh + 1) * 128]
                    for tc in range(4):
                        ps = psqkv[i % 2]
                        asl = slice(hp * T + tc * 512, hp * T + tc * 512 + 512)
                        k.tens(lambda e, ps=ps, lh=lh, asl=asl: e.matmul(
                            ps[:], lh, attnh[0:64, asl], start=True, stop=False),
                            wait=[t_add[i % 2]])
                        k.tens(lambda e, ps=ps, lh=lh, asl=asl: e.matmul(
                            ps[:], lh, attnl[0:64, asl], start=False, stop=False))
                        tg = k.tens(lambda e, ps=ps, ll=ll, asl=asl: e.matmul(
                            ps[:], ll, attnh[0:64, asl], start=False, stop=True),
                            sig=True)
                        t_add[i % 2] = k.vec(lambda e, ps=ps, dh=dh, tc=tc: e.tensor_tensor(
                            out=X[:, dh * T + tc * 512: dh * T + tc * 512 + 512],
                            in0=X[:, dh * T + tc * 512: dh * T + tc * 512 + 512],
                            in1=ps[:], op=ALU.add), wait=[tg], sig=True)
                        i += 1
                bar()
        k.free("psTT0", "psqkv1", "psqkv0", "psrb", "psPV1", "psPV0", "psc1", "psc0")
        k.free("negc_sb", "rs_sb",
               "attnl", "attnh", "VN", "vTl", "vTh", "kl_", "kh_", "ql", "qh",
               "P32t", "Plt", "Pht", "wsl", "wsh", "wsp")

    identf = k.din("identf", [128, 128])
    cstv_d = k.din("cstv_d", [128, 2])
    identfs = k.sbuf("identfs", [128, 128])
    k.dma("sync", identfs[:], identf[:])
    cstv = k.sbuf("cstv", [128, 2])
    k.dma("sync", cstv[:], cstv_d[:])

    def emit_xq():
        v32 = k.sbuf("xqv", [128, T])
        xki = k.sbuf("xki", [128, T], I16)
        k.vec(lambda e: e.tensor_scalar(out=ps_slot(S_SX), in0=ps_slot(S_XG),
                                        scalar1=1.0 / 32767.0, scalar2=1e-12,
                                        op0=ALU.mult, op1=ALU.max), sig=True)
        bar()
        k.vec(lambda e: e.reciprocal(out=ps_slot(S_FX), in_=ps_slot(S_SX)), sig=True)
        bar()
        bcast_panel(S_FX, 0)
        bar()
        for c in range(NCH):
            sl = slice(c * T, (c + 1) * T)
            k.vec(lambda e, sl=sl: e.tensor_add(out=v32[:], in0=XH[:, sl], in1=XL[:, sl]), sig=True)
            bar()
            k.vec(lambda e: e.tensor_scalar(out=v32[:], in0=v32[:],
                                            scalar1=pbc[:, 0:1], scalar2=MAGIC,
                                            op0=ALU.mult, op1=ALU.add), sig=True)
            bar()
            k.vec(lambda e: e.tensor_scalar(out=xki[:], in0=v32[:], scalar1=MAGIC,
                                            scalar2=None, op0=ALU.subtract), sig=True)
            bar()
            k.sca(lambda e, sl=sl: e.activation(out=XH[:, sl], in_=xki[:],
                                                func=AF.Copy, bias=0.0, scale=1.0), sig=True)
            bar()
            k.vec(lambda e, sl=sl: e.scalar_tensor_tensor(
                out=XL[:, sl], in0=xki[:], scalar=1.0, in1=XH[:, sl],
                op0=ALU.mult, op1=ALU.subtract), sig=True)
            bar()
        k.free("xki", "xqv")

    def emit_lpbq_dequant(wbuf, wdq, wqi, bmv, sv, W):
        ng = W // 32
        in3 = wbuf[:].rearrange("p (g j) -> p g j", j=32)[0:128, 0:ng, 0:32]
        pitch_s = sv[:].ap[0][0]
        bc3 = AP(sv[:].tensor, 0, [[pitch_s, 128], [1, ng], [0, 32]])
        k.vec(lambda e: e.tensor_reduce(out=bmv[0:128, 0:ng], in_=in3, axis=AX.X,
                                        op=ALU.max, apply_absolute_value=True), sig=True)
        bar()
        k.vec(lambda e: e.tensor_scalar(out=sv[0:128, 0:ng], in0=bmv[0:128, 0:ng],
                                        scalar1=1.0 / 7.0, scalar2=1e-12,
                                        op0=ALU.mult, op1=ALU.max), sig=True)
        bar()
        k.vec(lambda e: e.reciprocal(out=bmv[0:128, 0:ng], in_=sv[0:128, 0:ng]), sig=True)
        bar()
        pitch_b = bmv[:].ap[0][0]
        ibc3 = AP(bmv[:].tensor, 0, [[pitch_b, 128], [1, ng], [0, 32]])
        out3 = wdq[:].rearrange("p (g j) -> p g j", j=32)[0:128, 0:ng, 0:32]
        in03 = wbuf[:].rearrange("p (g j) -> p g j", j=32)[0:128, 0:ng, 0:32]
        k.vec(lambda e: e.tensor_tensor(out=out3, in0=in03,
                                        in1=ibc3, op=ALU.mult), sig=True)
        bar()
        k.vec(lambda e: e.tensor_scalar(out=wqi[0:128, 0:W], in0=wdq[0:128, 0:W],
                                        scalar1=MAGIC, scalar2=MAGIC,
                                        op0=ALU.add, op1=ALU.subtract), sig=True)
        bar()
        ini3 = wqi[:].rearrange("p (g j) -> p g j", j=32)[0:128, 0:ng, 0:32]
        k.vec(lambda e: e.tensor_tensor(out=out3, in0=ini3,
                                        in1=bc3, op=ALU.mult), sig=True)
        bar()

    def ini3(wqi, ng):
        return wqi[:].rearrange("p (g j) -> p g j", j=32)[0:128, 0:ng, 0:32]

    def emit_lpbq_dequant_tok(wbuf, wdq, wqi, bmv, sv, W, wait0):
        ng = W // 32
        in3 = wbuf[:].rearrange("p (g j) -> p g j", j=32)[0:128, 0:ng, 0:32]
        pitch_s = sv[:].ap[0][0]
        bc3 = AP(sv[:].tensor, 0, [[pitch_s, 128], [1, ng], [0, 32]])
        pitch_b = bmv[:].ap[0][0]
        ibc3 = AP(bmv[:].tensor, 0, [[pitch_b, 128], [1, ng], [0, 32]])
        out3 = wdq[:].rearrange("p (g j) -> p g j", j=32)[0:128, 0:ng, 0:32]
        in03 = wbuf[:].rearrange("p (g j) -> p g j", j=32)[0:128, 0:ng, 0:32]
        k.vec(lambda e: e.tensor_reduce(out=bmv[0:128, 0:ng], in_=in3, axis=AX.X,
                                        op=ALU.max, apply_absolute_value=True),
              wait=wait0)
        k.vec(lambda e: e.tensor_scalar(out=sv[0:128, 0:ng], in0=bmv[0:128, 0:ng],
                                        scalar1=1.0 / 7.0, scalar2=1e-12,
                                        op0=ALU.mult, op1=ALU.max))
        k.vec(lambda e: e.reciprocal(out=bmv[0:128, 0:ng], in_=sv[0:128, 0:ng]))
        t_free = k.vec(lambda e: e.tensor_tensor(out=out3, in0=in03,
                                                 in1=ibc3, op=ALU.mult), sig=True)
        k.vec(lambda e: e.tensor_scalar(out=wqi[0:128, 0:W], in0=wdq[0:128, 0:W],
                                        scalar1=MAGIC, scalar2=MAGIC,
                                        op0=ALU.add, op1=ALU.subtract))
        t_done = k.vec(lambda e, w3=ini3(wqi, ng): e.tensor_tensor(
            out=out3, in0=w3, in1=bc3, op=ALU.mult), sig=True)
        return t_free, t_done

    def emit_ffn0_mm():
        NO = IFF0 // 128
        wbuf = [k.sbuf("wbuf0", [128, 1024]), k.sbuf("wbuf1", [128, 1024])]
        wdq = k.sbuf("wdq", [128, 1024])
        wqi = k.sbuf("wqi", [128, 1024], I16)
        bmv = k.sbuf("bmv", [128, 32]); sv = k.sbuf("sv", [128, 32])
        wTh = [k.sbuf("wTh0", [128, 1024], BF16), k.sbuf("wTh1", [128, 1024], BF16)]
        wTl = [k.sbuf("wTl0", [128, 1024], BF16), k.sbuf("wTl1", [128, 1024], BF16)]
        stage = [k.sbuf("stage0", [128, T]), k.sbuf("stage1", [128, T])]
        psW = k.psum("psW", [128, 128])
        psU = [k.psum("psU%d" % i) for i in range(4)]
        k.vec(lambda e: e.memset(runmax[:, 2:4], 0.0), sig=True)
        bar()
        jobs = [(wup, sp_up, 2, o) for o in range(NO)] + \
               [(wgate, sp_gate, 3, o) for o in range(NO)]
        t_load = [None] * len(jobs)
        t_tp = [None] * len(jobs)
        t_mm = {}
        t_evac = {}
        t_spill = [None] * len(jobs)
        t_amax = [None] * len(jobs)
        t_wdqfree = [None] * len(jobs)
        t_load[0] = k.dma("sync", wbuf[0][:], jobs[0][0][0:128, :])
        t_load[1] = k.dma("gpsimd", wbuf[1][:], jobs[1][0][128:256, :])
        for j, (wsrc, spdst, rcol, o) in enumerate(jobs):
            wb = wbuf[j % 2]
            w0 = [t_load[j]] + ([t_wdqfree[j - 1]] if j > 0 else [])
            t_free, t_deq = emit_lpbq_dequant_tok(wb, wdq, wqi, bmv, sv, 1024, w0)
            if j + 2 < len(jobs):
                ws2, _, _, o2 = jobs[j + 2]
                eng = "sync" if (j % 2 == 0) else "gpsimd"
                t_load[j + 2] = k.dma(eng, wb[:], ws2[o2 * 128:(o2 + 1) * 128, :],
                                      wait=[t_free])
            last_tt = None
            prev_user = t_mm.get((j - 2, 3))
            for kc in range(NCH):
                t_t = k.tens(lambda e, kc=kc: e.transpose(
                    psW[:], wdq[:, kc * 128:(kc + 1) * 128], identfs[:]),
                    wait=[t_deq, prev_user] + ([last_tt] if last_tt else []), sig=True)
                th = k.sca(lambda e, kc=kc, j=j: e.activation(
                    out=wTh[j % 2][:, kc * 128:(kc + 1) * 128], in_=psW[:],
                    func=AF.Copy, bias=0.0, scale=1.0), wait=[t_t], sig=True)
                last_tt = k.vec(lambda e, kc=kc, j=j: e.scalar_tensor_tensor(
                    out=wTl[j % 2][:, kc * 128:(kc + 1) * 128], in0=psW[:], scalar=1.0,
                    in1=wTh[j % 2][:, kc * 128:(kc + 1) * 128],
                    op0=ALU.mult, op1=ALU.subtract), wait=[th], sig=True)
            t_tp[j] = last_tt
            t_wdqfree[j] = last_tt
            for tc in range(4):
                ps = psU[tc]
                w_mm = [t_tp[j], t_evac.get((j - 1, tc))]
                for kc in range(NCH):
                    lh = wTh[j % 2][:, kc * 128:(kc + 1) * 128]
                    ll = wTl[j % 2][:, kc * 128:(kc + 1) * 128]
                    rh = XH[:, kc * T + tc * 512: kc * T + tc * 512 + 512]
                    rl = XL[:, kc * T + tc * 512: kc * T + tc * 512 + 512]
                    k.tens(lambda e, ps=ps, lh=lh, rh=rh, first=(kc == 0): e.matmul(
                        ps[:], lh, rh, start=first, stop=False),
                        wait=w_mm if kc == 0 else [])
                    k.tens(lambda e, ps=ps, lh=lh, rl=rl: e.matmul(
                        ps[:], lh, rl, start=False, stop=False))
                    t = k.tens(lambda e, ps=ps, ll=ll, rh=rh, last=(kc == NCH - 1): e.matmul(
                        ps[:], ll, rh, start=False, stop=last), sig=(kc == NCH - 1))
                t_mm[(j, tc)] = t
                t_evac[(j, tc)] = k.sca(lambda e, tc=tc, ps=ps, j=j: e.activation(
                    out=stage[j % 2][:, tc * 512:(tc + 1) * 512], in_=ps[:],
                    func=AF.Copy, bias=0.0, scale=1.0),
                    wait=[t, t_spill[j - 2] if j >= 2 else None], sig=True)
            t_r = k.vec(lambda e, j=j: e.reduce_max(
                out=runmax[:, 1:2], in_=stage[j % 2][:], axis=AX.X,
                apply_absolute_value=True),
                wait=[t_evac[(j, tc)] for tc in range(4)], sig=True)
            t_amax[j] = k.vec(lambda e, rcol=rcol: e.tensor_max(
                out=runmax[:, rcol:rcol + 1], in0=runmax[:, rcol:rcol + 1],
                in1=runmax[:, 1:2]), wait=[t_r], sig=True)
            t_spill[j] = k.dma("sync", spdst[o * 128:(o + 1) * 128, :], stage[j % 2][:],
                               wait=[t_amax[j]])
        bar()
        psT = k.psum("psT", [1, 128])
        for rcol, slot in ((2, S_UG), (3, S_GG)):
            k.tens(lambda e, rcol=rcol: e.transpose(
                psT[:], runmax[:, rcol:rcol + 1], identfs[:]), sig=True)
            bar()
            k.vec(lambda e, slot=slot: e.reduce_max(out=ps_slot(slot), in_=psT[:],
                                                    axis=AX.X), sig=True)
            bar()
        k.free("psT")
        for i in reversed(range(4)):
            k.free("psU%d" % i)
        k.free("psW")
        k.free("stage1", "stage0", "wTl1", "wTl0", "wTh1", "wTh0",
               "sv", "bmv", "wqi", "wdq", "wbuf1", "wbuf0")

    def emit_scales_after_ar2():
        for gslot, sslot, fslot, col in ((S_UG, S_SU, S_FU, 1), (S_GG, S_SG, S_FG, 2)):
            k.vec(lambda e, gslot=gslot: e.tensor_mul(
                out=ps_slot(S_T0), in0=ps_slot(gslot), in1=ps_slot(S_SX)), sig=True)
            bar()
            k.vec(lambda e, sslot=sslot: e.tensor_scalar(
                out=ps_slot(sslot), in0=ps_slot(S_T0), scalar1=1.0 / 32767.0,
                scalar2=1e-12, op0=ALU.mult, op1=ALU.max), sig=True)
            bar()
            k.vec(lambda e, sslot=sslot: e.reciprocal(out=ps_slot(S_T1),
                                                      in_=ps_slot(sslot)), sig=True)
            bar()
            k.vec(lambda e, fslot=fslot: e.tensor_mul(
                out=ps_slot(fslot), in0=ps_slot(S_T1), in1=ps_slot(S_SX)), sig=True)
            bar()
            bcast_panel(fslot, col)
            bar()
        bcast_panel(S_SG, 3)
        bar()

    def emit_scales_after_ar3():
        k.vec(lambda e: e.tensor_mul(out=ps_slot(S_T0), in0=ps_slot(S_QG),
                                     in1=ps_slot(S_SG)), sig=True)
        bar()
        k.vec(lambda e: e.tensor_scalar(out=ps_slot(S_SQ2), in0=ps_slot(S_T0),
                                        scalar1=(1.0 / 65536.0) / 32767.0, scalar2=1e-12,
                                        op0=ALU.mult, op1=ALU.max), sig=True)
        bar()
        k.vec(lambda e: e.reciprocal(out=ps_slot(S_T1), in_=ps_slot(S_SQ2)), sig=True)
        bar()
        k.vec(lambda e: e.tensor_mul(out=ps_slot(S_T2), in0=ps_slot(S_T1),
                                     in1=ps_slot(S_SG)), sig=True)
        bar()
        k.vec(lambda e: e.tensor_scalar_mul(out=ps_slot(S_FQ2), in0=ps_slot(S_T2),
                                            scalar1=1.0 / 65536.0), sig=True)
        bar()
        bcast_panel(S_FQ2, 4)
        bar()

    def emit_scales_after_ar4():
        k.vec(lambda e: e.tensor_mul(out=ps_slot(S_T0), in0=ps_slot(S_PG),
                                     in1=ps_slot(S_SQ2)), sig=True)
        bar()
        k.vec(lambda e: e.tensor_mul(out=ps_slot(S_T0), in0=ps_slot(S_T0),
                                     in1=ps_slot(S_SU)), sig=True)
        bar()
        k.vec(lambda e: e.tensor_scalar(out=ps_slot(S_SO), in0=ps_slot(S_T0),
                                        scalar1=1.0 / 32767.0, scalar2=1e-12,
                                        op0=ALU.mult, op1=ALU.max), sig=True)
        bar()
        k.vec(lambda e: e.reciprocal(out=ps_slot(S_T1), in_=ps_slot(S_SO)), sig=True)
        bar()
        k.vec(lambda e: e.tensor_mul(out=ps_slot(S_T2), in0=ps_slot(S_SQ2),
                                     in1=ps_slot(S_SU)), sig=True)
        bar()
        k.vec(lambda e: e.tensor_mul(out=ps_slot(S_FO), in0=ps_slot(S_T2),
                                     in1=ps_slot(S_T1)), sig=True)
        bar()
        bcast_panel(S_FO, 5)
        bcast_panel(S_SO, 6)
        bar()

    def emit_ffn0_ew_B():
        NO = IFF0 // 128
        ebuf = [k.sbuf("ebuf0", [128, T]), k.sbuf("ebuf1", [128, T])]
        gbuf = [k.sbuf("gbuf0", [128, T]), k.sbuf("gbuf1", [128, T])]
        eki = k.sbuf("eki", [128, T], I16)
        uk16 = [k.sbuf("uk0", [128, T], I16), k.sbuf("uk1", [128, T], I16)]
        sigb = k.sbuf("sigb", [128, T])
        ski = k.sbuf("ski", [128, T], I32)
        k.vec(lambda e: e.memset(runmax[:, 2:3], 0.0), sig=True)
        bar()
        t_lu = [None] * (NO + 2); t_lg = [None] * (NO + 2)
        t_su = [None] * (NO + 2); t_sq = [None] * (NO + 2)
        t_ub = [None] * (NO + 2); t_gb = [None] * (NO + 2)
        t_sig_prev = [None]
        t_lu[0] = k.dma("sync", ebuf[0][:], sp_up[0:128, :])
        t_lg[0] = k.dma("gpsimd", gbuf[0][:], sp_gate[0:128, :])
        t_lu[1] = k.dma("sync", ebuf[1][:], sp_up[128:256, :])
        t_lg[1] = k.dma("gpsimd", gbuf[1][:], sp_gate[128:256, :])
        for o in range(NO):
            osl = slice(o * 128, (o + 1) * 128)
            eb = ebuf[o % 2]; gb = gbuf[o % 2]; uk = uk16[o % 2]
            k.vec(lambda e, eb=eb: e.tensor_scalar(out=eb[:], in0=eb[:],
                                                   scalar1=pbc[:, 1:2], scalar2=MAGIC,
                                                   op0=ALU.mult, op1=ALU.add),
                  wait=[t_lu[o], t_su[o - 2]])
            t_upk = k.vec(lambda e, eb=eb, uk=uk: e.tensor_scalar(
                out=uk[:], in0=eb[:], scalar1=MAGIC, scalar2=None,
                op0=ALU.subtract), wait=[t_su[o - 2]], sig=True)
            t_su[o] = k.dma("sync", sp_upk[osl, :], uk[:], wait=[t_upk])
            if o + 2 < NO:
                t_lu[o + 2] = k.dma("sync", eb[:], sp_up[(o + 2) * 128:(o + 3) * 128, :],
                                    wait=[t_upk])
            k.vec(lambda e, gb=gb: e.tensor_scalar(out=gb[:], in0=gb[:],
                                                   scalar1=pbc[:, 2:3], scalar2=MAGIC,
                                                   op0=ALU.mult, op1=ALU.add),
                  wait=[t_lg[o]])
            t_gk = k.vec(lambda e, gb=gb: e.tensor_scalar(
                out=eki[:], in0=gb[:], scalar1=MAGIC, scalar2=None,
                op0=ALU.subtract), wait=[t_sig_prev[0]], sig=True)
            if o + 2 < NO:
                t_lg[o + 2] = k.dma("gpsimd", gb[:],
                                    sp_gate[(o + 2) * 128:(o + 3) * 128, :], wait=[t_gk])
            t_sg = k.sca(lambda e: e.activation(out=sigb[:], in_=eki[:], func=AF.Sigmoid,
                                                bias=0.0, scale=pbc[:, 3:4]),
                         wait=[t_gk, t_sq[o - 1]], sig=True)
            t_sig_prev[0] = t_sg
            k.vec(lambda e: e.tensor_scalar(out=sigb[:], in0=sigb[:],
                                            scalar1=65536.0, scalar2=MAGIC,
                                            op0=ALU.mult, op1=ALU.add), wait=[t_sg])
            k.vec(lambda e: e.tensor_scalar(out=ski[:], in0=sigb[:], scalar1=MAGIC,
                                            scalar2=65535.0, op0=ALU.subtract,
                                            op1=ALU.min))
            k.vec(lambda e: e.tensor_tensor(out=sigb[:], in0=eki[:], in1=ski[:],
                                            op=ALU.mult))
            k.vec(lambda e: e.reduce_max(out=runmax[:, 1:2], in_=sigb[:], axis=AX.X,
                                         apply_absolute_value=True))
            t_qm = k.vec(lambda e: e.tensor_max(out=runmax[:, 2:3], in0=runmax[:, 2:3],
                                                in1=runmax[:, 1:2]), sig=True)
            t_sq[o] = k.dma("gpsimd", sp_q[osl, :], sigb[:], wait=[t_qm])
        bar()
        psT = k.psum("psT", [1, 128])
        k.tens(lambda e: e.transpose(psT[:], runmax[:, 2:3], identfs[:]), sig=True)
        bar()
        k.vec(lambda e: e.reduce_max(out=ps_slot(S_QG), in_=psT[:], axis=AX.X), sig=True)
        bar()
        k.free("psT")
        k.free("ski", "sigb", "uk1", "uk0", "eki", "gbuf1", "gbuf0", "ebuf1", "ebuf0")

    def emit_ffn0_ew_CD():
        NO = IFF0 // 128
        ebuf = [k.sbuf("ebuf0", [128, T]), k.sbuf("ebuf1", [128, T])]
        uki = [k.sbuf("uki0", [128, T], I16), k.sbuf("uki1", [128, T], I16)]
        eki = [k.sbuf("eki0", [128, T], I16), k.sbuf("eki1", [128, T], I16)]
        pbuf = k.sbuf("pbuf", [128, T])
        k.vec(lambda e: e.memset(runmax[:, 2:3], 0.0), sig=True)
        bar()
        t_lq = [None] * (NO + 2); t_lu = [None] * (NO + 2)
        t_sg = [None] * (NO + 2); t_pm = [None] * (NO + 2)
        t_lq[0] = k.dma("sync", ebuf[0][:], sp_q[0:128, :])
        t_lu[0] = k.dma("gpsimd", uki[0][:], sp_upk[0:128, :])
        t_lq[1] = k.dma("sync", ebuf[1][:], sp_q[128:256, :])
        t_lu[1] = k.dma("gpsimd", uki[1][:], sp_upk[128:256, :])
        for o in range(NO):
            osl = slice(o * 128, (o + 1) * 128)
            eb = ebuf[o % 2]; uk = uki[o % 2]; ek = eki[o % 2]
            k.vec(lambda e, eb=eb: e.tensor_scalar(out=eb[:], in0=eb[:],
                                                   scalar1=pbc[:, 4:5], scalar2=MAGIC,
                                                   op0=ALU.mult, op1=ALU.add),
                  wait=[t_lq[o], t_sg[o - 2]])
            t_g2 = k.vec(lambda e, eb=eb, ek=ek: e.tensor_scalar(
                out=ek[:], in0=eb[:], scalar1=MAGIC, scalar2=None,
                op0=ALU.subtract), sig=True)
            t_sg[o] = k.dma("sync", sp_g2k[osl, :], ek[:], wait=[t_g2])
            if o + 2 < NO:
                t_lq[o + 2] = k.dma("sync", eb[:], sp_q[(o + 2) * 128:(o + 3) * 128, :],
                                    wait=[t_g2])
            t_p = k.vec(lambda e, uk=uk, ek=ek: e.tensor_tensor(
                out=pbuf[:], in0=ek[:], in1=uk[:], op=ALU.mult),
                wait=[t_lu[o]], sig=True)
            if o + 2 < NO:
                t_lu[o + 2] = k.dma("gpsimd", uk[:],
                                    sp_upk[(o + 2) * 128:(o + 3) * 128, :], wait=[t_p])
            k.vec(lambda e: e.reduce_max(out=runmax[:, 1:2], in_=pbuf[:], axis=AX.X,
                                         apply_absolute_value=True))
            t_pm[o] = k.vec(lambda e: e.tensor_max(out=runmax[:, 2:3], in0=runmax[:, 2:3],
                                                   in1=runmax[:, 1:2]), sig=True)
        bar()
        psT = k.psum("psT", [1, 128])
        k.tens(lambda e: e.transpose(psT[:], runmax[:, 2:3], identfs[:]), sig=True)
        bar()
        k.vec(lambda e: e.reduce_max(out=ps_slot(S_PG), in_=psT[:], axis=AX.X), sig=True)
        bar()
        k.free("psT")
        k.free("pbuf", "eki1", "eki0", "uki1", "uki0", "ebuf1", "ebuf0")

    def emit_ffn0_ew_E():
        NO = IFF0 // 128
        ebuf = [k.sbuf("ebuf0", [128, T]), k.sbuf("ebuf1", [128, T])]
        uki = [k.sbuf("uki0", [128, T], I16), k.sbuf("uki1", [128, T], I16)]
        gki = [k.sbuf("gki0", [128, T], I16), k.sbuf("gki1", [128, T], I16)]
        oh_t = [k.sbuf("oh0", [128, T], BF16), k.sbuf("oh1", [128, T], BF16)]
        ol_t = [k.sbuf("ol0", [128, T], BF16), k.sbuf("ol1", [128, T], BF16)]
        t_lg = [None] * (NO + 2); t_lu = [None] * (NO + 2)
        t_so = [None] * (NO + 2); t_soh2 = [None] * (NO + 2)
        t_lg[0] = k.dma("sync", gki[0][:], sp_g2k[0:128, :])
        t_lu[0] = k.dma("gpsimd", uki[0][:], sp_upk[0:128, :])
        t_lg[1] = k.dma("sync", gki[1][:], sp_g2k[128:256, :])
        t_lu[1] = k.dma("gpsimd", uki[1][:], sp_upk[128:256, :])
        for o in range(NO):
            osl = slice(o * 128, (o + 1) * 128)
            eb = ebuf[o % 2]; uk = uki[o % 2]; gk = gki[o % 2]
            oh = oh_t[o % 2]; ol = ol_t[o % 2]
            t_p = k.vec(lambda e, eb=eb, gk=gk, uk=uk: e.tensor_tensor(
                out=eb[:], in0=gk[:], in1=uk[:], op=ALU.mult),
                wait=[t_lg[o], t_lu[o]], sig=True)
            k.vec(lambda e, eb=eb: e.tensor_scalar(out=eb[:], in0=eb[:],
                                                   scalar1=pbc[:, 5:6], scalar2=MAGIC,
                                                   op0=ALU.mult, op1=ALU.add))
            t_ok = k.vec(lambda e, eb=eb, uk=uk: e.tensor_scalar(
                out=uk[:], in0=eb[:], scalar1=MAGIC, scalar2=None,
                op0=ALU.subtract), sig=True)
            th = k.sca(lambda e, uk=uk, oh=oh: e.activation(
                out=oh[:], in_=uk[:], func=AF.Copy, bias=0.0, scale=1.0),
                wait=[t_ok, t_soh2[o - 2]], sig=True)
            tl = k.vec(lambda e, uk=uk, oh=oh, ol=ol: e.scalar_tensor_tensor(
                out=ol[:], in0=uk[:], scalar=1.0, in1=oh[:], op0=ALU.mult,
                op1=ALU.subtract), wait=[th, t_so[o - 2]], sig=True)
            if o + 2 < NO:
                t_lg[o + 2] = k.dma("sync", gk[:], sp_g2k[(o + 2) * 128:(o + 3) * 128, :],
                                    wait=[t_p])
                t_lu[o + 2] = k.dma("gpsimd", uk[:],
                                    sp_upk[(o + 2) * 128:(o + 3) * 128, :], wait=[tl])
            t_soh = k.dma("sync", sp_oh[osl, :], oh[:], wait=[th])
            t_sol = k.dma("gpsimd", sp_ol[osl, :], ol[:], wait=[tl])
            t_so[o] = t_sol
            t_soh2[o] = t_soh
        bar()
        k.free("ol1", "ol0", "oh1", "oh0", "gki1", "gki0", "uki1", "uki0",
               "ebuf1", "ebuf0")

    wdTh_d = k.dscratch("wdTh_d", [NCH, IFF0, 128], BF16)
    wdTl_d = k.dscratch("wdTl_d", [NCH, IFF0, 128], BF16)

    def emit_ffn0_down_prep():
        wbuf = [k.sbuf("wbufA", [128, IFF0]), k.sbuf("wbufB", [128, IFF0])]
        wdq = k.sbuf("wdq", [128, IFF0])
        wqi = k.sbuf("wqi", [128, IFF0], I16)
        bmv = k.sbuf("bmv", [128, 96]); sv = k.sbuf("sv", [128, 96])
        wTh = k.sbuf("wTh", [128, 256], BF16); wTl = k.sbuf("wTl", [128, 256], BF16)
        psW = k.psum("psW", [128, 128])
        NIC = IFF0 // 128
        t_load = [None] * (NCH + 2)
        t_wdqfree = [None] * (NCH + 2)
        t_sh = [None] * (NCH * NIC); t_sl = [None] * (NCH * NIC)
        t_load[0] = k.dma("sync", wbuf[0][:], wdown[0:128, :])
        t_load[1] = k.dma("gpsimd", wbuf[1][:], wdown[128:256, :])
        for dh in range(NCH):
            wb = wbuf[dh % 2]
            w0 = [t_load[dh]] + ([t_wdqfree[dh - 1]] if dh > 0 else [])
            t_free, t_deq = emit_lpbq_dequant_tok(wb, wdq, wqi, bmv, sv, IFF0, w0)
            if dh + 2 < NCH:
                eng = "sync" if dh % 2 == 0 else "gpsimd"
                t_load[dh + 2] = k.dma(eng, wb[:],
                                       wdown[(dh + 2) * 128:(dh + 3) * 128, :],
                                       wait=[t_free])
            last_tt = None
            for ic in range(NIC):
                gc = dh * NIC + ic
                sl2 = slice((gc % 2) * 128, (gc % 2) * 128 + 128)
                t_t = k.tens(lambda e, ic=ic: e.transpose(
                    psW[:], wdq[:, ic * 128:(ic + 1) * 128], identfs[:]),
                    wait=[t_deq] + ([last_tt] if last_tt else []), sig=True)
                th = k.sca(lambda e, sl2=sl2: e.activation(
                    out=wTh[:, sl2], in_=psW[:], func=AF.Copy, bias=0.0, scale=1.0),
                    wait=[t_t, t_sh[gc - 2] if gc >= 2 else None], sig=True)
                last_tt = k.vec(lambda e, sl2=sl2: e.scalar_tensor_tensor(
                    out=wTl[:, sl2], in0=psW[:], scalar=1.0, in1=wTh[:, sl2],
                    op0=ALU.mult, op1=ALU.subtract),
                    wait=[th, t_sl[gc - 2] if gc >= 2 else None], sig=True)
                t_sh[gc] = k.dma("sync",
                                 AP(wdTh_d.tensor, (dh * IFF0 + ic * 128) * 128,
                                    [[128, 128], [1, 128]]), wTh[:, sl2], wait=[th])
                t_sl[gc] = k.dma("gpsimd",
                                 AP(wdTl_d.tensor, (dh * IFF0 + ic * 128) * 128,
                                    [[128, 128], [1, 128]]), wTl[:, sl2], wait=[last_tt])
            t_wdqfree[dh] = last_tt
        bar()
        k.free("psW", "wTl", "wTh", "sv", "bmv", "wqi", "wdq", "wbufB", "wbufA")

    def emit_ffn0_down():
        oth = k.sbuf("oth", [128, 24 * 512], BF16)
        otl = k.sbuf("otl", [128, 24 * 512], BF16)
        wTh = k.sbuf("wTh", [128, IFF0], BF16)
        wTl = k.sbuf("wTl", [128, IFF0], BF16)
        dtmp = k.sbuf("dtmp", [128, 512])
        psD = [k.psum("psD0"), k.psum("psD1")]
        for tc in range(4):
            tsl = slice(tc * 512, (tc + 1) * 512)
            for ic in range(IFF0 // 128):
                k.dma("sync", oth[:, ic * 512:(ic + 1) * 512], sp_oh[ic * 128:(ic + 1) * 128, tsl])
                k.dma("gpsimd", otl[:, ic * 512:(ic + 1) * 512], sp_ol[ic * 128:(ic + 1) * 128, tsl])
            bar()
            for dh in range(NCH):
                k.dma("sync", wTh[:].rearrange("p (a b) -> p a b", b=128),
                      AP(wdTh_d.tensor, dh * IFF0 * 128,
                         [[128, 128], [128 * 128, 24], [1, 128]]))
                k.dma("gpsimd", wTl[:].rearrange("p (a b) -> p a b", b=128),
                      AP(wdTl_d.tensor, dh * IFF0 * 128,
                         [[128, 128], [128 * 128, 24], [1, 128]]))
                bar()
                ps = psD[dh % 2]
                for ic in range(IFF0 // 128):
                    lh = wTh[:, ic * 128:(ic + 1) * 128]
                    ll = wTl[:, ic * 128:(ic + 1) * 128]
                    rh = oth[:, ic * 512:(ic + 1) * 512]
                    rl = otl[:, ic * 512:(ic + 1) * 512]
                    k.tens(lambda e, ps=ps, lh=lh, rh=rh, first=(ic == 0): e.matmul(
                        ps[:], lh, rh, start=first, stop=False))
                    k.tens(lambda e, ps=ps, lh=lh, rl=rl: e.matmul(
                        ps[:], lh, rl, start=False, stop=False))
                    k.tens(lambda e, ps=ps, ll=ll, rh=rh, last=(ic == IFF0 // 128 - 1): e.matmul(
                        ps[:], ll, rh, start=False, stop=last), sig=(ic == IFF0 // 128 - 1))
                bar()
                k.vec(lambda e, ps=ps: e.tensor_scalar(out=dtmp[:], in0=ps[:],
                                                       scalar1=pbc[:, 6:7], scalar2=None,
                                                       op0=ALU.mult), sig=True)
                bar()
                k.vec(lambda e, dh=dh, tc=tc: e.tensor_tensor(
                    out=X[:, dh * T + tc * 512: dh * T + tc * 512 + 512],
                    in0=X[:, dh * T + tc * 512: dh * T + tc * 512 + 512],
                    in1=dtmp[:], op=ALU.add), sig=True)
                bar()
        k.free("psD1", "psD0")
        k.free("dtmp", "wTl", "wTh", "otl", "oth")

    b1col = k.din("b1col", [128, 32])

    def emit_ffn1():
        for dh in range(NCH):
            k.vec(lambda e, dh=dh: e.tensor_scalar(
                out=X[:, dh * T:(dh + 1) * T], in0=X[:, dh * T:(dh + 1) * T],
                scalar1=bcols[:, 32 + dh: 32 + dh + 1], scalar2=None,
                op0=ALU.add), sig=True)
        b1c = k.sbuf("b1c", [128, 32])
        k.dma("sync", b1c[:], b1col[:])
        wbufq = [k.sbuf("wbufq0", [128, DFF1]), k.sbuf("wbufq1", [128, DFF1])]
        wq16 = [k.sbuf("wq16a", [128, DFF1], BF16), k.sbuf("wq16b", [128, DFF1], BF16)]
        gacc = k.sbuf("gacc", [128, 1])
        gtmp = k.sbuf("gtmp", [128, 1])
        psT = k.psum("psT", [1, 128])
        rd_tok = [None, None]
        st_tok = [None, None]
        for wsrc, wdst, nchk, wid, col in ((w1T, w1q_d, 8, DFF1, 7),
                                           (w2T, w2q_d, 32, D, 8)):
            k.vec(lambda e: e.memset(gacc[:], 0.0), sig=True)
            ld = [None] * (nchk + 2)
            ld[0] = k.dma("sync", wbufq[0][0:128, 0:wid], wsrc[0:128, :],
                          wait=[rd_tok[0]])
            if nchk > 1:
                ld[1] = k.dma("gpsimd", wbufq[1][0:128, 0:wid], wsrc[128:256, :],
                              wait=[rd_tok[1]])
            t_add = None
            for c in range(nchk):
                wb = wbufq[c % 2]
                t_rs = k.vec(lambda e, wb=wb, wid=wid: e.reduce_sum(
                    out=gtmp[:], in_=wb[0:128, 0:wid], axis=AX.X,
                    apply_absolute_value=True), wait=[ld[c]], sig=True)
                rd_tok[c % 2] = t_rs
                t_add = k.vec(lambda e: e.tensor_add(out=gacc[:], in0=gacc[:],
                                                     in1=gtmp[:]), sig=True)
                if c + 2 < nchk:
                    eng = "sync" if c % 2 == 0 else "gpsimd"
                    ld[c + 2] = k.dma(eng, wb[0:128, 0:wid],
                                      wsrc[(c + 2) * 128:(c + 3) * 128, :],
                                      wait=[t_rs])
            tt = k.tens(lambda e: e.transpose(psT[:], gacc[:], identfs[:]),
                        wait=[t_add], sig=True)
            k.vec(lambda e: e.reduce_sum(out=ps_slot(S_T0), in_=psT[:], axis=AX.X),
                  wait=[tt], sig=True)
            k.vec(lambda e: e.tensor_scalar(out=ps_slot(S_T1), in0=ps_slot(S_T0),
                                            scalar1=1.0 / (D * DFF1), scalar2=1e-5,
                                            op0=ALU.mult, op1=ALU.add), sig=True)
            k.vec(lambda e: e.reciprocal(out=ps_slot(S_T2), in_=ps_slot(S_T1)), sig=True)
            bar()
            bcast_panel(S_T2, col)
            ld2 = [None] * (nchk + 2)
            ld2[0] = k.dma("sync", wbufq[0][0:128, 0:wid], wsrc[0:128, :],
                           wait=[rd_tok[0]])
            if nchk > 1:
                ld2[1] = k.dma("gpsimd", wbufq[1][0:128, 0:wid], wsrc[128:256, :],
                               wait=[rd_tok[1]])
            for c in range(nchk):
                wb = wbufq[c % 2]; wq = wq16[c % 2]
                k.vec(lambda e, wb=wb, wid=wid, col=col: e.tensor_scalar(
                    out=wb[0:128, 0:wid], in0=wb[0:128, 0:wid],
                    scalar1=pbc[:, col:col + 1], scalar2=MAGIC,
                    op0=ALU.mult, op1=ALU.add), wait=[ld2[c]], sig=True)
                k.vec(lambda e, wb=wb, wid=wid: e.tensor_scalar(
                    out=wb[0:128, 0:wid], in0=wb[0:128, 0:wid],
                    scalar1=MAGIC, scalar2=1.0, op0=ALU.subtract, op1=ALU.min), sig=True)
                t3 = k.vec(lambda e, wb=wb, wq=wq, wid=wid: e.tensor_scalar(
                    out=wq[0:128, 0:wid], in0=wb[0:128, 0:wid],
                    scalar1=-1.0, scalar2=None, op0=ALU.max),
                    wait=[st_tok[c % 2]], sig=True)
                rd_tok[c % 2] = t3
                if c + 2 < nchk:
                    eng = "sync" if c % 2 == 0 else "gpsimd"
                    ld2[c + 2] = k.dma(eng, wb[0:128, 0:wid],
                                       wsrc[(c + 2) * 128:(c + 3) * 128, :],
                                       wait=[t3])
                st_tok[c % 2] = k.dma("gpsimd" if c % 2 == 0 else "sync",
                                      wdst[c * 128:(c + 1) * 128, :],
                                      wq[0:128, 0:wid], wait=[t3])
        bar()
        k.free("psT", "gtmp", "gacc", "wq16b", "wq16a", "wbufq1", "wbufq0")
        TC1 = 256
        h_h = k.sbuf("h_h", [128, 32 * TC1], BF16)
        h_l = k.sbuf("h_l", [128, 32 * TC1], BF16)
        htmp = [k.sbuf("htmp0", [128, TC1]), k.sbuf("htmp1", [128, TC1])]
        w1blk = [k.sbuf("w1blk0", [128, 1024], BF16), k.sbuf("w1blk1", [128, 1024], BF16)]
        w2rows = [k.sbuf("w2rows0", [128, 1024], BF16), k.sbuf("w2rows1", [128, 1024], BF16)]
        w1src = lambda o: AP(w1q_d.tensor, o * 128, [[DFF1, 128], [128 * DFF1, 8], [1, 128]])
        for tc in range(T // TC1):
            psF = [k.psum("psF0", [128, TC1]), k.psum("psF1", [128, TC1])]
            t_l = [None] * 34
            t_mm = [None] * 34
            t_hl = [None] * 34
            t_l[0] = k.dma("sync", w1blk[0][:].rearrange("p (a b) -> p a b", b=128), w1src(0))
            t_l[1] = k.dma("gpsimd", w1blk[1][:].rearrange("p (a b) -> p a b", b=128), w1src(1))
            for o in range(32):
                ps = psF[o % 2]
                for kc in range(NCH):
                    lh = w1blk[o % 2][:, kc * 128:(kc + 1) * 128]
                    rh = XH[:, kc * T + tc * TC1: kc * T + tc * TC1 + TC1]
                    rl = XL[:, kc * T + tc * TC1: kc * T + tc * TC1 + TC1]
                    k.tens(lambda e, ps=ps, lh=lh, rh=rh, first=(kc == 0): e.matmul(
                        ps[:], lh, rh, start=first, stop=False),
                        wait=[t_l[o], t_hl[o - 2]] if kc == 0 else [])
                    t = k.tens(lambda e, ps=ps, lh=lh, rl=rl, last=(kc == NCH - 1): e.matmul(
                        ps[:], lh, rl, start=False, stop=last), sig=(kc == NCH - 1))
                t_mm[o] = t
                if o + 2 < 32:
                    eng = "sync" if o % 2 == 0 else "gpsimd"
                    t_l[o + 2] = k.dma(eng, w1blk[o % 2][:].rearrange("p (a b) -> p a b", b=128),
                                       w1src(o + 2), wait=[t])
                th1 = k.sca(lambda e, o=o, ps=ps: e.activation(
                    out=h_h[:, o * TC1:(o + 1) * TC1], in_=ps[:], func=AF.Relu,
                    bias=b1c[:, o:o + 1], scale=1.0), wait=[t], sig=True)
                th2 = k.sca(lambda e, ps=ps, o=o: e.activation(
                    out=htmp[o % 2][:], in_=ps[:], func=AF.Relu, bias=b1c[:, o:o + 1],
                    scale=1.0), wait=[t_hl[o - 2]], sig=True)
                t_hl[o] = k.vec(lambda e, o=o: e.tensor_sub(
                    out=h_l[:, o * TC1:(o + 1) * TC1], in0=htmp[o % 2][:],
                    in1=h_h[:, o * TC1:(o + 1) * TC1]), wait=[th1, th2], sig=True)
            k.free("psF1", "psF0")
            psD2 = [k.psum("psD2_%d" % i, [128, TC1]) for i in range(NCH)]
            t_l2 = [None] * 34
            t_mml = [None] * 34
            t_l2[0] = k.dma("sync", w2rows[0][:], w2q_d[0:128, :], wait=[t_hl[30]])
            t_l2[1] = k.dma("gpsimd", w2rows[1][:], w2q_d[128:256, :], wait=[t_hl[31]])
            for ic in range(32):
                for dh in range(NCH):
                    lh = w2rows[ic % 2][:, dh * 128:(dh + 1) * 128]
                    rh = h_h[:, ic * TC1:(ic + 1) * TC1]
                    rl = h_l[:, ic * TC1:(ic + 1) * TC1]
                    ps = psD2[dh]
                    k.tens(lambda e, lh=lh, rh=rh, ps=ps, first=(ic == 0): e.matmul(
                        ps[:], lh, rh, start=first, stop=False),
                        wait=[t_l2[ic], t_hl[ic], t_hl[31]] if dh == 0 else [])
                    t = k.tens(lambda e, lh=lh, rl=rl, ps=ps, last=(ic == 31): e.matmul(
                        ps[:], lh, rl, start=False, stop=last), sig=(dh == NCH - 1))
                t_mml[ic] = t
                if ic + 2 < 32:
                    eng = "sync" if ic % 2 == 0 else "gpsimd"
                    t_l2[ic + 2] = k.dma(eng, w2rows[ic % 2][:],
                                         w2q_d[(ic + 2) * 128:(ic + 3) * 128, :], wait=[t])
            bar()
            for dh in range(NCH):
                ps = psD2[dh]
                k.vec(lambda e, dh=dh, tc=tc, ps=ps: e.tensor_tensor(
                    out=X[:, dh * T + tc * TC1: dh * T + tc * TC1 + TC1],
                    in0=X[:, dh * T + tc * TC1: dh * T + tc * TC1 + TC1],
                    in1=ps[:], op=ALU.add), sig=True)
            bar()
            for i in reversed(range(NCH)):
                k.free("psD2_%d" % i)
        k.free("w2rows1", "w2rows0", "w1blk1", "w1blk0", "htmp1", "htmp0",
               "h_l", "h_h", "b1c")

    Bre = k.din("Bre", [NSSM, D]); Bim = k.din("Bim", [NSSM, D])
    Dcol = k.din("Dcol", [128, NCH])

    def emit_ssm_prep():
        lr = k.sbuf("lr", [128, 4]); li = k.sbuf("li", [128, 4])
        dtv = k.sbuf("dtv", [128, 4])
        av = k.sbuf("av", [128, 4]); th = k.sbuf("th", [128, 4])
        lbr = k.sbuf("lbr", [128, 4]); lbi = k.sbuf("lbi", [128, 4])
        lir = k.sbuf("lir", [128, 4]); lii = k.sbuf("lii", [128, 4])
        rr_ = k.sbuf("rr_", [128, 4]); ri_ = k.sbuf("ri_", [128, 4])
        u1 = k.sbuf("u1", [128, 4]); u2 = k.sbuf("u2", [128, 4]); u3 = k.sbuf("u3", [128, 4])
        k.dma("sync", lr[:], lam_re[:]); k.dma("sync", li[:], lam_im[:])
        k.dma("sync", dtv[:], stepv[:])
        bar()
        k.vec(lambda e: e.tensor_scalar_min(out=lr[:], in0=lr[:], scalar1=-1e-4), sig=True)
        bar()
        k.sca(lambda e: e.activation(out=dtv[:], in_=dtv[:], func=AF.Exp,
                                     bias=0.0, scale=1.0), sig=True)
        bar()
        k.vec(lambda e: e.tensor_mul(out=av[:], in0=lr[:], in1=dtv[:]), sig=True)
        k.vec(lambda e: e.tensor_mul(out=th[:], in0=li[:], in1=dtv[:]), sig=True)
        bar()
        k.sca(lambda e: e.activation(out=u1[:], in_=av[:], func=AF.Exp,
                                     bias=0.0, scale=1.0 / 128), sig=True)
        k.sca(lambda e: e.activation(out=u2[:], in_=th[:], func=AF.Sin,
                                     bias=0.0, scale=1.0 / 128), sig=True)
        k.sca(lambda e: e.activation(out=u3[:], in_=th[:], func=AF.Sin,
                                     bias=cstv[:, 1:2], scale=1.0 / 128), sig=True)
        bar()
        k.vec(lambda e: e.tensor_mul(out=lbr[:], in0=u1[:], in1=u3[:]), sig=True)
        k.vec(lambda e: e.tensor_mul(out=lbi[:], in0=u1[:], in1=u2[:]), sig=True)
        bar()
        for _ in range(7):
            k.vec(lambda e: e.tensor_mul(out=u1[:], in0=lbr[:], in1=lbr[:]), sig=True)
            k.vec(lambda e: e.tensor_mul(out=u2[:], in0=lbi[:], in1=lbi[:]), sig=True)
            k.vec(lambda e: e.tensor_mul(out=u3[:], in0=lbr[:], in1=lbi[:]), sig=True)
            bar()
            k.vec(lambda e: e.tensor_sub(out=lbr[:], in0=u1[:], in1=u2[:]), sig=True)
            k.vec(lambda e: e.tensor_scalar_mul(out=lbi[:], in0=u3[:], scalar1=2.0), sig=True)
            bar()
        k.sca(lambda e: e.activation(out=u1[:], in_=av[:], func=AF.Exp,
                                     bias=0.0, scale=-2.0), sig=True)
        bar()
        k.vec(lambda e: e.tensor_mul(out=lir[:], in0=lbr[:], in1=u1[:]), sig=True)
        k.vec(lambda e: e.tensor_mul(out=lii[:], in0=lbi[:], in1=u1[:]), sig=True)
        bar()
        k.vec(lambda e: e.tensor_scalar_mul(out=lii[:], in0=lii[:], scalar1=-1.0), sig=True)
        bar()
        k.vec(lambda e: e.tensor_mul(out=u1[:], in0=lr[:], in1=lr[:]), sig=True)
        k.vec(lambda e: e.tensor_mul(out=u2[:], in0=li[:], in1=li[:]), sig=True)
        bar()
        k.vec(lambda e: e.tensor_add(out=u1[:], in0=u1[:], in1=u2[:]), sig=True)
        bar()
        k.vec(lambda e: e.reciprocal(out=u1[:], in_=u1[:]), sig=True)
        bar()
        k.vec(lambda e: e.tensor_scalar_add(out=u2[:], in0=lbr[:], scalar1=-1.0), sig=True)
        bar()
        k.vec(lambda e: e.tensor_mul(out=rr_[:], in0=u2[:], in1=lr[:]), sig=True)
        k.vec(lambda e: e.tensor_mul(out=u3[:], in0=lbi[:], in1=li[:]), sig=True)
        bar()
        k.vec(lambda e: e.tensor_add(out=rr_[:], in0=rr_[:], in1=u3[:]), sig=True)
        bar()
        k.vec(lambda e: e.tensor_mul(out=rr_[:], in0=rr_[:], in1=u1[:]), sig=True)
        k.vec(lambda e: e.tensor_mul(out=ri_[:], in0=lbi[:], in1=lr[:]), sig=True)
        k.vec(lambda e: e.tensor_mul(out=u3[:], in0=u2[:], in1=li[:]), sig=True)
        bar()
        k.vec(lambda e: e.tensor_sub(out=ri_[:], in0=ri_[:], in1=u3[:]), sig=True)
        bar()
        k.vec(lambda e: e.tensor_mul(out=ri_[:], in0=ri_[:], in1=u1[:]), sig=True)
        bar()
        k.free("u3", "u2", "u1")
        return lr, li, dtv, av, th, lbr, lbi, lir, lii, rr_, ri_

    def emit_ssm_bu(lbr, lbi, lir, lii, rr_, ri_):
        bbth = k.sbuf("bbth", [128, NCH * 512], BF16)
        bbtl = k.sbuf("bbtl", [128, NCH * 512], BF16)
        bbih = k.sbuf("bbih", [128, NCH * 512], BF16)
        bbil = k.sbuf("bbil", [128, NCH * 512], BF16)
        bb1 = k.sbuf("bb1", [128, 1024]); bb2 = k.sbuf("bb2", [128, 1024])
        bb3 = k.sbuf("bb3", [128, 1024]); bb4 = k.sbuf("bb4", [128, 1024])
        psW = k.psum("psW", [128, 128])
        for c in range(4):
            csl = slice(c * 128, (c + 1) * 128)
            k.dma("sync", bb1[:], Bre[csl, :])
            k.dma("gpsimd", bb2[:], Bim[csl, :])
            bar()
            for comp in range(2):
                s1 = (rr_, ri_)[comp]; s2 = (ri_, rr_)[comp]
                dsth = (bbth, bbih)[comp]; dstl = (bbtl, bbil)[comp]
                k.vec(lambda e, s1=s1, c=c: e.tensor_scalar(
                    out=bb3[:], in0=bb1[:], scalar1=s1[:, c:c + 1], scalar2=None,
                    op0=ALU.mult), sig=True)
                k.vec(lambda e, s2=s2, c=c: e.tensor_scalar(
                    out=bb4[:], in0=bb2[:], scalar1=s2[:, c:c + 1], scalar2=None,
                    op0=ALU.mult), sig=True)
                bar()
                op = ALU.subtract if comp == 0 else ALU.add
                k.vec(lambda e, op=op: e.tensor_tensor(out=bb3[:], in0=bb3[:],
                                                       in1=bb4[:], op=op), sig=True)
                bar()
                for hc in range(NCH):
                    k.tens(lambda e, hc=hc: e.transpose(
                        psW[:], bb3[:, hc * 128:(hc + 1) * 128], identfs[:]), sig=True)
                    bar()
                    dst = slice(hc * 512 + c * 128, hc * 512 + c * 128 + 128)
                    k.sca(lambda e, dst=dst, dsth=dsth: e.activation(
                        out=dsth[:, dst], in_=psW[:], func=AF.Copy,
                        bias=0.0, scale=1.0), sig=True)
                    bar()
                    k.vec(lambda e, dst=dst, dsth=dsth, dstl=dstl: e.scalar_tensor_tensor(
                        out=dstl[:, dst], in0=psW[:], scalar=1.0, in1=dsth[:, dst],
                        op0=ALU.mult, op1=ALU.subtract), sig=True)
                    bar()
        k.free("psW", "bb4", "bb3", "bb2", "bb1")
        stage = k.sbuf("stage", [128, 512])
        psB2 = [k.psum("psB20"), k.psum("psB21")]
        for c in range(4):
            for tc in range(4):
                for comp in range(2):
                    ps = psB2[comp]
                    hsrc = (bbth, bbih)[comp]; lsrc = (bbtl, bbil)[comp]
                    spdst = (sp_bur, sp_bui)[comp]
                    for hc in range(NCH):
                        lh = hsrc[:, hc * 512 + c * 128: hc * 512 + c * 128 + 128]
                        ll = lsrc[:, hc * 512 + c * 128: hc * 512 + c * 128 + 128]
                        rh = XH[:, hc * T + tc * 512: hc * T + tc * 512 + 512]
                        rl = XL[:, hc * T + tc * 512: hc * T + tc * 512 + 512]
                        k.tens(lambda e, ps=ps, lh=lh, rh=rh, first=(hc == 0): e.matmul(
                            ps[:], lh, rh, start=first, stop=False))
                        k.tens(lambda e, ps=ps, lh=lh, rl=rl: e.matmul(
                            ps[:], lh, rl, start=False, stop=False))
                        k.tens(lambda e, ps=ps, ll=ll, rh=rh, last=(hc == NCH - 1): e.matmul(
                            ps[:], ll, rh, start=False, stop=last), sig=(hc == NCH - 1))
                    bar()
                    k.sca(lambda e, ps=ps: e.activation(out=stage[:], in_=ps[:],
                                                        func=AF.Copy, bias=0.0,
                                                        scale=1.0), sig=True)
                    bar()
                    k.dma("sync", spdst[c * 128:(c + 1) * 128, tc * 512:(tc + 1) * 512],
                          stage[:])
                    bar()
        k.free("psB21", "psB20", "stage")
        k.free("bbil", "bbih", "bbtl", "bbth")
        v32 = k.sbuf("duv", [128, T])
        dcl = k.sbuf("dcl", [128, NCH])
        k.dma("sync", dcl[:], Dcol[:])
        bar()
        for hc in range(NCH):
            sl = slice(hc * T, (hc + 1) * T)
            k.vec(lambda e, sl=sl: e.tensor_add(out=v32[:], in0=XH[:, sl], in1=XL[:, sl]), sig=True)
            bar()
            k.vec(lambda e, hc=hc: e.tensor_scalar(
                out=v32[:], in0=v32[:], scalar1=dcl[:, hc:hc + 1], scalar2=None,
                op0=ALU.mult), sig=True)
            bar()
            k.dma("sync", sp_du[hc * 128:(hc + 1) * 128, :], v32[:])
            bar()
        k.free("dcl", "duv")
        cb1 = k.sbuf("cb1", [128, 1024])
        cbh = k.sbuf("cbh", [128, 1024], BF16)
        for src, hname, lname, neg in ((CreT, "reh", "rel", 1.0), (CimT, "imh", "iml", -1.0)):
            for c in range(4):
                csl = slice(c * 128, (c + 1) * 128)
                k.dma("sync", cb1[:], src[csl, :])
                bar()
                k.sca(lambda e, neg=neg: e.activation(out=cbh[:], in_=cb1[:],
                                                      func=AF.Copy, bias=0.0,
                                                      scale=neg), sig=True)
                bar()
                k.vec(lambda e, neg=neg: e.scalar_tensor_tensor(
                    out=cb1[:], in0=cb1[:], scalar=neg, in1=cbh[:],
                    op0=ALU.mult, op1=ALU.subtract), sig=True)
                bar()
                k.dma("sync", csp_d[hname][csl, :], cbh[:])
                bar()
                k.vec(lambda e: e.tensor_copy(out=cbh[:], in_=cb1[:]), sig=True)
                bar()
                k.dma("sync", csp_d[lname][csl, :], cbh[:])
                bar()
        k.free("cbh", "cb1")

    def emit_ssm_core(lbr, lbi, lir, lii):
        Tre = k.sbuf("Tre", [128, 4 * 512]); Tim = k.sbuf("Tim", [128, 4 * 512])
        smr = k.sbuf("smr", [128, 1]); smi = k.sbuf("smi", [128, 1])
        t1 = k.sbuf("sst1", [128, 512]); t2 = k.sbuf("sst2", [128, 512])
        onet = k.sbuf("onet", [128, 512])
        k.vec(lambda e: e.memset(onet[:], 1.0), sig=True)
        for Tr, Ti, br, bi in ((Tre, Tim, lbr, lbi),):
            for c in range(4):
                base = c * 512
                k.vec(lambda e, Tr=Tr, base=base: e.memset(Tr[:, base:base + 1], 1.0), sig=True)
                k.vec(lambda e, Ti=Ti, base=base: e.memset(Ti[:, base:base + 1], 0.0), sig=True)
                m = 1
                while m < 512:
                    k.vec(lambda e, Tr=Tr, base=base, m=m, br=br, c=c: e.tensor_scalar(
                        out=smr[:], in0=Tr[:, base + m - 1: base + m],
                        scalar1=br[:, c:c + 1], scalar2=None, op0=ALU.mult))
                    k.vec(lambda e, Ti=Ti, base=base, m=m, bi=bi, c=c: e.tensor_scalar(
                        out=t1[:, 0:1], in0=Ti[:, base + m - 1: base + m],
                        scalar1=bi[:, c:c + 1], scalar2=None, op0=ALU.mult))
                    k.vec(lambda e: e.tensor_sub(out=smr[:], in0=smr[:], in1=t1[:, 0:1]))
                    k.vec(lambda e, Tr=Tr, base=base, m=m, bi=bi, c=c: e.tensor_scalar(
                        out=smi[:], in0=Tr[:, base + m - 1: base + m],
                        scalar1=bi[:, c:c + 1], scalar2=None, op0=ALU.mult))
                    k.vec(lambda e, Ti=Ti, base=base, m=m, br=br, c=c: e.tensor_scalar(
                        out=t1[:, 0:1], in0=Ti[:, base + m - 1: base + m],
                        scalar1=br[:, c:c + 1], scalar2=None, op0=ALU.mult))
                    k.vec(lambda e: e.tensor_add(out=smi[:], in0=smi[:], in1=t1[:, 0:1]))
                    w = min(m, 512 - m)
                    k.vec(lambda e, Tr=Tr, base=base, w=w: e.tensor_scalar(
                        out=t1[:, 0:w], in0=Tr[:, base:base + w], scalar1=smr[:, 0:1],
                        scalar2=None, op0=ALU.mult))
                    k.vec(lambda e, Ti=Ti, base=base, w=w: e.tensor_scalar(
                        out=t2[:, 0:w], in0=Ti[:, base:base + w], scalar1=smi[:, 0:1],
                        scalar2=None, op0=ALU.mult))
                    k.vec(lambda e, Tr=Tr, base=base, m=m, w=w: e.tensor_sub(
                        out=Tr[:, base + m:base + m + w], in0=t1[:, 0:w], in1=t2[:, 0:w]))
                    k.vec(lambda e, Tr=Tr, base=base, w=w: e.tensor_scalar(
                        out=t1[:, 0:w], in0=Tr[:, base:base + w], scalar1=smi[:, 0:1],
                        scalar2=None, op0=ALU.mult))
                    k.vec(lambda e, Ti=Ti, base=base, w=w: e.tensor_scalar(
                        out=t2[:, 0:w], in0=Ti[:, base:base + w], scalar1=smr[:, 0:1],
                        scalar2=None, op0=ALU.mult))
                    k.vec(lambda e, Ti=Ti, base=base, m=m, w=w: e.tensor_add(
                        out=Ti[:, base + m:base + m + w], in0=t1[:, 0:w], in1=t2[:, 0:w]))
                    m *= 2
        nvt = k.sbuf("nvt", [128, 512])
        carr = k.sbuf("carr", [128, 4]); cari = k.sbuf("cari", [128, 4])
        inr = k.sbuf("inr", [128, 1]); ini = k.sbuf("ini", [128, 1])
        k.vec(lambda e: e.memset(carr[:], 0.0), sig=True)
        k.vec(lambda e: e.memset(cari[:], 0.0), sig=True)
        bur = [k.sbuf("bur0", [128, 512]), k.sbuf("bur1", [128, 512])]
        bui = [k.sbuf("bui0", [128, 512]), k.sbuf("bui1", [128, 512])]
        gre = k.sbuf("gre", [128, 512]); gim = k.sbuf("gim", [128, 512])
        cre = k.sbuf("cre", [128, 512]); cim = k.sbuf("cim", [128, 512])
        srh = k.sbuf("srh", [128, 4 * 512], BF16); srl = k.sbuf("srl", [128, 4 * 512], BF16)
        sih = k.sbuf("sih", [128, 4 * 512], BF16); sil = k.sbuf("sil", [128, 4 * 512], BF16)
        crh = k.sbuf("crh", [128, 4 * 512], BF16); crl = k.sbuf("crl", [128, 4 * 512], BF16)
        cih = k.sbuf("cih", [128, 4 * 512], BF16); cil = k.sbuf("cil", [128, 4 * 512], BF16)
        dut = k.sbuf("dut", [128, 512])
        dt2 = k.sbuf("dt2", [128, 512])
        t_relu_prev = [None]
        psY = [k.psum("psY0"), k.psum("psY1")]
        bar()
        t_lb = {}
        t_bufree = [None]
        t_lb[(0, 0)] = (k.dma("sync", bur[0][:], sp_bur[0:128, 0:512]),
                        k.dma("gpsimd", bui[0][:], sp_bui[0:128, 0:512]))
        t_split = {}
        t_ymm_all = [None]
        t_cload = [None]
        t_xadd = [None, None]
        t_duld = {}
        for tc in range(4):
            tsl = slice(tc * 512, (tc + 1) * 512)
            for c in range(4):
                base = c * 512
                nc_ = (tc, c)
                nxt = (tc, c + 1) if c < 3 else (tc + 1, 0)
                b_r = bur[c % 2]; b_i = bui[c % 2]
                tlr, tli = t_lb[nc_]
                k.vec(lambda e, base=base: e.tensor_mul(out=t1[:], in0=Tre[:, base:base + 512],
                                                        in1=Tre[:, base:base + 512]),
                      wait=[t_ymm_all[0]])
                k.vec(lambda e, base=base: e.tensor_mul(out=t2[:], in0=Tim[:, base:base + 512],
                                                        in1=Tim[:, base:base + 512]))
                k.vec(lambda e: e.tensor_add(out=t1[:], in0=t1[:], in1=t2[:]))
                k.vec(lambda e: e.reciprocal(out=nvt[:], in_=t1[:]))
                k.vec(lambda e, b_r=b_r, base=base: e.tensor_mul(
                    out=gre[:], in0=b_r[:], in1=Tre[:, base:base + 512]),
                    wait=[tlr])
                k.vec(lambda e, b_i=b_i, base=base: e.tensor_mul(
                    out=t1[:], in0=b_i[:], in1=Tim[:, base:base + 512]), wait=[tli])
                k.vec(lambda e: e.tensor_add(out=gre[:], in0=gre[:], in1=t1[:]))
                k.vec(lambda e: e.tensor_mul(out=gre[:], in0=gre[:], in1=nvt[:]))
                k.vec(lambda e, b_i=b_i, base=base: e.tensor_mul(
                    out=gim[:], in0=b_i[:], in1=Tre[:, base:base + 512]))
                t_bufree[0] = k.vec(lambda e, b_r=b_r, base=base: e.tensor_mul(
                    out=t1[:], in0=b_r[:], in1=Tim[:, base:base + 512]), sig=True)
                k.vec(lambda e: e.tensor_sub(out=gim[:], in0=gim[:], in1=t1[:]))
                k.vec(lambda e: e.tensor_mul(out=gim[:], in0=gim[:], in1=nvt[:]))
                if nxt[0] < 4:
                    nb = bur[(c + 1) % 2]; nbi_ = bui[(c + 1) % 2]
                    t_lb[nxt] = (
                        k.dma("sync", nb[:], sp_bur[nxt[1] * 128:(nxt[1] + 1) * 128,
                                                    nxt[0] * 512:(nxt[0] + 1) * 512],
                              wait=[t_bufree[0]]),
                        k.dma("gpsimd", nbi_[:], sp_bui[nxt[1] * 128:(nxt[1] + 1) * 128,
                                                        nxt[0] * 512:(nxt[0] + 1) * 512],
                              wait=[t_bufree[0]]))
                k.vec(lambda e, c=c: e.tensor_scalar(out=inr[:], in0=carr[:, c:c + 1],
                                                     scalar1=lbr[:, c:c + 1], scalar2=None,
                                                     op0=ALU.mult))
                k.vec(lambda e, c=c: e.tensor_scalar(out=t1[:, 0:1], in0=cari[:, c:c + 1],
                                                     scalar1=lbi[:, c:c + 1], scalar2=None,
                                                     op0=ALU.mult))
                k.vec(lambda e: e.tensor_sub(out=inr[:], in0=inr[:], in1=t1[:, 0:1]))
                k.vec(lambda e, c=c: e.tensor_scalar(out=ini[:], in0=carr[:, c:c + 1],
                                                     scalar1=lbi[:, c:c + 1], scalar2=None,
                                                     op0=ALU.mult))
                k.vec(lambda e, c=c: e.tensor_scalar(out=t1[:, 0:1], in0=cari[:, c:c + 1],
                                                     scalar1=lbr[:, c:c + 1], scalar2=None,
                                                     op0=ALU.mult))
                k.vec(lambda e: e.tensor_add(out=ini[:], in0=ini[:], in1=t1[:, 0:1]))
                k.vec(lambda e: e.tensor_tensor_scan(out=cre[:], data0=onet[:], data1=gre[:],
                                                     initial=inr[:], op0=ALU.mult,
                                                     op1=ALU.add))
                k.vec(lambda e: e.tensor_tensor_scan(out=cim[:], data0=onet[:], data1=gim[:],
                                                     initial=ini[:], op0=ALU.mult,
                                                     op1=ALU.add))
                k.vec(lambda e, base=base: e.tensor_mul(out=t1[:], in0=cre[:],
                                                        in1=Tre[:, base:base + 512]))
                k.vec(lambda e, base=base: e.tensor_mul(out=t2[:], in0=cim[:],
                                                        in1=Tim[:, base:base + 512]))
                tsr = k.vec(lambda e: e.tensor_sub(out=t1[:], in0=t1[:], in1=t2[:]), sig=True)
                k.vec(lambda e, c=c: e.tensor_copy(out=carr[:, c:c + 1], in_=t1[:, 511:512]))
                th1 = k.sca(lambda e, base=base: e.activation(
                    out=srh[:, base:base + 512], in_=t1[:], func=AF.Copy, bias=0.0,
                    scale=1.0), wait=[tsr, t_ymm_all[0]], sig=True)
                tl1 = k.vec(lambda e, base=base: e.scalar_tensor_tensor(
                    out=srl[:, base:base + 512], in0=t1[:], scalar=1.0,
                    in1=srh[:, base:base + 512], op0=ALU.mult, op1=ALU.subtract),
                    wait=[th1], sig=True)
                k.vec(lambda e, base=base: e.tensor_mul(out=t1[:], in0=cim[:],
                                                        in1=Tre[:, base:base + 512]))
                k.vec(lambda e, base=base: e.tensor_mul(out=t2[:], in0=cre[:],
                                                        in1=Tim[:, base:base + 512]))
                tsi = k.vec(lambda e: e.tensor_add(out=t1[:], in0=t1[:], in1=t2[:]), sig=True)
                k.vec(lambda e, c=c: e.tensor_copy(out=cari[:, c:c + 1], in_=t1[:, 511:512]))
                th2 = k.sca(lambda e, base=base: e.activation(
                    out=sih[:, base:base + 512], in_=t1[:], func=AF.Copy, bias=0.0,
                    scale=1.0), wait=[tsi], sig=True)
                tl2 = k.vec(lambda e, base=base: e.scalar_tensor_tensor(
                    out=sil[:, base:base + 512], in0=t1[:], scalar=1.0,
                    in1=sih[:, base:base + 512], op0=ALU.mult, op1=ALU.subtract),
                    wait=[th2], sig=True)
                t_split[nc_] = tl2
            for dhh in range(2):
                dsl = slice(dhh * 512, (dhh + 1) * 512)
                tcl = []
                for c in range(4):
                    csl = slice(c * 128, (c + 1) * 128)
                    tcl.append(k.dma("sync", crh[:, c * 512:(c + 1) * 512],
                                     csp_d["reh"][csl, dsl],
                                     wait=[t_xadd[0], t_xadd[1]]))
                    tcl.append(k.dma("gpsimd", crl[:, c * 512:(c + 1) * 512],
                                     csp_d["rel"][csl, dsl]))
                    tcl.append(k.dma("sync", cih[:, c * 512:(c + 1) * 512],
                                     csp_d["imh"][csl, dsl]))
                    tcl.append(k.dma("gpsimd", cil[:, c * 512:(c + 1) * 512],
                                     csp_d["iml"][csl, dsl]))
                for dh in range(dhh * 4, dhh * 4 + 4):
                    ps = psY[dh % 2]
                    dho = dh - dhh * 4
                    t_duld[dh] = k.dma("gpsimd", dut[:],
                                       sp_du[dh * 128:(dh + 1) * 128, tsl],
                                       wait=[t_xadd[0], t_xadd[1]])
                    nmm = 0
                    for c in range(4):
                        for hsrc, lsrc, sh, sl_ in ((crh, crl, srh, srl),
                                                    (cih, cil, sih, sil)):
                            lh = hsrc[:, c * 512 + dho * 128: c * 512 + dho * 128 + 128]
                            ll = lsrc[:, c * 512 + dho * 128: c * 512 + dho * 128 + 128]
                            rh = sh[:, c * 512:(c + 1) * 512]
                            rl = sl_[:, c * 512:(c + 1) * 512]
                            w0 = ([t_split[(tc, 3)], t_xadd[dh % 2]] + tcl) if nmm == 0 else []
                            k.tens(lambda e, ps=ps, lh=lh, rh=rh, first=(nmm == 0): e.matmul(
                                ps[:], lh, rh, start=first, stop=False), wait=w0)
                            k.tens(lambda e, ps=ps, lh=lh, rl=rl: e.matmul(
                                ps[:], lh, rl, start=False, stop=False))
                            last = (nmm == 7)
                            t = k.tens(lambda e, ps=ps, ll=ll, rh=rh, last=last: e.matmul(
                                ps[:], ll, rh, start=False, stop=last), sig=last)
                            nmm += 1
                    tv1 = k.vec(lambda e, ps=ps: e.tensor_add(
                        out=dt2[:], in0=dut[:], in1=ps[:]),
                        wait=[t, t_duld[dh], t_relu_prev[0]], sig=True)
                    tv2 = k.sca(lambda e: e.activation(
                        out=dt2[:], in_=dt2[:], func=AF.Relu,
                        bias=0.0, scale=1.0), wait=[tv1], sig=True)
                    t_relu_prev[0] = tv2
                    t_xadd[dh % 2] = k.vec(lambda e, dh=dh, tc=tc: e.tensor_tensor(
                        out=X[:, dh * T + tc * 512: dh * T + tc * 512 + 512],
                        in0=X[:, dh * T + tc * 512: dh * T + tc * 512 + 512],
                        in1=dt2[:], op=ALU.add), wait=[tv2], sig=True)
            t_ymm_all[0] = t_xadd[1]
        bar()
        k.free("psY1", "psY0")
        k.free("dt2", "dut", "cil", "cih", "crl", "crh",
               "sil", "sih", "srl", "srh", "cim", "cre", "gim", "gre",
               "bui1", "bui0", "bur1", "bur0", "ini", "inr", "cari", "carr", "nvt",
               "onet", "sst2", "sst1", "smi", "smr", "Tim", "Tre")

    ALL_PHASES = ("bias", "ln0", "ar1", "attn", "xq", "ffn0mm", "ffn0ew",
                  "ffn0down", "ln1", "ffn1", "ssm")
    if phases is None:
        phases = ALL_PHASES
    lambox = {}

    def run_phase(name):
        if name == "bias":
            emit_bias_prep()
        elif name == "ln0" or name == "ln1":
            emit_ln()
        elif name == "ar1":
            emit_allreduce_max([S_XG])
        elif name == "attn":
            emit_attention()
        elif name == "xq":
            emit_xq()
        elif name == "ffn0mm":
            emit_ffn0_mm()
            emit_allreduce_max([S_UG, S_GG])
            emit_scales_after_ar2()
        elif name == "ffn0ew":
            emit_ffn0_ew_B()
            emit_allreduce_max([S_QG])
            emit_scales_after_ar3()
            emit_ffn0_ew_CD()
            emit_allreduce_max([S_PG])
            emit_scales_after_ar4()
            emit_ffn0_ew_E()
        elif name == "ffn0down":
            emit_ffn0_down_prep()
            emit_ffn0_down()
        elif name == "ffn1":
            emit_ffn1()
        elif name == "ssm":
            lam = emit_ssm_prep()
            lr, li, dtv, av, th, lbr, lbi, lir, lii, rr_, ri_ = lam
            emit_ssm_bu(lbr, lbi, lir, lii, rr_, ri_)
            emit_ssm_core(lbr, lbi, lir, lii)

    for name in phases:
        run_phase(name)
    for nm, buf, dt_ in (("XH", XH, BF16), ("XL", XL, BF16)):
        if nm in k.dbg_names:
            dt = k.dout("dbg_" + nm, [128, NCH * T], dt_)
            k.dma("sync", dt[:], buf[:])
    if "panel" in k.dbg_names:
        dt = k.dout("dbg_panel", [1, 64], F32)
        k.dma("sync", dt[:], panel[:])
    if "pbc" in k.dbg_names:
        dt = k.dout("dbg_pbc", [128, 16], F32)
        k.dma("sync", dt[:], pbc[:])
    for nm, dr, shp, dt_ in (("w1q", w1q_d, [D, DFF1], BF16),
                             ("w2q", w2q_d, [DFF1, D], BF16),
                             ("spg2k", sp_g2k, [IFF0, T], I16),
                             ("spup", sp_up, [IFF0, T], F32),
                             ("spgate", sp_gate, [IFF0, T], F32),
                             ("spq", sp_q, [IFF0, T], F32),
                             ("spupk", sp_upk, [IFF0, T], I16),
                             ("spoh", sp_oh, [IFF0, T], BF16),
                             ("spol", sp_ol, [IFF0, T], BF16)):
        if nm in k.dbg_names:
            dt = k.dout("dbg_" + nm, shp, dt_)
            k.dma("sync", dt[:], dr[:])
    for c in range(NCH):
        k.dma("sync", out[c * 128:(c + 1) * 128, :], X[:, c * T:(c + 1) * T])
    bar()

    p.check()
    handles = {}
    from contextlib import ExitStack
    es = ExitStack()
    for s in SEMS:
        handles[s] = es.enter_context(nc.semaphore("sem_" + s))
    with nc.Block() as block:
        p.emit(nc, block, handles)
    es.close()
    return k



def host_inputs(inputs):
    import ml_dtypes
    bf16 = ml_dtypes.bfloat16
    C = np.ascontiguousarray
    f32 = np.float32
    g = lambda n: np.asarray(inputs[n], dtype=f32)
    halfA = np.zeros((128, 1), f32); halfA[:64] = 1.0
    halfB = np.zeros((128, 1), f32); halfB[64:] = 1.0
    shared = {
        "wqB": C(g("L0_attn_wq").T.reshape(D, 8, 128).transpose(1, 0, 2)),
        "wkB": C(g("L0_attn_wk").T.reshape(D, 8, 128).transpose(1, 0, 2)),
        "wvB": C(g("L0_attn_wv").T.reshape(D, 8, 128).transpose(1, 0, 2)),
        "woT": C(g("L0_attn_wo").T),
        "bq": g("L0_attn_bq").reshape(1, D), "bk": g("L0_attn_bk").reshape(1, D),
        "bqcol": C(g("L0_attn_bq").reshape(8, 128).T),
        "bkcol": C(g("L0_attn_bk").reshape(8, 128).T),
        "bvcol": C(g("L0_attn_bv").reshape(8, 128).T),
        "bocol": C(g("L0_attn_bo").reshape(8, 128).T),
        "b2col": C(g("L1_ffn_b2").reshape(8, 128).T),
        "bv": g("L0_attn_bv").reshape(1, D), "bo": g("L0_attn_bo").reshape(1, D),
        "relb": g("L0_attn_rel_bias"),
        "onehotT": host_bucket_onehot(),
        "identc": np.eye(128, dtype=f32).astype(bf16),
        "identf": np.eye(128, dtype=f32),
        "ones_r": np.ones((1, 128), f32),
        "onesall": np.ones((128, 128), f32),
        "ones_c": np.ones((128, 1), f32),
        "halfA_c": halfA, "halfB_c": halfB,
        "onesrow_b": np.ones((1, 512), f32).astype(bf16),
        "cstv_d": np.tile(np.array([[1e-5, np.pi / 2]], f32), (128, 1)),
        "wup": g("L0_ffn_w_up"), "wgate": g("L0_ffn_w_gate"),
        "wdown": g("L0_ffn_w_down"),
        "lam_re": C(g("L1_attn_Lambda_re").reshape(4, 128).T),
        "lam_im": C(g("L1_attn_Lambda_imag").reshape(4, 128).T),
        "stepv": C(g("L1_attn_step").reshape(-1).reshape(4, 128).T),
        "Bre": C(np.asarray(inputs["L1_attn_B"])[:, :, 0].astype(f32)),
        "Bim": C(np.asarray(inputs["L1_attn_B"])[:, :, 1].astype(f32)),
        "CreT": C(np.asarray(inputs["L1_attn_C"])[:, :, 0].T.astype(f32)),
        "CimT": C(np.asarray(inputs["L1_attn_C"])[:, :, 1].T.astype(f32)),
        "Dcol": C(g("L1_attn_D").reshape(8, 128).T),
        "Dvec": g("L1_attn_D").reshape(1, D),
        "w1T": C(g("L1_ffn_W1").T), "b1": g("L1_ffn_b1").reshape(1, DFF1),
        "b1col": C(g("L1_ffn_b1").reshape(32, 128).T),
        "w2T": C(g("L1_ffn_W2").T), "b2": g("L1_ffn_b2").reshape(1, D),
    }
    x = np.asarray(inputs["L0_attn_x"], dtype=f32)
    per_core = [dict(shared, x=C(x[b].T)) for b in range(x.shape[0])]
    return per_core


def kernel(**inputs):
    k = build_kernel(n_cores=8)
    in_maps = host_inputs(inputs)
    res = run_bass_kernel_spmd(k.nc, in_maps, core_ids=list(range(8)))
    outs = [np.ascontiguousarray(r["out"].T) for r in res.results]
    return np.stack(outs, axis=0)


# version 11 (final)
# speedup vs baseline: 1.0248x; 1.0248x over previous
import numpy as np
import concourse.bass as bass
import concourse.mybir as mybir
from concourse.bass import AP
from concourse.bass_utils import run_bass_kernel_spmd

F32 = mybir.dt.float32
BF16 = mybir.dt.bfloat16
I16 = mybir.dt.int16
I32 = mybir.dt.int32
ALU = mybir.AluOpType
AF = mybir.ActivationFunctionType
AX = mybir.AxisListType

D = 1024
T = 2048
NCH = 8
HEADS = 16
DH = 64
IFF0 = 3072
DFF1 = 4096
NSSM = 512
MAGIC = 12582912.0

VW = 4095
PITCH = 4352

ENGINES = ("tensor", "vector", "scalar", "gpsimd", "sync")
SEMS = ("T", "V", "S", "G", "Dy", "Dg", "Ds")
ENG_SEM = {"tensor": "T", "vector": "V", "scalar": "S", "gpsimd": "G"}
DMA_SEM = {"sync": "Dy", "gpsimd": "Dg", "scalar": "Ds"}


class Prog:

    def __init__(self):
        self.q = {e: [] for e in ENGINES}
        self.cnt = {s: 0 for s in SEMS}

    AUTO = ("vector", "scalar", "gpsimd")

    def do(self, eng, fn, wait=(), sig=False, dma=False):
        waits = []
        own = ENG_SEM.get(eng)
        for tok in wait:
            if tok is None:
                continue
            sn, val = tok
            if sn == own:
                assert val <= self.cnt[sn], "forward wait on own engine"
                continue
            waits.append((sn, val))
        sig_spec = None
        tok = None
        auto = (eng in Prog.AUTO) and (fn is not None) and not dma
        if auto:
            sn = ENG_SEM[eng]
            waits.append((sn, self.cnt[sn]))
            self.cnt[sn] += 1
            sig_spec = (sn, 1)
            tok = (sn, self.cnt[sn])
        elif sig:
            sn = DMA_SEM[eng] if dma else ENG_SEM[eng]
            amt = 16 if dma else 1
            self.cnt[sn] += amt
            sig_spec = (sn, amt)
            tok = (sn, self.cnt[sn])
        self.q[eng].append({"waits": waits, "fn": fn, "sig": sig_spec})
        return tok

    def barrier_tokens(self):
        return [(s, self.cnt[s]) for s in SEMS if self.cnt[s] > 0]

    def barrier(self, toks=None):
        toks = self.barrier_tokens() if toks is None else [t for t in toks if t]
        for e in ENGINES:
            own = ENG_SEM.get(e)
            need = [(sn, v) for (sn, v) in toks if sn != own]
            if need:
                self.q[e].append({"waits": need, "fn": None, "sig": None})

    def check(self):
        ptr = {e: 0 for e in ENGINES}
        cnt = {s: 0 for s in SEMS}
        progress = True
        while progress:
            progress = False
            for e in ENGINES:
                while ptr[e] < len(self.q[e]):
                    it = self.q[e][ptr[e]]
                    if all(cnt[sn] >= v for sn, v in it["waits"]):
                        if it["sig"]:
                            sn, amt = it["sig"]
                            cnt[sn] += amt
                        ptr[e] += 1
                        progress = True
                    else:
                        break
        stuck = {e: (ptr[e], len(self.q[e])) for e in ENGINES if ptr[e] < len(self.q[e])}
        if stuck:
            detail = {}
            for e in stuck:
                it = self.q[e][ptr[e]]
                detail[e] = (ptr[e], it["waits"], {s: cnt[s] for s in SEMS})
            raise RuntimeError(f"program deadlock: {detail}")

    def emit(self, nc, block, handles):
        for ename in ENGINES:
            items = self.q[ename]

            def body(eng, items=items):
                for it in items:
                    seen = {}
                    for sn, v in it["waits"]:
                        seen[sn] = max(seen.get(sn, 0), v)
                    for sn, v in seen.items():
                        eng.wait_ge(handles[sn], v)
                    ins = None
                    if it["fn"] is not None:
                        ins = it["fn"](eng)
                    if it["sig"] is not None:
                        sn, amt = it["sig"]
                        ins.then_inc(handles[sn], amt)

            getattr(block, ename)(body)


def host_bucket_onehot():
    t = np.arange(VW)
    n = (t - 2047).astype(np.int64)
    nf = np.maximum(n, 1).astype(np.float32)
    val = np.int64(16) + (
        np.log(nf / np.float32(16.0)).astype(np.float32)
        / np.float32(np.log(np.float64(8.0)))
        * np.float32(16.0)
    ).astype(np.int32)
    val = np.minimum(val, 31).astype(np.int64)
    ret = np.where(n < 16, n, val)
    bucket = np.maximum(ret, 0).astype(np.int64)
    oh = np.zeros((32, PITCH), np.float32)
    oh[bucket, t] = 1.0
    return oh


class K:
    def __init__(self, n_cores=8, dbg=()):
        self.nc = bass.Bass("TRN2", target_bir_lowering=False, debug=False)
        self.p = Prog()
        self.n_cores = n_cores
        self.dbg_names = set(dbg)
        self.dbg_outs = {}
        self.inputs = {}
        self.guards = []
        self.sb = {}

    def din(self, name, shape, dtype=F32):
        ap = self.nc.dram_tensor(name, list(shape), dtype, kind="ExternalInput").ap()
        self.inputs[name] = ap
        return ap

    def dscratch(self, name, shape, dtype=F32):
        return self.nc.dram_tensor(name, list(shape), dtype, kind="Internal").ap()

    def dout(self, name, shape, dtype=F32):
        return self.nc.dram_tensor(name, list(shape), dtype, kind="ExternalOutput").ap()

    def dbg(self, name, shape, dtype=F32):
        if name not in self.dbg_names:
            return None
        t = self.dout("dbg_" + name, shape, dtype)
        self.dbg_outs[name] = t
        return t

    _uid = 0

    def _uname(self, name):
        K._uid += 1
        return f"{name}__u{K._uid}"

    def sbuf(self, name, shape, dtype=F32):
        g = self.nc.sbuf_tensor(self._uname(name), list(shape), dtype)
        h = g.__enter__()
        self.guards.append((name, g))
        self.sb[name] = h
        return h

    def psum(self, name, shape=(128, 512), dtype=F32):
        g = self.nc.psum_tensor(self._uname(name), list(shape), dtype)
        h = g.__enter__()
        self.guards.append((name, g))
        self.sb[name] = h
        return h

    def free(self, *names):
        names = list(names)
        while names:
            nm, g = self.guards[-1]
            assert nm in names, f"free order: top is {nm}, wanted {names}"
            g.__exit__(None, None, None)
            self.guards.pop()
            self.sb.pop(nm)
            names.remove(nm)

    def dma(self, eng, out, in_, wait=()):
        return self.p.do(eng, lambda e: e.dma_start(out=out, in_=in_),
                         wait=wait, sig=True, dma=True)

    def tens(self, fn, wait=(), sig=False):
        return self.p.do("tensor", fn, wait=wait, sig=sig)

    def vec(self, fn, wait=(), sig=False):
        return self.p.do("vector", fn, wait=wait, sig=sig)

    def sca(self, fn, wait=(), sig=False):
        return self.p.do("scalar", fn, wait=wait, sig=sig)

    def gps(self, fn, wait=(), sig=False):
        return self.p.do("gpsimd", fn, wait=wait, sig=sig)

    def split_evac(self, ps, hi, lo, scale=1.0, wait=()):
        th = self.sca(lambda e, ps=ps, hi=hi, scale=scale: e.activation(
            out=hi, in_=ps, func=AF.Copy, bias=0.0, scale=float(scale)),
            wait=wait, sig=True)
        tl = self.vec(lambda e, ps=ps, hi=hi, lo=lo, scale=scale:
                      e.scalar_tensor_tensor(out=lo, in0=ps, scalar=float(scale),
                                             in1=hi, op0=ALU.mult, op1=ALU.subtract),
                      wait=list(wait) + [th], sig=True)
        return tl




def build_kernel(n_cores=8, dbg=(), phases=None):
    k = K(n_cores=n_cores, dbg=dbg)
    nc, p = k.nc, k.p
    bar = p.barrier

    x_in = k.din("x", [D, T])
    wqB = k.din("wqB", [NCH, D, 128])
    wkB = k.din("wkB", [NCH, D, 128])
    wvB = k.din("wvB", [NCH, D, 128])
    woT = k.din("woT", [D, D])
    bq = k.din("bq", [1, D]); bk = k.din("bk", [1, D])
    bv = k.din("bv", [1, D]); bo = k.din("bo", [1, D])
    relb = k.din("relb", [32, 16])
    onehotT = k.din("onehotT", [32, PITCH])
    identc = k.din("identc", [128, 128], BF16)
    ones_r = k.din("ones_r", [1, 128])
    onesall = k.din("onesall", [128, 128])
    ones_c = k.din("ones_c", [128, 1])
    halfA_c = k.din("halfA_c", [128, 1])
    halfB_c = k.din("halfB_c", [128, 1])
    onesrow_b = k.din("onesrow_b", [1, 512], BF16)
    bqcol = k.din("bqcol", [128, 8]); bkcol = k.din("bkcol", [128, 8])
    bvcol = k.din("bvcol", [128, 8]); bocol = k.din("bocol", [128, 8])
    b2col = k.din("b2col", [128, 8])
    wup = k.din("wup", [IFF0, D]); wgate = k.din("wgate", [IFF0, D])
    wdown = k.din("wdown", [D, IFF0])
    lam_re = k.din("lam_re", [128, 4]); lam_im = k.din("lam_im", [128, 4])
    stepv = k.din("stepv", [128, 4])
    CreT = k.din("CreT", [NSSM, D]); CimT = k.din("CimT", [NSSM, D])
    Dvec = k.din("Dvec", [1, D])
    w1T = k.din("w1T", [D, DFF1]); b1 = k.din("b1", [1, DFF1])
    w2T = k.din("w2T", [DFF1, D]); b2 = k.din("b2", [1, D])
    out = k.dout("out", [D, T])

    vdram = k.dscratch("vdram", [16, PITCH], BF16)
    mbias = k.dscratch("mbias", [16 * 128 * PITCH], BF16)
    sp_up = k.dscratch("sp_up", [IFF0, T])
    sp_gate = k.dscratch("sp_gate", [IFF0, T])
    sp_q = k.dscratch("sp_q", [IFF0, T])
    sp_g2k = k.dscratch("sp_g2k", [IFF0, T], I16)
    sp_upk = k.dscratch("sp_upk", [IFF0, T], I16)
    sp_oh = k.dscratch("sp_oh", [IFF0, T], BF16)
    sp_ol = k.dscratch("sp_ol", [IFF0, T], BF16)
    w1q_d = k.dscratch("w1q_d", [D, DFF1], BF16)
    w2q_d = k.dscratch("w2q_d", [DFF1, D], BF16)
    sp_bur = k.dscratch("sp_bur", [NSSM, T]); sp_bui = k.dscratch("sp_bui", [NSSM, T])
    sp_du = k.dscratch("sp_du", [D, T])
    csp_d = {nm: k.dscratch("c_" + nm, [NSSM, D], BF16)
             for nm in ("reh", "rel", "imh", "iml")}
    cc_in = k.dscratch("cc_in", [1, 16])
    cc_out = k.dscratch("cc_out", [1, 16])

    X = k.sbuf("X", [128, NCH * T])
    XH = k.sbuf("XH", [128, NCH * T], BF16)
    XL = k.sbuf("XL", [128, NCH * T], BF16)
    ident = k.sbuf("ident", [128, 128], BF16)
    onesr = k.sbuf("onesr", [1, 128])
    onesa = k.sbuf("onesa", [128, 128])
    onesc = k.sbuf("onesc", [128, 1])
    halfA = k.sbuf("halfA", [128, 1]); halfB = k.sbuf("halfB", [128, 1])
    bcols = k.sbuf("bcols", [128, 40])
    panel = k.sbuf("panel", [1, 64])
    pbc = k.sbuf("pbc", [128, 16])
    runmax = k.sbuf("runmax", [128, 4])
    ccst = k.sbuf("ccst", [1, 16])

    for c in range(NCH):
        k.dma("sync", X[:, c * T:(c + 1) * T], x_in[c * 128:(c + 1) * 128, :])
    k.dma("sync", ident[:], identc[:])
    k.dma("sync", onesr[:], ones_r[:])
    k.dma("sync", onesa[:], onesall[:])
    k.dma("sync", onesc[:], ones_c[:])
    k.dma("sync", halfA[:], halfA_c[:])
    k.dma("sync", halfB[:], halfB_c[:])
    for j, bc in enumerate((bqcol, bkcol, bvcol, bocol, b2col)):
        k.dma("sync", bcols[:, j * 8:(j + 1) * 8], bc[:])
    bar()

    S_XG, S_SX, S_FX, S_UG, S_GG, S_SU, S_FU, S_SG, S_FG, S_QG, S_SQ2, S_FQ2, \
        S_PG, S_SO, S_FO, S_G1, S_G2, S_T0, S_T1, S_T2 = range(20)

    def ps_slot(i):
        return panel[0:1, i:i + 1]

    def bcast_panel(slot, col, extra_wait=()):
        psb = k.psum("psb_bc", [128, 1])
        t1 = k.tens(lambda e: e.matmul(psb[:], onesr[:], ps_slot(slot),
                                       start=True, stop=True), wait=extra_wait, sig=True)
        t2 = k.vec(lambda e, col=col: e.tensor_copy(out=pbc[:, col:col + 1], in_=psb[:]),
                   wait=[t1], sig=True)
        self_bar = k.p.barrier()
        k.free("psb_bc")
        return t2

    def emit_ln():
        mrow = k.sbuf("mrow", [1, T]); qrow = k.sbuf("qrow", [1, T])
        inv = k.sbuf("inv", [1, T])
        mb = k.sbuf("mb", [128, T]); ib = k.sbuf("ib", [128, T])
        sqb = k.sbuf("sqb", [128, 1024])
        t2b = k.sbuf("t2b", [128, T])
        psS = [k.psum("psS0", [1, 512]), k.psum("psS1", [1, 512])]
        psQ = [k.psum("psQ0", [1, 512]), k.psum("psQ1", [1, 512])]
        psB = [k.psum("psB0", [128, 512]), k.psum("psB1", [128, 512])]
        bar()
        k.vec(lambda e: e.memset(runmax[:, 0:1], 0.0), sig=True)
        t_cs = [None] * 6; t_cq = [None] * 6
        prev_mm = [None, None]
        for tc in range(4):
            pS = psS[tc % 2]; pQ = psQ[tc % 2]
            t_smm = None
            for c in range(NCH):
                xs = X[:, c * T + tc * 512: c * T + tc * 512 + 512]
                sq = sqb[:, (c % 2) * 512:(c % 2) * 512 + 512]
                tq = k.sca(lambda e, xs=xs, sq=sq: e.activation(
                    out=sq, in_=xs, func=AF.Square, bias=0.0, scale=1.0),
                    wait=[prev_mm[c % 2]], sig=True)
                t_smm = k.tens(lambda e, xs=xs, c=c, pS=pS: e.matmul(
                    pS[:], onesc[:], xs, start=(c == 0), stop=(c == NCH - 1)),
                    wait=[t_cs[tc - 2] if (c == 0 and tc >= 2) else None],
                    sig=(c == NCH - 1))
                prev_mm[c % 2] = k.tens(lambda e, sq=sq, c=c, pQ=pQ: e.matmul(
                    pQ[:], onesc[:], sq, start=(c == 0), stop=(c == NCH - 1)),
                    wait=[tq, t_cq[tc - 2] if (c == 0 and tc >= 2) else None],
                    sig=True)
            t_cs[tc] = k.vec(lambda e, tc=tc, pS=pS: e.tensor_copy(
                out=mrow[0:1, tc * 512:(tc + 1) * 512], in_=pS[:]),
                wait=[t_smm], sig=True)
            t_cq[tc] = k.vec(lambda e, tc=tc, pQ=pQ: e.tensor_copy(
                out=qrow[0:1, tc * 512:(tc + 1) * 512], in_=pQ[:]),
                wait=[prev_mm[(NCH - 1) % 2]], sig=True)
        t_m = k.vec(lambda e: e.tensor_scalar_mul(out=mrow[:], in0=mrow[:],
                                                  scalar1=1.0 / D), sig=True)
        k.vec(lambda e: e.tensor_scalar_mul(out=qrow[:], in0=qrow[:],
                                            scalar1=1.0 / D), sig=True)
        t_i = k.sca(lambda e: e.activation(out=inv[:], in_=mrow[:], func=AF.Square,
                                           bias=0.0, scale=1.0), wait=[t_m], sig=True)
        t_q2 = k.vec(lambda e: e.tensor_sub(out=qrow[:], in0=qrow[:], in1=inv[:]),
                     wait=[t_i], sig=True)
        t_s = k.sca(lambda e: e.activation(out=qrow[:], in_=qrow[:], func=AF.Sqrt,
                                           bias=cstv[0:1, 0:1], scale=1.0),
                    wait=[t_q2], sig=True)
        t_r = k.vec(lambda e: e.reciprocal(out=inv[:], in_=qrow[:]),
                    wait=[t_s], sig=True)
        cp_tok = [None] * 10
        jobs = [(mrow, mb, tc, t_m) for tc in range(4)] + \
               [(inv, ib, tc, t_r) for tc in range(4)]
        for j, (srow, dst, tc, trdy) in enumerate(jobs):
            pB = psB[j % 2]
            tmm = k.tens(lambda e, srow=srow, tc=tc, pB=pB: e.matmul(
                pB[:], onesr[:], srow[0:1, tc * 512:(tc + 1) * 512],
                start=True, stop=True),
                wait=[trdy, cp_tok[j - 2] if j >= 2 else None], sig=True)
            cp_tok[j] = k.vec(lambda e, dst=dst, tc=tc, pB=pB: e.tensor_copy(
                out=dst[:, tc * 512:(tc + 1) * 512], in_=pB[:]),
                wait=[tmm], sig=True)
        th = None
        tmax = None
        for c in range(NCH):
            sl = slice(c * T, (c + 1) * T)
            ta = k.vec(lambda e, sl=sl: e.tensor_sub(out=t2b[:], in0=X[:, sl], in1=mb[:]),
                       wait=[th], sig=True)
            tb_ = k.vec(lambda e: e.tensor_mul(out=t2b[:], in0=t2b[:], in1=ib[:]),
                        wait=[ta], sig=True)
            tr = k.vec(lambda e: e.reduce_max(out=runmax[:, 1:2], in_=t2b[:], axis=AX.X,
                                              apply_absolute_value=True), wait=[tb_], sig=True)
            tmax = k.vec(lambda e: e.tensor_max(out=runmax[:, 0:1], in0=runmax[:, 0:1],
                                                in1=runmax[:, 1:2]), wait=[tr], sig=True)
            th = k.sca(lambda e, sl=sl: e.activation(out=XH[:, sl], in_=t2b[:],
                                                     func=AF.Copy, bias=0.0, scale=1.0),
                       wait=[tb_], sig=True)
            k.vec(lambda e, sl=sl: e.scalar_tensor_tensor(
                out=XL[:, sl], in0=t2b[:], scalar=1.0, in1=XH[:, sl],
                op0=ALU.mult, op1=ALU.subtract), wait=[th], sig=True)
        psT = k.psum("psT", [1, 128])
        tt = k.tens(lambda e: e.transpose(psT[:], runmax[:, 0:1], identfs[:]),
                    wait=[tmax], sig=True)
        k.vec(lambda e: e.reduce_max(out=ps_slot(S_XG), in_=psT[:], axis=AX.X),
              wait=[tt], sig=True)
        bar()
        k.free("psT")
        k.free("psB1", "psB0", "psQ1", "psQ0", "psS1", "psS0")
        k.free("t2b", "sqb", "ib", "mb", "inv", "qrow", "mrow")
    def emit_allreduce_max(slots):
        k.vec(lambda e: e.memset(ccst[:], 0.0), sig=True)
        bar()
        for j, s in enumerate(slots):
            k.vec(lambda e, j=j, s=s: e.tensor_copy(out=ccst[0:1, j:j + 1],
                                                    in_=ps_slot(s)), sig=True)
        bar()
        k.dma("sync", cc_in[:], ccst[:])
        bar()
        k.gps(lambda e: e.collective_compute(
            "AllReduce", ALU.max, replica_groups=[list(range(n_cores))],
            ins=[cc_in[:]], outs=[cc_out[:]]), sig=True)
        bar()
        k.dma("sync", ccst[:], cc_out[:])
        bar()
        for j, s in enumerate(slots):
            k.vec(lambda e, j=j, s=s: e.tensor_copy(out=ps_slot(s),
                                                    in_=ccst[0:1, j:j + 1]), sig=True)
        bar()

    bmax_row = k.sbuf("bmax_row", [1, 16])

    def emit_bias_prep():
        relf = k.sbuf("relf", [32, 16]); relbb = k.sbuf("relbb", [32, 16], BF16)
        ohf = k.sbuf("ohf", [32, 512]); ohb = k.sbuf("ohb", [32, 512], BF16)
        vsb = k.sbuf("vsb", [16, PITCH], BF16)
        bm16 = k.sbuf("bm16", [16, 1], BF16)
        psV = k.psum("psV", [16, 512])
        psT2 = k.psum("psT2", [1, 16], BF16)
        k.dma("sync", relf[:], relb[:])
        bar()
        k.vec(lambda e: e.tensor_copy(out=relbb[:], in_=relf[:]), sig=True)
        bar()
        for j in range(0, PITCH, 512):
            w = min(512, PITCH - j)
            k.dma("sync", ohf[0:32, 0:w], onehotT[0:32, j:j + w])
            bar()
            k.vec(lambda e, w=w: e.tensor_copy(out=ohb[0:32, 0:w], in_=ohf[0:32, 0:w]), sig=True)
            bar()
            k.tens(lambda e, w=w: e.matmul(psV[0:16, 0:w], relbb[:], ohb[0:32, 0:w],
                                           start=True, stop=True), sig=True)
            bar()
            k.sca(lambda e, j=j, w=w: e.activation(out=vsb[0:16, j:j + w],
                                                   in_=psV[0:16, 0:w], func=AF.Copy,
                                                   bias=0.0, scale=1.0), sig=True)
            bar()
        k.vec(lambda e: e.reduce_max(out=bm16[:], in_=vsb[0:16, 0:VW], axis=AX.X), sig=True)
        bar()
        k.tens(lambda e: e.transpose(psT2[:], bm16[:], ident[0:16, 0:16]), sig=True)
        bar()
        k.vec(lambda e: e.tensor_copy(out=bmax_row[:], in_=psT2[:]), sig=True)
        k.dma("sync", vdram[:], vsb[:])
        bar()
        k.dma("sync",
              AP(mbias.tensor, 0, [[128 * PITCH, 16], [PITCH + 1, 128], [1, VW]]),
              AP(vdram.tensor, 0, [[PITCH, 16], [0, 128], [1, VW]]))
        bar()
        k.free("psT2", "psV", "bm16", "vsb", "ohb", "ohf", "relbb", "relf")

    def emit_attention():
        wsp = k.sbuf("wsp", [128, 1024])
        wsh = k.sbuf("wsh", [128, 1024], BF16)
        wsl = k.sbuf("wsl", [128, 1024], BF16)
        Pht = k.sbuf("Pht", [128, 512], BF16); Plt = k.sbuf("Plt", [128, 512], BF16)
        P32t = k.sbuf("P32t", [128, 512])
        vh_t = Pht; vl_t = Plt
        qh = k.sbuf("qh", [128, T], BF16); ql = k.sbuf("ql", [128, T], BF16)
        kh_ = k.sbuf("kh_", [128, T], BF16); kl_ = k.sbuf("kl_", [128, T], BF16)
        vTh = k.sbuf("vTh", [128, 16 * 130], BF16)
        vTl = k.sbuf("vTl", [128, 16 * 130], BF16)
        VN = k.sbuf("VN", [128, 4096], BF16)
        attnh = k.sbuf("attnh", [64, 2 * T], BF16); attnl = k.sbuf("attnl", [64, 2 * T], BF16)
        rs_sb = k.sbuf("rs_sb", [128, 512])
        sqt = rs_sb
        tmp64 = P32t[0:64, :]
        negc_sb = k.sbuf("negc_sb", [128, 2])
        psc = [k.psum("psc0"), k.psum("psc1")]
        psPV = [k.psum("psPV0"), k.psum("psPV1")]
        psrb = k.psum("psrb")
        psqkv = [k.psum("psqkv0"), k.psum("psqkv1")]
        psTT = [k.psum("psTT0", [128, 128], BF16)] * 2
        psNA = psrb; psNB = psrb
        S_T = 20

        def load_wslice_blocked(wB, pr):
            src = AP(wB.tensor, pr * (D * 128),
                     [[128, 128], [128 * 128, 8], [1, 128]])
            dst = wsp[:].rearrange("p (a b) -> p a b", b=128)
            k.dma("sync", dst, src)

        def split_w(width=1024):
            bar()
            k.sca(lambda e: e.activation(out=wsh[:, 0:width], in_=wsp[:, 0:width],
                                         func=AF.Copy, bias=0.0, scale=1.0), sig=True)
            bar()
            k.vec(lambda e: e.scalar_tensor_tensor(out=wsl[:, 0:width], in0=wsp[:, 0:width],
                                                   scalar=1.0, in1=wsh[:, 0:width],
                                                   op0=ALU.mult, op1=ALU.subtract), sig=True)
            bar()

        for dh in range(NCH):
            k.vec(lambda e, dh=dh: e.tensor_scalar(
                out=X[:, dh * T:(dh + 1) * T], in0=X[:, dh * T:(dh + 1) * T],
                scalar1=bcols[:, 24 + dh: 24 + dh + 1], scalar2=None,
                op0=ALU.add), sig=True)
        bar()
        for pr in range(NCH):
            k.vec(lambda e: e.memset(panel[0:1, S_T:S_T + 8], 0.0), sig=True)
            bar()
            for pj, (wB, bn, scale) in enumerate((
                    (wqB, "bq", 0.125), (wkB, "bk", 1.0), (wvB, "bv", 1.0))):
                load_wslice_blocked(wB, pr)
                split_w()
                if pj == 2:
                    k.vec(lambda e: e.memset(vTh[:], 1.0), sig=True)
                    k.vec(lambda e: e.memset(vTl[:], 0.0), sig=True)
                t_grp = [None, None]
                t_bias = [None] * 6
                t_done = [None] * 6
                bslot = {"bq": 0, "bk": 1, "bv": 2}[bn]
                for tc in range(4):
                    ps = psqkv[tc % 2]
                    tsl = slice(tc * 512, (tc + 1) * 512)
                    for c in range(NCH):
                        lh = wsh[:, c * 128:(c + 1) * 128]
                        ll = wsl[:, c * 128:(c + 1) * 128]
                        rh = XH[:, c * T + tc * 512: c * T + tc * 512 + 512]
                        rl = XL[:, c * T + tc * 512: c * T + tc * 512 + 512]
                        k.tens(lambda e, ps=ps, lh=lh, rh=rh, first=(c == 0): e.matmul(
                            ps[:], lh, rh, start=first, stop=False),
                            wait=[t_done[tc - 2]] if c == 0 else [])
                        k.tens(lambda e, ps=ps, lh=lh, rl=rl: e.matmul(
                            ps[:], lh, rl, start=False, stop=False))
                        tg = k.tens(lambda e, ps=ps, ll=ll, rh=rh, last=(c == NCH - 1):
                                    e.matmul(ps[:], ll, rh, start=False, stop=last),
                                    sig=(c == NCH - 1))
                    t_grp[tc % 2] = tg
                    tb = k.vec(lambda e, ps=ps: e.tensor_scalar(
                        out=P32t[:], in0=ps[:],
                        scalar1=bcols[:, bslot * 8 + pr: bslot * 8 + pr + 1],
                        scalar2=None, op0=ALU.add), wait=[tg], sig=True)
                    t_bias[tc] = tb
                    if pj < 2:
                        dsth = (qh, kh_)[pj]; dstl = (ql, kl_)[pj]
                        th = k.sca(lambda e, dsth=dsth, tsl=tsl, scale=scale: e.activation(
                            out=dsth[:, tsl], in_=P32t[:], func=AF.Copy,
                            bias=0.0, scale=scale), wait=[tb], sig=True)
                        tlo = k.vec(lambda e, dsth=dsth, dstl=dstl, tsl=tsl, scale=scale:
                                    e.scalar_tensor_tensor(out=dstl[:, tsl], in0=P32t[:],
                                                           scalar=scale, in1=dsth[:, tsl],
                                                           op0=ALU.mult, op1=ALU.subtract),
                                    wait=[th], sig=True)
                        tsq = k.sca(lambda e, scale=scale: e.activation(
                            out=sqt[:], in_=P32t[:], func=AF.Square,
                            bias=0.0, scale=scale), wait=[tb], sig=True)
                        nsl = S_T + pj * 2
                        tA = k.tens(lambda e: e.matmul(psNA[0:1, :], halfA[:], sqt[:],
                                                       start=True, stop=True),
                                    wait=[tsq], sig=True)
                        trA = k.vec(lambda e: e.reduce_max(out=ps_slot(S_T + 6),
                                                           in_=psNA[0:1, :], axis=AX.X),
                                    wait=[tA], sig=True)
                        k.vec(lambda e, nsl=nsl: e.tensor_max(
                            out=ps_slot(nsl), in0=ps_slot(nsl),
                            in1=ps_slot(S_T + 6)), sig=True)
                        tB = k.tens(lambda e: e.matmul(psNB[0:1, :], halfB[:], sqt[:],
                                                       start=True, stop=True),
                                    wait=[trA], sig=True)
                        trB = k.vec(lambda e: e.reduce_max(out=ps_slot(S_T + 6),
                                                           in_=psNB[0:1, :], axis=AX.X),
                                    wait=[tB], sig=True)
                        tmx = k.vec(lambda e, nsl=nsl: e.tensor_max(
                            out=ps_slot(nsl + 1), in0=ps_slot(nsl + 1),
                            in1=ps_slot(S_T + 6)), sig=True)
                        t_done[tc] = tmx
                    else:
                        th = k.sca(lambda e: e.activation(out=vh_t[:], in_=P32t[:],
                                                          func=AF.Copy, bias=0.0,
                                                          scale=1.0), wait=[tb], sig=True)
                        tlo = k.vec(lambda e: e.scalar_tensor_tensor(
                            out=vl_t[:], in0=P32t[:], scalar=1.0, in1=vh_t[:],
                            op0=ALU.mult, op1=ALU.subtract), wait=[th], sig=True)
                        t_cp = None
                        for src_t, dst in ((vh_t, vTh), (vl_t, vTl)):
                            for tb_ in range(4):
                                blk = tc * 4 + tb_
                                pT = psTT[blk % 2]
                                t_t = k.tens(lambda e, src_t=src_t, tb_=tb_, pT=pT:
                                             e.transpose(pT[:],
                                                         src_t[:, tb_ * 128:(tb_ + 1) * 128],
                                                         ident[:]),
                                             wait=[tlo if src_t is vl_t else th,
                                                   t_cp], sig=True)
                                t_c1 = k.vec(lambda e, dst=dst, blk=blk, pT=pT: e.tensor_copy(
                                    out=dst[:, blk * 130: blk * 130 + 64],
                                    in_=pT[:, 0:64]), wait=[t_t], sig=True)
                                t_cp = k.vec(lambda e, dst=dst, blk=blk, pT=pT: e.tensor_copy(
                                    out=dst[:, blk * 130 + 65: blk * 130 + 129],
                                    in_=pT[:, 64:128]), wait=[t_c1], sig=True)
                        t_done[tc] = t_cp
                bar()
            k.sca(lambda e: e.activation(out=panel[0:1, S_T:S_T + 4],
                                         in_=panel[0:1, S_T:S_T + 4],
                                         func=AF.Sqrt, bias=0.0, scale=1.0), sig=True)
            bar()
            for hp in range(2):
                hh = 2 * pr + hp
                k.vec(lambda e, hp=hp: e.tensor_mul(
                    out=ps_slot(S_T + 4 + hp), in0=ps_slot(S_T + hp),
                    in1=ps_slot(S_T + 2 + hp)), sig=True)
                bar()
                k.vec(lambda e, hp=hp, hh=hh: e.tensor_add(
                    out=ps_slot(S_T + 4 + hp), in0=ps_slot(S_T + 4 + hp),
                    in1=bmax_row[0:1, hh:hh + 1]), sig=True)
                bar()
                k.vec(lambda e, hp=hp: e.tensor_scalar(
                    out=ps_slot(S_T + 4 + hp), in0=ps_slot(S_T + 4 + hp),
                    scalar1=0.05, scalar2=-1.0, op0=ALU.add, op1=ALU.mult), sig=True)
                bar()
            k.tens(lambda e: e.matmul(psrb[:, 0:2], onesr[:],
                                      panel[0:1, S_T + 4:S_T + 6],
                                      start=True, stop=True), sig=True)
            bar()
            k.vec(lambda e: e.tensor_copy(out=negc_sb[:], in_=psrb[:, 0:2]), sig=True)
            bar()
            for hp in range(2):
                hh = 2 * pr + hp
                k.dma("sync", VN[:],
                      AP(mbias.tensor, hh * 128 * PITCH, [[PITCH, 128], [1, 4096]]))
                bar()
                hsl = slice(hp * 64, (hp + 1) * 64)
                t_ep = [None, None]
                t_pv_last = [None] * 65
                te1s = [None] * 65; te2s = [None] * 65; tpls = [None] * 65
                tscs = [None] * 65

                def emit_pv(gp, hp=hp):
                    Ip, Jp = divmod(gp, 16)
                    pvp = psPV[Ip % 2]
                    lh = vTh[:, Jp * 130 + hp * 65: Jp * 130 + hp * 65 + 65]
                    ll = vTl[:, Jp * 130 + hp * 65: Jp * 130 + hp * 65 + 65]
                    k.tens(lambda e, pvp=pvp, lh=lh, first=(Jp == 0): e.matmul(
                        pvp[0:65, :], lh, Pht[:], start=first, stop=False),
                        wait=[te1s[gp]])
                    k.tens(lambda e, pvp=pvp, lh=lh: e.matmul(
                        pvp[0:65, :], lh, Plt[:], start=False, stop=False),
                        wait=[tpls[gp]])
                    t_pv_last[gp] = k.tens(lambda e, pvp=pvp, ll=ll,
                                           last=(Jp == 15): e.matmul(
                        pvp[0:65, :], ll, Pht[:], start=False, stop=last), sig=True)

                def emit_epilogue(I, hp=hp):
                    pv = psPV[I % 2]
                    asl = slice(hp * T + I * 512, hp * T + I * 512 + 512)
                    tpv_done = t_pv_last[I * 16 + 15]
                    tr1 = k.vec(lambda e, pv=pv: e.reciprocal(out=rs_sb[64:65, :],
                                                              in_=pv[64:65, :]),
                                wait=[tpv_done, t_ep[I % 2]], sig=True)
                    tr2 = k.tens(lambda e: e.matmul(psrb[0:65, :], onesa[64:65, 0:65],
                                                    rs_sb[64:65, :], start=True,
                                                    stop=True), wait=[tr1], sig=True)
                    tr3 = k.vec(lambda e: e.tensor_copy(out=rs_sb[0:64, :],
                                                        in_=psrb[0:64, :]),
                                wait=[tr2], sig=True)
                    tr4 = k.vec(lambda e, pv=pv: e.tensor_mul(out=tmp64, in0=pv[0:64, :],
                                                              in1=rs_sb[0:64, :]),
                                wait=[tr3], sig=True)
                    tr5 = k.sca(lambda e, asl=asl: e.activation(
                        out=attnh[0:64, asl], in_=tmp64, func=AF.Copy,
                        bias=0.0, scale=1.0), wait=[tr4], sig=True)
                    t_ep[I % 2] = k.vec(lambda e, asl=asl: e.scalar_tensor_tensor(
                        out=attnl[0:64, asl], in0=tmp64, scalar=1.0,
                        in1=attnh[0:64, asl], op0=ALU.mult, op1=ALU.subtract),
                        wait=[tr5], sig=True)

                for g in range(64):
                    I, J = divmod(g, 16)
                    ps = psc[g % 2]
                    isl = slice(I * 512, (I + 1) * 512)
                    c0 = 2047 + I * 512 - J * 128
                    w0 = ([te2s[g - 2]] if g >= 2 else [])
                    if J == 0:
                        w0 = w0 + [t_ep[I % 2]]
                    k.tens(lambda e, ps=ps, J=J, hsl=hsl, isl=isl: e.matmul(
                        ps[:], kh_[hsl, J * 128:(J + 1) * 128], qh[hsl, isl],
                        start=True, stop=False), wait=w0)
                    k.tens(lambda e, ps=ps, J=J, hsl=hsl, isl=isl: e.matmul(
                        ps[:], kh_[hsl, J * 128:(J + 1) * 128], ql[hsl, isl],
                        start=False, stop=False))
                    k.tens(lambda e, ps=ps, J=J, hsl=hsl, isl=isl: e.matmul(
                        ps[:], kl_[hsl, J * 128:(J + 1) * 128], qh[hsl, isl],
                        start=False, stop=False))
                    tscs[g] = k.tens(lambda e, ps=ps, hp=hp, c0=c0: e.matmul(
                        ps[:], ident[:], VN[:, c0:c0 + 512],
                        start=False, stop=True), sig=True)
                    if g >= 1:
                        emit_pv(g - 1)
                        if g % 16 == 0:
                            emit_epilogue(I - 1)
                    te1s[g] = k.sca(lambda e, ps=ps, hp=hp: e.activation(
                        out=Pht[:], in_=ps[:], func=AF.Exp,
                        bias=negc_sb[:, hp:hp + 1], scale=1.0),
                        wait=[tscs[g], t_pv_last[g - 1]], sig=True)
                    te2s[g] = k.sca(lambda e, ps=ps, hp=hp: e.activation(
                        out=P32t[:], in_=ps[:], func=AF.Exp,
                        bias=negc_sb[:, hp:hp + 1], scale=1.0),
                        wait=[tpls[g - 1]], sig=True)
                    tpls[g] = k.vec(lambda e: e.tensor_sub(out=Plt[:], in0=P32t[:],
                                                           in1=Pht[:]),
                                    wait=[te1s[g], te2s[g], t_pv_last[g - 1]], sig=True)
                emit_pv(63)
                emit_epilogue(3)
                bar()
            for hp in range(2):
                k.dma("sync", wsp[0:64, 0:1024],
                      woT[pr * 128 + hp * 64: pr * 128 + hp * 64 + 64, :])
                bar()
                k.sca(lambda e: e.activation(out=wsh[0:64, 0:1024], in_=wsp[0:64, 0:1024],
                                             func=AF.Copy, bias=0.0, scale=1.0), sig=True)
                bar()
                k.vec(lambda e: e.scalar_tensor_tensor(
                    out=wsl[0:64, 0:1024], in0=wsp[0:64, 0:1024], scalar=1.0,
                    in1=wsh[0:64, 0:1024], op0=ALU.mult, op1=ALU.subtract), sig=True)
                bar()
                t_add = [None, None]
                i = 0
                for dh in range(NCH):
                    lh = wsh[0:64, dh * 128:(dh + 1) * 128]
                    ll = wsl[0:64, dh * 128:(dh + 1) * 128]
                    for tc in range(4):
                        ps = psqkv[i % 2]
                        asl = slice(hp * T + tc * 512, hp * T + tc * 512 + 512)
                        k.tens(lambda e, ps=ps, lh=lh, asl=asl: e.matmul(
                            ps[:], lh, attnh[0:64, asl], start=True, stop=False),
                            wait=[t_add[i % 2]])
                        k.tens(lambda e, ps=ps, lh=lh, asl=asl: e.matmul(
                            ps[:], lh, attnl[0:64, asl], start=False, stop=False))
                        tg = k.tens(lambda e, ps=ps, ll=ll, asl=asl: e.matmul(
                            ps[:], ll, attnh[0:64, asl], start=False, stop=True),
                            sig=True)
                        t_add[i % 2] = k.vec(lambda e, ps=ps, dh=dh, tc=tc: e.tensor_tensor(
                            out=X[:, dh * T + tc * 512: dh * T + tc * 512 + 512],
                            in0=X[:, dh * T + tc * 512: dh * T + tc * 512 + 512],
                            in1=ps[:], op=ALU.add), wait=[tg], sig=True)
                        i += 1
                bar()
        k.free("psTT0", "psqkv1", "psqkv0", "psrb", "psPV1", "psPV0", "psc1", "psc0")
        k.free("negc_sb", "rs_sb",
               "attnl", "attnh", "VN", "vTl", "vTh", "kl_", "kh_", "ql", "qh",
               "P32t", "Plt", "Pht", "wsl", "wsh", "wsp")

    identf = k.din("identf", [128, 128])
    cstv_d = k.din("cstv_d", [128, 2])
    identfs = k.sbuf("identfs", [128, 128])
    k.dma("sync", identfs[:], identf[:])
    cstv = k.sbuf("cstv", [128, 2])
    k.dma("sync", cstv[:], cstv_d[:])

    def emit_xq():
        v32 = k.sbuf("xqv", [128, T])
        xki = k.sbuf("xki", [128, T], I16)
        k.vec(lambda e: e.tensor_scalar(out=ps_slot(S_SX), in0=ps_slot(S_XG),
                                        scalar1=1.0 / 32767.0, scalar2=1e-12,
                                        op0=ALU.mult, op1=ALU.max), sig=True)
        bar()
        k.vec(lambda e: e.reciprocal(out=ps_slot(S_FX), in_=ps_slot(S_SX)), sig=True)
        bar()
        bcast_panel(S_FX, 0)
        bar()
        for c in range(NCH):
            sl = slice(c * T, (c + 1) * T)
            k.vec(lambda e, sl=sl: e.tensor_add(out=v32[:], in0=XH[:, sl], in1=XL[:, sl]), sig=True)
            bar()
            k.vec(lambda e: e.tensor_scalar(out=v32[:], in0=v32[:],
                                            scalar1=pbc[:, 0:1], scalar2=MAGIC,
                                            op0=ALU.mult, op1=ALU.add), sig=True)
            bar()
            k.vec(lambda e: e.tensor_scalar(out=xki[:], in0=v32[:], scalar1=MAGIC,
                                            scalar2=None, op0=ALU.subtract), sig=True)
            bar()
            k.sca(lambda e, sl=sl: e.activation(out=XH[:, sl], in_=xki[:],
                                                func=AF.Copy, bias=0.0, scale=1.0), sig=True)
            bar()
            k.vec(lambda e, sl=sl: e.scalar_tensor_tensor(
                out=XL[:, sl], in0=xki[:], scalar=1.0, in1=XH[:, sl],
                op0=ALU.mult, op1=ALU.subtract), sig=True)
            bar()
        k.free("xki", "xqv")

    def emit_lpbq_dequant(wbuf, wdq, wqi, bmv, sv, W):
        ng = W // 32
        in3 = wbuf[:].rearrange("p (g j) -> p g j", j=32)[0:128, 0:ng, 0:32]
        pitch_s = sv[:].ap[0][0]
        bc3 = AP(sv[:].tensor, 0, [[pitch_s, 128], [1, ng], [0, 32]])
        k.vec(lambda e: e.tensor_reduce(out=bmv[0:128, 0:ng], in_=in3, axis=AX.X,
                                        op=ALU.max, apply_absolute_value=True), sig=True)
        bar()
        k.vec(lambda e: e.tensor_scalar(out=sv[0:128, 0:ng], in0=bmv[0:128, 0:ng],
                                        scalar1=1.0 / 7.0, scalar2=1e-12,
                                        op0=ALU.mult, op1=ALU.max), sig=True)
        bar()
        k.vec(lambda e: e.reciprocal(out=bmv[0:128, 0:ng], in_=sv[0:128, 0:ng]), sig=True)
        bar()
        pitch_b = bmv[:].ap[0][0]
        ibc3 = AP(bmv[:].tensor, 0, [[pitch_b, 128], [1, ng], [0, 32]])
        out3 = wdq[:].rearrange("p (g j) -> p g j", j=32)[0:128, 0:ng, 0:32]
        in03 = wbuf[:].rearrange("p (g j) -> p g j", j=32)[0:128, 0:ng, 0:32]
        k.vec(lambda e: e.tensor_tensor(out=out3, in0=in03,
                                        in1=ibc3, op=ALU.mult), sig=True)
        bar()
        k.vec(lambda e: e.tensor_scalar(out=wqi[0:128, 0:W], in0=wdq[0:128, 0:W],
                                        scalar1=MAGIC, scalar2=MAGIC,
                                        op0=ALU.add, op1=ALU.subtract), sig=True)
        bar()
        ini3 = wqi[:].rearrange("p (g j) -> p g j", j=32)[0:128, 0:ng, 0:32]
        k.vec(lambda e: e.tensor_tensor(out=out3, in0=ini3,
                                        in1=bc3, op=ALU.mult), sig=True)
        bar()

    def ini3(wqi, ng):
        return wqi[:].rearrange("p (g j) -> p g j", j=32)[0:128, 0:ng, 0:32]

    def emit_lpbq_dequant_tok(wbuf, wdq, wqi, bmv, sv, W, wait0):
        ng = W // 32
        in3 = wbuf[:].rearrange("p (g j) -> p g j", j=32)[0:128, 0:ng, 0:32]
        pitch_s = sv[:].ap[0][0]
        bc3 = AP(sv[:].tensor, 0, [[pitch_s, 128], [1, ng], [0, 32]])
        pitch_b = bmv[:].ap[0][0]
        ibc3 = AP(bmv[:].tensor, 0, [[pitch_b, 128], [1, ng], [0, 32]])
        out3 = wdq[:].rearrange("p (g j) -> p g j", j=32)[0:128, 0:ng, 0:32]
        in03 = wbuf[:].rearrange("p (g j) -> p g j", j=32)[0:128, 0:ng, 0:32]
        k.vec(lambda e: e.tensor_reduce(out=bmv[0:128, 0:ng], in_=in3, axis=AX.X,
                                        op=ALU.max, apply_absolute_value=True),
              wait=wait0)
        k.vec(lambda e: e.tensor_scalar(out=sv[0:128, 0:ng], in0=bmv[0:128, 0:ng],
                                        scalar1=1.0 / 7.0, scalar2=1e-12,
                                        op0=ALU.mult, op1=ALU.max))
        k.vec(lambda e: e.reciprocal(out=bmv[0:128, 0:ng], in_=sv[0:128, 0:ng]))
        t_free = k.vec(lambda e: e.tensor_tensor(out=out3, in0=in03,
                                                 in1=ibc3, op=ALU.mult), sig=True)
        k.vec(lambda e: e.tensor_scalar(out=wqi[0:128, 0:W], in0=wdq[0:128, 0:W],
                                        scalar1=MAGIC, scalar2=MAGIC,
                                        op0=ALU.add, op1=ALU.subtract))
        t_done = k.vec(lambda e, w3=ini3(wqi, ng): e.tensor_tensor(
            out=out3, in0=w3, in1=bc3, op=ALU.mult), sig=True)
        return t_free, t_done

    def emit_ffn0_mm():
        NO = IFF0 // 128
        wbuf = [k.sbuf("wbuf0", [128, 1024]), k.sbuf("wbuf1", [128, 1024])]
        wdq = k.sbuf("wdq", [128, 1024])
        wqi = k.sbuf("wqi", [128, 1024], I16)
        bmv = k.sbuf("bmv", [128, 32]); sv = k.sbuf("sv", [128, 32])
        wTh = [k.sbuf("wTh0", [128, 1024], BF16), k.sbuf("wTh1", [128, 1024], BF16)]
        wTl = [k.sbuf("wTl0", [128, 1024], BF16), k.sbuf("wTl1", [128, 1024], BF16)]
        stage = [k.sbuf("stage0", [128, T]), k.sbuf("stage1", [128, T])]
        psW = k.psum("psW", [128, 128])
        psU = [k.psum("psU%d" % i) for i in range(4)]
        k.vec(lambda e: e.memset(runmax[:, 2:4], 0.0), sig=True)
        bar()
        jobs = [(wup, sp_up, 2, o) for o in range(NO)] + \
               [(wgate, sp_gate, 3, o) for o in range(NO)]
        t_load = [None] * len(jobs)
        t_tp = [None] * len(jobs)
        t_mm = {}
        t_evac = {}
        t_spill = [None] * len(jobs)
        t_amax = [None] * len(jobs)
        t_wdqfree = [None] * len(jobs)
        t_load[0] = k.dma("sync", wbuf[0][:], jobs[0][0][0:128, :])
        t_load[1] = k.dma("gpsimd", wbuf[1][:], jobs[1][0][128:256, :])
        for j, (wsrc, spdst, rcol, o) in enumerate(jobs):
            wb = wbuf[j % 2]
            w0 = [t_load[j]] + ([t_wdqfree[j - 1]] if j > 0 else [])
            t_free, t_deq = emit_lpbq_dequant_tok(wb, wdq, wqi, bmv, sv, 1024, w0)
            if j + 2 < len(jobs):
                ws2, _, _, o2 = jobs[j + 2]
                eng = "sync" if (j % 2 == 0) else "gpsimd"
                t_load[j + 2] = k.dma(eng, wb[:], ws2[o2 * 128:(o2 + 1) * 128, :],
                                      wait=[t_free])
            last_tt = None
            prev_user = t_mm.get((j - 2, 3))
            for kc in range(NCH):
                t_t = k.tens(lambda e, kc=kc: e.transpose(
                    psW[:], wdq[:, kc * 128:(kc + 1) * 128], identfs[:]),
                    wait=[t_deq, prev_user] + ([last_tt] if last_tt else []), sig=True)
                th = k.sca(lambda e, kc=kc, j=j: e.activation(
                    out=wTh[j % 2][:, kc * 128:(kc + 1) * 128], in_=psW[:],
                    func=AF.Copy, bias=0.0, scale=1.0), wait=[t_t], sig=True)
                last_tt = k.vec(lambda e, kc=kc, j=j: e.scalar_tensor_tensor(
                    out=wTl[j % 2][:, kc * 128:(kc + 1) * 128], in0=psW[:], scalar=1.0,
                    in1=wTh[j % 2][:, kc * 128:(kc + 1) * 128],
                    op0=ALU.mult, op1=ALU.subtract), wait=[th], sig=True)
            t_tp[j] = last_tt
            t_wdqfree[j] = last_tt
            for tc in range(4):
                ps = psU[tc]
                w_mm = [t_tp[j], t_evac.get((j - 1, tc))]
                for kc in range(NCH):
                    lh = wTh[j % 2][:, kc * 128:(kc + 1) * 128]
                    ll = wTl[j % 2][:, kc * 128:(kc + 1) * 128]
                    rh = XH[:, kc * T + tc * 512: kc * T + tc * 512 + 512]
                    rl = XL[:, kc * T + tc * 512: kc * T + tc * 512 + 512]
                    k.tens(lambda e, ps=ps, lh=lh, rh=rh, first=(kc == 0): e.matmul(
                        ps[:], lh, rh, start=first, stop=False),
                        wait=w_mm if kc == 0 else [])
                    k.tens(lambda e, ps=ps, lh=lh, rl=rl: e.matmul(
                        ps[:], lh, rl, start=False, stop=False))
                    t = k.tens(lambda e, ps=ps, ll=ll, rh=rh, last=(kc == NCH - 1): e.matmul(
                        ps[:], ll, rh, start=False, stop=last), sig=(kc == NCH - 1))
                t_mm[(j, tc)] = t
                t_evac[(j, tc)] = k.sca(lambda e, tc=tc, ps=ps, j=j: e.activation(
                    out=stage[j % 2][:, tc * 512:(tc + 1) * 512], in_=ps[:],
                    func=AF.Copy, bias=0.0, scale=1.0),
                    wait=[t, t_spill[j - 2] if j >= 2 else None], sig=True)
            t_r = k.vec(lambda e, j=j: e.reduce_max(
                out=runmax[:, 1:2], in_=stage[j % 2][:], axis=AX.X,
                apply_absolute_value=True),
                wait=[t_evac[(j, tc)] for tc in range(4)], sig=True)
            t_amax[j] = k.vec(lambda e, rcol=rcol: e.tensor_max(
                out=runmax[:, rcol:rcol + 1], in0=runmax[:, rcol:rcol + 1],
                in1=runmax[:, 1:2]), wait=[t_r], sig=True)
            t_spill[j] = k.dma("sync", spdst[o * 128:(o + 1) * 128, :], stage[j % 2][:],
                               wait=[t_amax[j]])
        bar()
        psT = k.psum("psT", [1, 128])
        for rcol, slot in ((2, S_UG), (3, S_GG)):
            k.tens(lambda e, rcol=rcol: e.transpose(
                psT[:], runmax[:, rcol:rcol + 1], identfs[:]), sig=True)
            bar()
            k.vec(lambda e, slot=slot: e.reduce_max(out=ps_slot(slot), in_=psT[:],
                                                    axis=AX.X), sig=True)
            bar()
        k.free("psT")
        for i in reversed(range(4)):
            k.free("psU%d" % i)
        k.free("psW")
        k.free("stage1", "stage0", "wTl1", "wTl0", "wTh1", "wTh0",
               "sv", "bmv", "wqi", "wdq", "wbuf1", "wbuf0")

    def emit_scales_after_ar2():
        for gslot, sslot, fslot, col in ((S_UG, S_SU, S_FU, 1), (S_GG, S_SG, S_FG, 2)):
            k.vec(lambda e, gslot=gslot: e.tensor_mul(
                out=ps_slot(S_T0), in0=ps_slot(gslot), in1=ps_slot(S_SX)), sig=True)
            bar()
            k.vec(lambda e, sslot=sslot: e.tensor_scalar(
                out=ps_slot(sslot), in0=ps_slot(S_T0), scalar1=1.0 / 32767.0,
                scalar2=1e-12, op0=ALU.mult, op1=ALU.max), sig=True)
            bar()
            k.vec(lambda e, sslot=sslot: e.reciprocal(out=ps_slot(S_T1),
                                                      in_=ps_slot(sslot)), sig=True)
            bar()
            k.vec(lambda e, fslot=fslot: e.tensor_mul(
                out=ps_slot(fslot), in0=ps_slot(S_T1), in1=ps_slot(S_SX)), sig=True)
            bar()
            bcast_panel(fslot, col)
            bar()
        bcast_panel(S_SG, 3)
        bar()

    def emit_scales_after_ar3():
        k.vec(lambda e: e.tensor_mul(out=ps_slot(S_T0), in0=ps_slot(S_QG),
                                     in1=ps_slot(S_SG)), sig=True)
        bar()
        k.vec(lambda e: e.tensor_scalar(out=ps_slot(S_SQ2), in0=ps_slot(S_T0),
                                        scalar1=(1.0 / 65536.0) / 32767.0, scalar2=1e-12,
                                        op0=ALU.mult, op1=ALU.max), sig=True)
        bar()
        k.vec(lambda e: e.reciprocal(out=ps_slot(S_T1), in_=ps_slot(S_SQ2)), sig=True)
        bar()
        k.vec(lambda e: e.tensor_mul(out=ps_slot(S_T2), in0=ps_slot(S_T1),
                                     in1=ps_slot(S_SG)), sig=True)
        bar()
        k.vec(lambda e: e.tensor_scalar_mul(out=ps_slot(S_FQ2), in0=ps_slot(S_T2),
                                            scalar1=1.0 / 65536.0), sig=True)
        bar()
        bcast_panel(S_FQ2, 4)
        bar()

    def emit_scales_after_ar4():
        k.vec(lambda e: e.tensor_mul(out=ps_slot(S_T0), in0=ps_slot(S_PG),
                                     in1=ps_slot(S_SQ2)), sig=True)
        bar()
        k.vec(lambda e: e.tensor_mul(out=ps_slot(S_T0), in0=ps_slot(S_T0),
                                     in1=ps_slot(S_SU)), sig=True)
        bar()
        k.vec(lambda e: e.tensor_scalar(out=ps_slot(S_SO), in0=ps_slot(S_T0),
                                        scalar1=1.0 / 32767.0, scalar2=1e-12,
                                        op0=ALU.mult, op1=ALU.max), sig=True)
        bar()
        k.vec(lambda e: e.reciprocal(out=ps_slot(S_T1), in_=ps_slot(S_SO)), sig=True)
        bar()
        k.vec(lambda e: e.tensor_mul(out=ps_slot(S_T2), in0=ps_slot(S_SQ2),
                                     in1=ps_slot(S_SU)), sig=True)
        bar()
        k.vec(lambda e: e.tensor_mul(out=ps_slot(S_FO), in0=ps_slot(S_T2),
                                     in1=ps_slot(S_T1)), sig=True)
        bar()
        bcast_panel(S_FO, 5)
        bcast_panel(S_SO, 6)
        bar()

    def emit_ffn0_ew_B():
        NO = IFF0 // 128
        ebuf = [k.sbuf("ebuf0", [128, T]), k.sbuf("ebuf1", [128, T])]
        gbuf = [k.sbuf("gbuf0", [128, T]), k.sbuf("gbuf1", [128, T])]
        eki = k.sbuf("eki", [128, T], I16)
        uk16 = [k.sbuf("uk0", [128, T], I16), k.sbuf("uk1", [128, T], I16)]
        sigb = k.sbuf("sigb", [128, T])
        ski = k.sbuf("ski", [128, T], I32)
        k.vec(lambda e: e.memset(runmax[:, 2:3], 0.0), sig=True)
        bar()
        t_lu = [None] * (NO + 2); t_lg = [None] * (NO + 2)
        t_su = [None] * (NO + 2); t_sq = [None] * (NO + 2)
        t_ub = [None] * (NO + 2); t_gb = [None] * (NO + 2)
        t_sig_prev = [None]
        t_lu[0] = k.dma("sync", ebuf[0][:], sp_up[0:128, :])
        t_lg[0] = k.dma("gpsimd", gbuf[0][:], sp_gate[0:128, :])
        t_lu[1] = k.dma("sync", ebuf[1][:], sp_up[128:256, :])
        t_lg[1] = k.dma("gpsimd", gbuf[1][:], sp_gate[128:256, :])
        for o in range(NO):
            osl = slice(o * 128, (o + 1) * 128)
            eb = ebuf[o % 2]; gb = gbuf[o % 2]; uk = uk16[o % 2]
            k.vec(lambda e, eb=eb: e.tensor_scalar(out=eb[:], in0=eb[:],
                                                   scalar1=pbc[:, 1:2], scalar2=MAGIC,
                                                   op0=ALU.mult, op1=ALU.add),
                  wait=[t_lu[o], t_su[o - 2]])
            t_upk = k.vec(lambda e, eb=eb, uk=uk: e.tensor_scalar(
                out=uk[:], in0=eb[:], scalar1=MAGIC, scalar2=None,
                op0=ALU.subtract), wait=[t_su[o - 2]], sig=True)
            t_su[o] = k.dma("sync", sp_upk[osl, :], uk[:], wait=[t_upk])
            if o + 2 < NO:
                t_lu[o + 2] = k.dma("sync", eb[:], sp_up[(o + 2) * 128:(o + 3) * 128, :],
                                    wait=[t_upk])
            k.vec(lambda e, gb=gb: e.tensor_scalar(out=gb[:], in0=gb[:],
                                                   scalar1=pbc[:, 2:3], scalar2=MAGIC,
                                                   op0=ALU.mult, op1=ALU.add),
                  wait=[t_lg[o]])
            t_gk = k.vec(lambda e, gb=gb: e.tensor_scalar(
                out=eki[:], in0=gb[:], scalar1=MAGIC, scalar2=None,
                op0=ALU.subtract), wait=[t_sig_prev[0]], sig=True)
            if o + 2 < NO:
                t_lg[o + 2] = k.dma("gpsimd", gb[:],
                                    sp_gate[(o + 2) * 128:(o + 3) * 128, :], wait=[t_gk])
            t_sg = k.sca(lambda e: e.activation(out=sigb[:], in_=eki[:], func=AF.Sigmoid,
                                                bias=0.0, scale=pbc[:, 3:4]),
                         wait=[t_gk, t_sq[o - 1]], sig=True)
            t_sig_prev[0] = t_sg
            k.vec(lambda e: e.tensor_scalar(out=sigb[:], in0=sigb[:],
                                            scalar1=65536.0, scalar2=MAGIC,
                                            op0=ALU.mult, op1=ALU.add), wait=[t_sg])
            k.vec(lambda e: e.tensor_scalar(out=ski[:], in0=sigb[:], scalar1=MAGIC,
                                            scalar2=65535.0, op0=ALU.subtract,
                                            op1=ALU.min))
            k.vec(lambda e: e.tensor_tensor(out=sigb[:], in0=eki[:], in1=ski[:],
                                            op=ALU.mult))
            k.vec(lambda e: e.reduce_max(out=runmax[:, 1:2], in_=sigb[:], axis=AX.X,
                                         apply_absolute_value=True))
            t_qm = k.vec(lambda e: e.tensor_max(out=runmax[:, 2:3], in0=runmax[:, 2:3],
                                                in1=runmax[:, 1:2]), sig=True)
            t_sq[o] = k.dma("gpsimd", sp_q[osl, :], sigb[:], wait=[t_qm])
        bar()
        psT = k.psum("psT", [1, 128])
        k.tens(lambda e: e.transpose(psT[:], runmax[:, 2:3], identfs[:]), sig=True)
        bar()
        k.vec(lambda e: e.reduce_max(out=ps_slot(S_QG), in_=psT[:], axis=AX.X), sig=True)
        bar()
        k.free("psT")
        k.free("ski", "sigb", "uk1", "uk0", "eki", "gbuf1", "gbuf0", "ebuf1", "ebuf0")

    def emit_ffn0_ew_CD():
        NO = IFF0 // 128
        ebuf = [k.sbuf("ebuf0", [128, T]), k.sbuf("ebuf1", [128, T])]
        uki = [k.sbuf("uki0", [128, T], I16), k.sbuf("uki1", [128, T], I16)]
        eki = [k.sbuf("eki0", [128, T], I16), k.sbuf("eki1", [128, T], I16)]
        pbuf = k.sbuf("pbuf", [128, T])
        k.vec(lambda e: e.memset(runmax[:, 2:3], 0.0), sig=True)
        bar()
        t_lq = [None] * (NO + 2); t_lu = [None] * (NO + 2)
        t_sg = [None] * (NO + 2); t_pm = [None] * (NO + 2)
        t_lq[0] = k.dma("sync", ebuf[0][:], sp_q[0:128, :])
        t_lu[0] = k.dma("gpsimd", uki[0][:], sp_upk[0:128, :])
        t_lq[1] = k.dma("sync", ebuf[1][:], sp_q[128:256, :])
        t_lu[1] = k.dma("gpsimd", uki[1][:], sp_upk[128:256, :])
        for o in range(NO):
            osl = slice(o * 128, (o + 1) * 128)
            eb = ebuf[o % 2]; uk = uki[o % 2]; ek = eki[o % 2]
            k.vec(lambda e, eb=eb: e.tensor_scalar(out=eb[:], in0=eb[:],
                                                   scalar1=pbc[:, 4:5], scalar2=MAGIC,
                                                   op0=ALU.mult, op1=ALU.add),
                  wait=[t_lq[o], t_sg[o - 2]])
            t_g2 = k.vec(lambda e, eb=eb, ek=ek: e.tensor_scalar(
                out=ek[:], in0=eb[:], scalar1=MAGIC, scalar2=None,
                op0=ALU.subtract), sig=True)
            t_sg[o] = k.dma("sync", sp_g2k[osl, :], ek[:], wait=[t_g2])
            if o + 2 < NO:
                t_lq[o + 2] = k.dma("sync", eb[:], sp_q[(o + 2) * 128:(o + 3) * 128, :],
                                    wait=[t_g2])
            t_p = k.vec(lambda e, uk=uk, ek=ek: e.tensor_tensor(
                out=pbuf[:], in0=ek[:], in1=uk[:], op=ALU.mult),
                wait=[t_lu[o]], sig=True)
            if o + 2 < NO:
                t_lu[o + 2] = k.dma("gpsimd", uk[:],
                                    sp_upk[(o + 2) * 128:(o + 3) * 128, :], wait=[t_p])
            k.vec(lambda e: e.reduce_max(out=runmax[:, 1:2], in_=pbuf[:], axis=AX.X,
                                         apply_absolute_value=True))
            t_pm[o] = k.vec(lambda e: e.tensor_max(out=runmax[:, 2:3], in0=runmax[:, 2:3],
                                                   in1=runmax[:, 1:2]), sig=True)
        bar()
        psT = k.psum("psT", [1, 128])
        k.tens(lambda e: e.transpose(psT[:], runmax[:, 2:3], identfs[:]), sig=True)
        bar()
        k.vec(lambda e: e.reduce_max(out=ps_slot(S_PG), in_=psT[:], axis=AX.X), sig=True)
        bar()
        k.free("psT")
        k.free("pbuf", "eki1", "eki0", "uki1", "uki0", "ebuf1", "ebuf0")

    def emit_ffn0_ew_E():
        NO = IFF0 // 128
        ebuf = [k.sbuf("ebuf0", [128, T]), k.sbuf("ebuf1", [128, T])]
        uki = [k.sbuf("uki0", [128, T], I16), k.sbuf("uki1", [128, T], I16)]
        gki = [k.sbuf("gki0", [128, T], I16), k.sbuf("gki1", [128, T], I16)]
        oh_t = [k.sbuf("oh0", [128, T], BF16), k.sbuf("oh1", [128, T], BF16)]
        ol_t = [k.sbuf("ol0", [128, T], BF16), k.sbuf("ol1", [128, T], BF16)]
        t_lg = [None] * (NO + 2); t_lu = [None] * (NO + 2)
        t_so = [None] * (NO + 2); t_soh2 = [None] * (NO + 2)
        t_lg[0] = k.dma("sync", gki[0][:], sp_g2k[0:128, :])
        t_lu[0] = k.dma("gpsimd", uki[0][:], sp_upk[0:128, :])
        t_lg[1] = k.dma("sync", gki[1][:], sp_g2k[128:256, :])
        t_lu[1] = k.dma("gpsimd", uki[1][:], sp_upk[128:256, :])
        for o in range(NO):
            osl = slice(o * 128, (o + 1) * 128)
            eb = ebuf[o % 2]; uk = uki[o % 2]; gk = gki[o % 2]
            oh = oh_t[o % 2]; ol = ol_t[o % 2]
            t_p = k.vec(lambda e, eb=eb, gk=gk, uk=uk: e.tensor_tensor(
                out=eb[:], in0=gk[:], in1=uk[:], op=ALU.mult),
                wait=[t_lg[o], t_lu[o]], sig=True)
            k.vec(lambda e, eb=eb: e.tensor_scalar(out=eb[:], in0=eb[:],
                                                   scalar1=pbc[:, 5:6], scalar2=MAGIC,
                                                   op0=ALU.mult, op1=ALU.add))
            t_ok = k.vec(lambda e, eb=eb, uk=uk: e.tensor_scalar(
                out=uk[:], in0=eb[:], scalar1=MAGIC, scalar2=None,
                op0=ALU.subtract), sig=True)
            th = k.sca(lambda e, uk=uk, oh=oh: e.activation(
                out=oh[:], in_=uk[:], func=AF.Copy, bias=0.0, scale=1.0),
                wait=[t_ok, t_soh2[o - 2]], sig=True)
            tl = k.vec(lambda e, uk=uk, oh=oh, ol=ol: e.scalar_tensor_tensor(
                out=ol[:], in0=uk[:], scalar=1.0, in1=oh[:], op0=ALU.mult,
                op1=ALU.subtract), wait=[th, t_so[o - 2]], sig=True)
            if o + 2 < NO:
                t_lg[o + 2] = k.dma("sync", gk[:], sp_g2k[(o + 2) * 128:(o + 3) * 128, :],
                                    wait=[t_p])
                t_lu[o + 2] = k.dma("gpsimd", uk[:],
                                    sp_upk[(o + 2) * 128:(o + 3) * 128, :], wait=[tl])
            t_soh = k.dma("sync", sp_oh[osl, :], oh[:], wait=[th])
            t_sol = k.dma("gpsimd", sp_ol[osl, :], ol[:], wait=[tl])
            t_so[o] = t_sol
            t_soh2[o] = t_soh
        bar()
        k.free("ol1", "ol0", "oh1", "oh0", "gki1", "gki0", "uki1", "uki0",
               "ebuf1", "ebuf0")

    wdTh_d = k.dscratch("wdTh_d", [NCH, IFF0, 128], BF16)
    wdTl_d = k.dscratch("wdTl_d", [NCH, IFF0, 128], BF16)

    def emit_ffn0_down_prep():
        wbuf = [k.sbuf("wbufA", [128, IFF0]), k.sbuf("wbufB", [128, IFF0])]
        wdq = k.sbuf("wdq", [128, IFF0])
        wqi = k.sbuf("wqi", [128, IFF0], I16)
        bmv = k.sbuf("bmv", [128, 96]); sv = k.sbuf("sv", [128, 96])
        wTh = k.sbuf("wTh", [128, 256], BF16); wTl = k.sbuf("wTl", [128, 256], BF16)
        psW = k.psum("psW", [128, 128])
        NIC = IFF0 // 128
        t_load = [None] * (NCH + 2)
        t_wdqfree = [None] * (NCH + 2)
        t_sh = [None] * (NCH * NIC); t_sl = [None] * (NCH * NIC)
        t_load[0] = k.dma("sync", wbuf[0][:], wdown[0:128, :])
        t_load[1] = k.dma("gpsimd", wbuf[1][:], wdown[128:256, :])
        for dh in range(NCH):
            wb = wbuf[dh % 2]
            w0 = [t_load[dh]] + ([t_wdqfree[dh - 1]] if dh > 0 else [])
            t_free, t_deq = emit_lpbq_dequant_tok(wb, wdq, wqi, bmv, sv, IFF0, w0)
            if dh + 2 < NCH:
                eng = "sync" if dh % 2 == 0 else "gpsimd"
                t_load[dh + 2] = k.dma(eng, wb[:],
                                       wdown[(dh + 2) * 128:(dh + 3) * 128, :],
                                       wait=[t_free])
            last_tt = None
            for ic in range(NIC):
                gc = dh * NIC + ic
                sl2 = slice((gc % 2) * 128, (gc % 2) * 128 + 128)
                t_t = k.tens(lambda e, ic=ic: e.transpose(
                    psW[:], wdq[:, ic * 128:(ic + 1) * 128], identfs[:]),
                    wait=[t_deq] + ([last_tt] if last_tt else []), sig=True)
                th = k.sca(lambda e, sl2=sl2: e.activation(
                    out=wTh[:, sl2], in_=psW[:], func=AF.Copy, bias=0.0, scale=1.0),
                    wait=[t_t, t_sh[gc - 2] if gc >= 2 else None], sig=True)
                last_tt = k.vec(lambda e, sl2=sl2: e.scalar_tensor_tensor(
                    out=wTl[:, sl2], in0=psW[:], scalar=1.0, in1=wTh[:, sl2],
                    op0=ALU.mult, op1=ALU.subtract),
                    wait=[th, t_sl[gc - 2] if gc >= 2 else None], sig=True)
                t_sh[gc] = k.dma("sync",
                                 AP(wdTh_d.tensor, (dh * IFF0 + ic * 128) * 128,
                                    [[128, 128], [1, 128]]), wTh[:, sl2], wait=[th])
                t_sl[gc] = k.dma("gpsimd",
                                 AP(wdTl_d.tensor, (dh * IFF0 + ic * 128) * 128,
                                    [[128, 128], [1, 128]]), wTl[:, sl2], wait=[last_tt])
            t_wdqfree[dh] = last_tt
        bar()
        k.free("psW", "wTl", "wTh", "sv", "bmv", "wqi", "wdq", "wbufB", "wbufA")

    def emit_ffn0_down():
        oth = k.sbuf("oth", [128, 24 * 512], BF16)
        otl = k.sbuf("otl", [128, 24 * 512], BF16)
        wTh = k.sbuf("wTh", [128, IFF0], BF16)
        wTl = k.sbuf("wTl", [128, IFF0], BF16)
        dtmp = k.sbuf("dtmp", [128, 512])
        psD = [k.psum("psD0"), k.psum("psD1")]
        for tc in range(4):
            tsl = slice(tc * 512, (tc + 1) * 512)
            for ic in range(IFF0 // 128):
                k.dma("sync", oth[:, ic * 512:(ic + 1) * 512], sp_oh[ic * 128:(ic + 1) * 128, tsl])
                k.dma("gpsimd", otl[:, ic * 512:(ic + 1) * 512], sp_ol[ic * 128:(ic + 1) * 128, tsl])
            bar()
            for dh in range(NCH):
                k.dma("sync", wTh[:].rearrange("p (a b) -> p a b", b=128),
                      AP(wdTh_d.tensor, dh * IFF0 * 128,
                         [[128, 128], [128 * 128, 24], [1, 128]]))
                k.dma("gpsimd", wTl[:].rearrange("p (a b) -> p a b", b=128),
                      AP(wdTl_d.tensor, dh * IFF0 * 128,
                         [[128, 128], [128 * 128, 24], [1, 128]]))
                bar()
                ps = psD[dh % 2]
                for ic in range(IFF0 // 128):
                    lh = wTh[:, ic * 128:(ic + 1) * 128]
                    ll = wTl[:, ic * 128:(ic + 1) * 128]
                    rh = oth[:, ic * 512:(ic + 1) * 512]
                    rl = otl[:, ic * 512:(ic + 1) * 512]
                    k.tens(lambda e, ps=ps, lh=lh, rh=rh, first=(ic == 0): e.matmul(
                        ps[:], lh, rh, start=first, stop=False))
                    k.tens(lambda e, ps=ps, lh=lh, rl=rl: e.matmul(
                        ps[:], lh, rl, start=False, stop=False))
                    k.tens(lambda e, ps=ps, ll=ll, rh=rh, last=(ic == IFF0 // 128 - 1): e.matmul(
                        ps[:], ll, rh, start=False, stop=last), sig=(ic == IFF0 // 128 - 1))
                bar()
                k.vec(lambda e, ps=ps: e.tensor_scalar(out=dtmp[:], in0=ps[:],
                                                       scalar1=pbc[:, 6:7], scalar2=None,
                                                       op0=ALU.mult), sig=True)
                bar()
                k.vec(lambda e, dh=dh, tc=tc: e.tensor_tensor(
                    out=X[:, dh * T + tc * 512: dh * T + tc * 512 + 512],
                    in0=X[:, dh * T + tc * 512: dh * T + tc * 512 + 512],
                    in1=dtmp[:], op=ALU.add), sig=True)
                bar()
        k.free("psD1", "psD0")
        k.free("dtmp", "wTl", "wTh", "otl", "oth")

    b1col = k.din("b1col", [128, 32])

    def emit_ffn1():
        for dh in range(NCH):
            k.vec(lambda e, dh=dh: e.tensor_scalar(
                out=X[:, dh * T:(dh + 1) * T], in0=X[:, dh * T:(dh + 1) * T],
                scalar1=bcols[:, 32 + dh: 32 + dh + 1], scalar2=None,
                op0=ALU.add), sig=True)
        b1c = k.sbuf("b1c", [128, 32])
        k.dma("sync", b1c[:], b1col[:])
        wbufq = [k.sbuf("wbufq0", [128, DFF1]), k.sbuf("wbufq1", [128, DFF1])]
        wq16 = [k.sbuf("wq16a", [128, DFF1], BF16), k.sbuf("wq16b", [128, DFF1], BF16)]
        gacc = k.sbuf("gacc", [128, 1])
        gtmp = k.sbuf("gtmp", [128, 1])
        psT = k.psum("psT", [1, 128])
        rd_tok = [None, None]
        st_tok = [None, None]
        for wsrc, wdst, nchk, wid, col in ((w1T, w1q_d, 8, DFF1, 7),
                                           (w2T, w2q_d, 32, D, 8)):
            k.vec(lambda e: e.memset(gacc[:], 0.0), sig=True)
            ld = [None] * (nchk + 2)
            ld[0] = k.dma("sync", wbufq[0][0:128, 0:wid], wsrc[0:128, :],
                          wait=[rd_tok[0]])
            if nchk > 1:
                ld[1] = k.dma("gpsimd", wbufq[1][0:128, 0:wid], wsrc[128:256, :],
                              wait=[rd_tok[1]])
            t_add = None
            for c in range(nchk):
                wb = wbufq[c % 2]
                t_rs = k.vec(lambda e, wb=wb, wid=wid: e.reduce_sum(
                    out=gtmp[:], in_=wb[0:128, 0:wid], axis=AX.X,
                    apply_absolute_value=True), wait=[ld[c]], sig=True)
                rd_tok[c % 2] = t_rs
                t_add = k.vec(lambda e: e.tensor_add(out=gacc[:], in0=gacc[:],
                                                     in1=gtmp[:]), sig=True)
                if c + 2 < nchk:
                    eng = "sync" if c % 2 == 0 else "gpsimd"
                    ld[c + 2] = k.dma(eng, wb[0:128, 0:wid],
                                      wsrc[(c + 2) * 128:(c + 3) * 128, :],
                                      wait=[t_rs])
            tt = k.tens(lambda e: e.transpose(psT[:], gacc[:], identfs[:]),
                        wait=[t_add], sig=True)
            k.vec(lambda e: e.reduce_sum(out=ps_slot(S_T0), in_=psT[:], axis=AX.X),
                  wait=[tt], sig=True)
            k.vec(lambda e: e.tensor_scalar(out=ps_slot(S_T1), in0=ps_slot(S_T0),
                                            scalar1=1.0 / (D * DFF1), scalar2=1e-5,
                                            op0=ALU.mult, op1=ALU.add), sig=True)
            k.vec(lambda e: e.reciprocal(out=ps_slot(S_T2), in_=ps_slot(S_T1)), sig=True)
            bar()
            bcast_panel(S_T2, col)
            ld2 = [None] * (nchk + 2)
            ld2[0] = k.dma("sync", wbufq[0][0:128, 0:wid], wsrc[0:128, :],
                           wait=[rd_tok[0]])
            if nchk > 1:
                ld2[1] = k.dma("gpsimd", wbufq[1][0:128, 0:wid], wsrc[128:256, :],
                               wait=[rd_tok[1]])
            for c in range(nchk):
                wb = wbufq[c % 2]; wq = wq16[c % 2]
                k.vec(lambda e, wb=wb, wid=wid, col=col: e.tensor_scalar(
                    out=wb[0:128, 0:wid], in0=wb[0:128, 0:wid],
                    scalar1=pbc[:, col:col + 1], scalar2=MAGIC,
                    op0=ALU.mult, op1=ALU.add), wait=[ld2[c]], sig=True)
                k.vec(lambda e, wb=wb, wid=wid: e.tensor_scalar(
                    out=wb[0:128, 0:wid], in0=wb[0:128, 0:wid],
                    scalar1=MAGIC, scalar2=1.0, op0=ALU.subtract, op1=ALU.min), sig=True)
                t3 = k.vec(lambda e, wb=wb, wq=wq, wid=wid: e.tensor_scalar(
                    out=wq[0:128, 0:wid], in0=wb[0:128, 0:wid],
                    scalar1=-1.0, scalar2=None, op0=ALU.max),
                    wait=[st_tok[c % 2]], sig=True)
                rd_tok[c % 2] = t3
                if c + 2 < nchk:
                    eng = "sync" if c % 2 == 0 else "gpsimd"
                    ld2[c + 2] = k.dma(eng, wb[0:128, 0:wid],
                                       wsrc[(c + 2) * 128:(c + 3) * 128, :],
                                       wait=[t3])
                st_tok[c % 2] = k.dma("gpsimd" if c % 2 == 0 else "sync",
                                      wdst[c * 128:(c + 1) * 128, :],
                                      wq[0:128, 0:wid], wait=[t3])
        bar()
        k.free("psT", "gtmp", "gacc", "wq16b", "wq16a", "wbufq1", "wbufq0")
        TC1 = 256
        h_h = k.sbuf("h_h", [128, 32 * TC1], BF16)
        h_l = k.sbuf("h_l", [128, 32 * TC1], BF16)
        htmp = [k.sbuf("htmp0", [128, TC1]), k.sbuf("htmp1", [128, TC1])]
        w1blk = [k.sbuf("w1blk0", [128, 1024], BF16), k.sbuf("w1blk1", [128, 1024], BF16)]
        w2rows = [k.sbuf("w2rows0", [128, 1024], BF16), k.sbuf("w2rows1", [128, 1024], BF16)]
        w1src = lambda o: AP(w1q_d.tensor, o * 128, [[DFF1, 128], [128 * DFF1, 8], [1, 128]])
        for tc in range(T // TC1):
            psF = [k.psum("psF0", [128, TC1]), k.psum("psF1", [128, TC1])]
            t_l = [None] * 34
            t_mm = [None] * 34
            t_hl = [None] * 34
            t_l[0] = k.dma("sync", w1blk[0][:].rearrange("p (a b) -> p a b", b=128), w1src(0))
            t_l[1] = k.dma("gpsimd", w1blk[1][:].rearrange("p (a b) -> p a b", b=128), w1src(1))
            for o in range(32):
                ps = psF[o % 2]
                for kc in range(NCH):
                    lh = w1blk[o % 2][:, kc * 128:(kc + 1) * 128]
                    rh = XH[:, kc * T + tc * TC1: kc * T + tc * TC1 + TC1]
                    rl = XL[:, kc * T + tc * TC1: kc * T + tc * TC1 + TC1]
                    k.tens(lambda e, ps=ps, lh=lh, rh=rh, first=(kc == 0): e.matmul(
                        ps[:], lh, rh, start=first, stop=False),
                        wait=[t_l[o], t_hl[o - 2]] if kc == 0 else [])
                    t = k.tens(lambda e, ps=ps, lh=lh, rl=rl, last=(kc == NCH - 1): e.matmul(
                        ps[:], lh, rl, start=False, stop=last), sig=(kc == NCH - 1))
                t_mm[o] = t
                if o + 2 < 32:
                    eng = "sync" if o % 2 == 0 else "gpsimd"
                    t_l[o + 2] = k.dma(eng, w1blk[o % 2][:].rearrange("p (a b) -> p a b", b=128),
                                       w1src(o + 2), wait=[t])
                th1 = k.sca(lambda e, o=o, ps=ps: e.activation(
                    out=h_h[:, o * TC1:(o + 1) * TC1], in_=ps[:], func=AF.Relu,
                    bias=b1c[:, o:o + 1], scale=1.0), wait=[t], sig=True)
                th2 = k.sca(lambda e, ps=ps, o=o: e.activation(
                    out=htmp[o % 2][:], in_=ps[:], func=AF.Relu, bias=b1c[:, o:o + 1],
                    scale=1.0), wait=[t_hl[o - 2]], sig=True)
                t_hl[o] = k.vec(lambda e, o=o: e.tensor_sub(
                    out=h_l[:, o * TC1:(o + 1) * TC1], in0=htmp[o % 2][:],
                    in1=h_h[:, o * TC1:(o + 1) * TC1]), wait=[th1, th2], sig=True)
            k.free("psF1", "psF0")
            psD2 = [k.psum("psD2_%d" % i, [128, TC1]) for i in range(NCH)]
            t_l2 = [None] * 34
            t_mml = [None] * 34
            t_l2[0] = k.dma("sync", w2rows[0][:], w2q_d[0:128, :], wait=[t_hl[30]])
            t_l2[1] = k.dma("gpsimd", w2rows[1][:], w2q_d[128:256, :], wait=[t_hl[31]])
            for ic in range(32):
                for dh in range(NCH):
                    lh = w2rows[ic % 2][:, dh * 128:(dh + 1) * 128]
                    rh = h_h[:, ic * TC1:(ic + 1) * TC1]
                    rl = h_l[:, ic * TC1:(ic + 1) * TC1]
                    ps = psD2[dh]
                    k.tens(lambda e, lh=lh, rh=rh, ps=ps, first=(ic == 0): e.matmul(
                        ps[:], lh, rh, start=first, stop=False),
                        wait=[t_l2[ic], t_hl[ic], t_hl[31]] if dh == 0 else [])
                    t = k.tens(lambda e, lh=lh, rl=rl, ps=ps, last=(ic == 31): e.matmul(
                        ps[:], lh, rl, start=False, stop=last), sig=(dh == NCH - 1))
                t_mml[ic] = t
                if ic + 2 < 32:
                    eng = "sync" if ic % 2 == 0 else "gpsimd"
                    t_l2[ic + 2] = k.dma(eng, w2rows[ic % 2][:],
                                         w2q_d[(ic + 2) * 128:(ic + 3) * 128, :], wait=[t])
            bar()
            for dh in range(NCH):
                ps = psD2[dh]
                k.vec(lambda e, dh=dh, tc=tc, ps=ps: e.tensor_tensor(
                    out=X[:, dh * T + tc * TC1: dh * T + tc * TC1 + TC1],
                    in0=X[:, dh * T + tc * TC1: dh * T + tc * TC1 + TC1],
                    in1=ps[:], op=ALU.add), sig=True)
            bar()
            for i in reversed(range(NCH)):
                k.free("psD2_%d" % i)
        k.free("w2rows1", "w2rows0", "w1blk1", "w1blk0", "htmp1", "htmp0",
               "h_l", "h_h", "b1c")

    Bre = k.din("Bre", [NSSM, D]); Bim = k.din("Bim", [NSSM, D])
    Dcol = k.din("Dcol", [128, NCH])

    def emit_ssm_prep():
        lr = k.sbuf("lr", [128, 4]); li = k.sbuf("li", [128, 4])
        dtv = k.sbuf("dtv", [128, 4])
        av = k.sbuf("av", [128, 4]); th = k.sbuf("th", [128, 4])
        lbr = k.sbuf("lbr", [128, 4]); lbi = k.sbuf("lbi", [128, 4])
        lir = k.sbuf("lir", [128, 4]); lii = k.sbuf("lii", [128, 4])
        rr_ = k.sbuf("rr_", [128, 4]); ri_ = k.sbuf("ri_", [128, 4])
        u1 = k.sbuf("u1", [128, 4]); u2 = k.sbuf("u2", [128, 4]); u3 = k.sbuf("u3", [128, 4])
        k.dma("sync", lr[:], lam_re[:]); k.dma("sync", li[:], lam_im[:])
        k.dma("sync", dtv[:], stepv[:])
        bar()
        k.vec(lambda e: e.tensor_scalar_min(out=lr[:], in0=lr[:], scalar1=-1e-4), sig=True)
        bar()
        k.sca(lambda e: e.activation(out=dtv[:], in_=dtv[:], func=AF.Exp,
                                     bias=0.0, scale=1.0), sig=True)
        bar()
        k.vec(lambda e: e.tensor_mul(out=av[:], in0=lr[:], in1=dtv[:]), sig=True)
        k.vec(lambda e: e.tensor_mul(out=th[:], in0=li[:], in1=dtv[:]), sig=True)
        bar()
        k.sca(lambda e: e.activation(out=u1[:], in_=av[:], func=AF.Exp,
                                     bias=0.0, scale=1.0 / 128), sig=True)
        k.sca(lambda e: e.activation(out=u2[:], in_=th[:], func=AF.Sin,
                                     bias=0.0, scale=1.0 / 128), sig=True)
        k.sca(lambda e: e.activation(out=u3[:], in_=th[:], func=AF.Sin,
                                     bias=cstv[:, 1:2], scale=1.0 / 128), sig=True)
        bar()
        k.vec(lambda e: e.tensor_mul(out=lbr[:], in0=u1[:], in1=u3[:]), sig=True)
        k.vec(lambda e: e.tensor_mul(out=lbi[:], in0=u1[:], in1=u2[:]), sig=True)
        bar()
        for _ in range(7):
            k.vec(lambda e: e.tensor_mul(out=u1[:], in0=lbr[:], in1=lbr[:]), sig=True)
            k.vec(lambda e: e.tensor_mul(out=u2[:], in0=lbi[:], in1=lbi[:]), sig=True)
            k.vec(lambda e: e.tensor_mul(out=u3[:], in0=lbr[:], in1=lbi[:]), sig=True)
            bar()
            k.vec(lambda e: e.tensor_sub(out=lbr[:], in0=u1[:], in1=u2[:]), sig=True)
            k.vec(lambda e: e.tensor_scalar_mul(out=lbi[:], in0=u3[:], scalar1=2.0), sig=True)
            bar()
        k.sca(lambda e: e.activation(out=u1[:], in_=av[:], func=AF.Exp,
                                     bias=0.0, scale=-2.0), sig=True)
        bar()
        k.vec(lambda e: e.tensor_mul(out=lir[:], in0=lbr[:], in1=u1[:]), sig=True)
        k.vec(lambda e: e.tensor_mul(out=lii[:], in0=lbi[:], in1=u1[:]), sig=True)
        bar()
        k.vec(lambda e: e.tensor_scalar_mul(out=lii[:], in0=lii[:], scalar1=-1.0), sig=True)
        bar()
        k.vec(lambda e: e.tensor_mul(out=u1[:], in0=lr[:], in1=lr[:]), sig=True)
        k.vec(lambda e: e.tensor_mul(out=u2[:], in0=li[:], in1=li[:]), sig=True)
        bar()
        k.vec(lambda e: e.tensor_add(out=u1[:], in0=u1[:], in1=u2[:]), sig=True)
        bar()
        k.vec(lambda e: e.reciprocal(out=u1[:], in_=u1[:]), sig=True)
        bar()
        k.vec(lambda e: e.tensor_scalar_add(out=u2[:], in0=lbr[:], scalar1=-1.0), sig=True)
        bar()
        k.vec(lambda e: e.tensor_mul(out=rr_[:], in0=u2[:], in1=lr[:]), sig=True)
        k.vec(lambda e: e.tensor_mul(out=u3[:], in0=lbi[:], in1=li[:]), sig=True)
        bar()
        k.vec(lambda e: e.tensor_add(out=rr_[:], in0=rr_[:], in1=u3[:]), sig=True)
        bar()
        k.vec(lambda e: e.tensor_mul(out=rr_[:], in0=rr_[:], in1=u1[:]), sig=True)
        k.vec(lambda e: e.tensor_mul(out=ri_[:], in0=lbi[:], in1=lr[:]), sig=True)
        k.vec(lambda e: e.tensor_mul(out=u3[:], in0=u2[:], in1=li[:]), sig=True)
        bar()
        k.vec(lambda e: e.tensor_sub(out=ri_[:], in0=ri_[:], in1=u3[:]), sig=True)
        bar()
        k.vec(lambda e: e.tensor_mul(out=ri_[:], in0=ri_[:], in1=u1[:]), sig=True)
        bar()
        k.free("u3", "u2", "u1")
        return lr, li, dtv, av, th, lbr, lbi, lir, lii, rr_, ri_

    def emit_ssm_bu(lbr, lbi, lir, lii, rr_, ri_):
        bbth = k.sbuf("bbth", [128, NCH * 512], BF16)
        bbtl = k.sbuf("bbtl", [128, NCH * 512], BF16)
        bbih = k.sbuf("bbih", [128, NCH * 512], BF16)
        bbil = k.sbuf("bbil", [128, NCH * 512], BF16)
        bb1 = k.sbuf("bb1", [128, 1024]); bb2 = k.sbuf("bb2", [128, 1024])
        bb3 = k.sbuf("bb3", [128, 1024]); bb4 = k.sbuf("bb4", [128, 1024])
        psW2 = [k.psum("psWa", [128, 128]), k.psum("psWb", [128, 128])]
        stt = [None] * 70
        t_s1 = None; t_s2 = None
        last_tp = None
        for c in range(4):
            csl = slice(c * 128, (c + 1) * 128)
            tl1 = k.dma("sync", bb1[:], Bre[csl, :], wait=[t_s1])
            tl2 = k.dma("gpsimd", bb2[:], Bim[csl, :], wait=[t_s2])
            for comp in range(2):
                s1 = (rr_, ri_)[comp]; s2 = (ri_, rr_)[comp]
                dsth = (bbth, bbih)[comp]; dstl = (bbtl, bbil)[comp]
                t_s1 = k.vec(lambda e, s1=s1, c=c: e.tensor_scalar(
                    out=bb3[:], in0=bb1[:], scalar1=s1[:, c:c + 1], scalar2=None,
                    op0=ALU.mult), wait=[tl1, last_tp], sig=True)
                t_s2 = k.vec(lambda e, s2=s2, c=c: e.tensor_scalar(
                    out=bb4[:], in0=bb2[:], scalar1=s2[:, c:c + 1], scalar2=None,
                    op0=ALU.mult), wait=[tl2], sig=True)
                op = ALU.subtract if comp == 0 else ALU.add
                t_tt = k.vec(lambda e, op=op: e.tensor_tensor(out=bb3[:], in0=bb3[:],
                                                              in1=bb4[:], op=op), sig=True)
                for hc in range(NCH):
                    g = (c * 2 + comp) * NCH + hc
                    ps = psW2[g % 2]
                    last_tp = k.tens(lambda e, hc=hc, ps=ps: e.transpose(
                        ps[:], bb3[:, hc * 128:(hc + 1) * 128], identfs[:]),
                        wait=[t_tt if hc == 0 else None,
                              stt[g - 2] if g >= 2 else None], sig=True)
                    dst = slice(hc * 512 + c * 128, hc * 512 + c * 128 + 128)
                    th = k.sca(lambda e, dst=dst, dsth=dsth, ps=ps: e.activation(
                        out=dsth[:, dst], in_=ps[:], func=AF.Copy,
                        bias=0.0, scale=1.0), wait=[last_tp], sig=True)
                    stt[g] = k.vec(lambda e, dst=dst, dsth=dsth, dstl=dstl, ps=ps:
                                   e.scalar_tensor_tensor(
                        out=dstl[:, dst], in0=ps[:], scalar=1.0, in1=dsth[:, dst],
                        op0=ALU.mult, op1=ALU.subtract), wait=[th], sig=True)
        bar()
        k.free("psWb", "psWa", "bb4", "bb3", "bb2", "bb1")
        stage = k.sbuf("stage", [128, 1024])
        psB2 = [k.psum("psB20"), k.psum("psB21")]
        tact = [None] * 34
        t_st = [None] * 34
        for c in range(4):
            for tc in range(4):
                for comp in range(2):
                    j = (c * 4 + tc) * 2 + comp
                    ps = psB2[comp]
                    hsrc = (bbth, bbih)[comp]; lsrc = (bbtl, bbil)[comp]
                    spdst = (sp_bur, sp_bui)[comp]
                    t_mm = None
                    for hc in range(NCH):
                        lh = hsrc[:, hc * 512 + c * 128: hc * 512 + c * 128 + 128]
                        ll = lsrc[:, hc * 512 + c * 128: hc * 512 + c * 128 + 128]
                        rh = XH[:, hc * T + tc * 512: hc * T + tc * 512 + 512]
                        rl = XL[:, hc * T + tc * 512: hc * T + tc * 512 + 512]
                        k.tens(lambda e, ps=ps, lh=lh, rh=rh, first=(hc == 0): e.matmul(
                            ps[:], lh, rh, start=first, stop=False),
                            wait=[tact[j - 2] if (hc == 0 and j >= 2) else None])
                        k.tens(lambda e, ps=ps, lh=lh, rl=rl: e.matmul(
                            ps[:], lh, rl, start=False, stop=False))
                        t_mm = k.tens(lambda e, ps=ps, ll=ll, rh=rh,
                                      last=(hc == NCH - 1): e.matmul(
                            ps[:], ll, rh, start=False, stop=last), sig=(hc == NCH - 1))
                    sl2 = slice((j % 2) * 512, (j % 2) * 512 + 512)
                    tact[j] = k.sca(lambda e, ps=ps, sl2=sl2: e.activation(
                        out=stage[:, sl2], in_=ps[:], func=AF.Copy, bias=0.0,
                        scale=1.0), wait=[t_mm, t_st[j - 2] if j >= 2 else None],
                        sig=True)
                    t_st[j] = k.dma("sync" if j % 2 == 0 else "gpsimd",
                                    spdst[c * 128:(c + 1) * 128, tc * 512:(tc + 1) * 512],
                                    stage[:, sl2], wait=[tact[j]])
        k.free("psB21", "psB20", "stage")
        k.free("bbil", "bbih", "bbtl", "bbth")
        v32 = k.sbuf("duv", [128, T])
        dcl = k.sbuf("dcl", [128, NCH])
        k.dma("sync", dcl[:], Dcol[:])
        bar()
        for hc in range(NCH):
            sl = slice(hc * T, (hc + 1) * T)
            k.vec(lambda e, sl=sl: e.tensor_add(out=v32[:], in0=XH[:, sl], in1=XL[:, sl]), sig=True)
            bar()
            k.vec(lambda e, hc=hc: e.tensor_scalar(
                out=v32[:], in0=v32[:], scalar1=dcl[:, hc:hc + 1], scalar2=None,
                op0=ALU.mult), sig=True)
            bar()
            k.dma("sync", sp_du[hc * 128:(hc + 1) * 128, :], v32[:])
            bar()
        k.free("dcl", "duv")
        cb1 = k.sbuf("cb1", [128, 1024])
        cbh = k.sbuf("cbh", [128, 1024], BF16)
        for src, hname, lname, neg in ((CreT, "reh", "rel", 1.0), (CimT, "imh", "iml", -1.0)):
            for c in range(4):
                csl = slice(c * 128, (c + 1) * 128)
                k.dma("sync", cb1[:], src[csl, :])
                bar()
                k.sca(lambda e, neg=neg: e.activation(out=cbh[:], in_=cb1[:],
                                                      func=AF.Copy, bias=0.0,
                                                      scale=neg), sig=True)
                bar()
                k.vec(lambda e, neg=neg: e.scalar_tensor_tensor(
                    out=cb1[:], in0=cb1[:], scalar=neg, in1=cbh[:],
                    op0=ALU.mult, op1=ALU.subtract), sig=True)
                bar()
                k.dma("sync", csp_d[hname][csl, :], cbh[:])
                bar()
                k.vec(lambda e: e.tensor_copy(out=cbh[:], in_=cb1[:]), sig=True)
                bar()
                k.dma("sync", csp_d[lname][csl, :], cbh[:])
                bar()
        k.free("cbh", "cb1")

    def emit_ssm_core(lbr, lbi, lir, lii):
        Tre = k.sbuf("Tre", [128, 4 * 512]); Tim = k.sbuf("Tim", [128, 4 * 512])
        smr = k.sbuf("smr", [128, 1]); smi = k.sbuf("smi", [128, 1])
        t1 = k.sbuf("sst1", [128, 512]); t2 = k.sbuf("sst2", [128, 512])
        onet = k.sbuf("onet", [128, 512])
        k.vec(lambda e: e.memset(onet[:], 1.0), sig=True)
        for Tr, Ti, br, bi in ((Tre, Tim, lbr, lbi),):
            for c in range(4):
                base = c * 512
                k.vec(lambda e, Tr=Tr, base=base: e.memset(Tr[:, base:base + 1], 1.0), sig=True)
                k.vec(lambda e, Ti=Ti, base=base: e.memset(Ti[:, base:base + 1], 0.0), sig=True)
                m = 1
                while m < 512:
                    k.vec(lambda e, Tr=Tr, base=base, m=m, br=br, c=c: e.tensor_scalar(
                        out=smr[:], in0=Tr[:, base + m - 1: base + m],
                        scalar1=br[:, c:c + 1], scalar2=None, op0=ALU.mult))
                    k.vec(lambda e, Ti=Ti, base=base, m=m, bi=bi, c=c: e.tensor_scalar(
                        out=t1[:, 0:1], in0=Ti[:, base + m - 1: base + m],
                        scalar1=bi[:, c:c + 1], scalar2=None, op0=ALU.mult))
                    k.vec(lambda e: e.tensor_sub(out=smr[:], in0=smr[:], in1=t1[:, 0:1]))
                    k.vec(lambda e, Tr=Tr, base=base, m=m, bi=bi, c=c: e.tensor_scalar(
                        out=smi[:], in0=Tr[:, base + m - 1: base + m],
                        scalar1=bi[:, c:c + 1], scalar2=None, op0=ALU.mult))
                    k.vec(lambda e, Ti=Ti, base=base, m=m, br=br, c=c: e.tensor_scalar(
                        out=t1[:, 0:1], in0=Ti[:, base + m - 1: base + m],
                        scalar1=br[:, c:c + 1], scalar2=None, op0=ALU.mult))
                    k.vec(lambda e: e.tensor_add(out=smi[:], in0=smi[:], in1=t1[:, 0:1]))
                    w = min(m, 512 - m)
                    k.vec(lambda e, Tr=Tr, base=base, w=w: e.tensor_scalar(
                        out=t1[:, 0:w], in0=Tr[:, base:base + w], scalar1=smr[:, 0:1],
                        scalar2=None, op0=ALU.mult))
                    k.vec(lambda e, Ti=Ti, base=base, w=w: e.tensor_scalar(
                        out=t2[:, 0:w], in0=Ti[:, base:base + w], scalar1=smi[:, 0:1],
                        scalar2=None, op0=ALU.mult))
                    k.vec(lambda e, Tr=Tr, base=base, m=m, w=w: e.tensor_sub(
                        out=Tr[:, base + m:base + m + w], in0=t1[:, 0:w], in1=t2[:, 0:w]))
                    k.vec(lambda e, Tr=Tr, base=base, w=w: e.tensor_scalar(
                        out=t1[:, 0:w], in0=Tr[:, base:base + w], scalar1=smi[:, 0:1],
                        scalar2=None, op0=ALU.mult))
                    k.vec(lambda e, Ti=Ti, base=base, w=w: e.tensor_scalar(
                        out=t2[:, 0:w], in0=Ti[:, base:base + w], scalar1=smr[:, 0:1],
                        scalar2=None, op0=ALU.mult))
                    k.vec(lambda e, Ti=Ti, base=base, m=m, w=w: e.tensor_add(
                        out=Ti[:, base + m:base + m + w], in0=t1[:, 0:w], in1=t2[:, 0:w]))
                    m *= 2
        nvt = k.sbuf("nvt", [128, 512])
        carr = k.sbuf("carr", [128, 4]); cari = k.sbuf("cari", [128, 4])
        inr = k.sbuf("inr", [128, 1]); ini = k.sbuf("ini", [128, 1])
        k.vec(lambda e: e.memset(carr[:], 0.0), sig=True)
        k.vec(lambda e: e.memset(cari[:], 0.0), sig=True)
        bur = [k.sbuf("bur0", [128, 512]), k.sbuf("bur1", [128, 512])]
        bui = [k.sbuf("bui0", [128, 512]), k.sbuf("bui1", [128, 512])]
        gre = k.sbuf("gre", [128, 512]); gim = k.sbuf("gim", [128, 512])
        cre = k.sbuf("cre", [128, 512]); cim = k.sbuf("cim", [128, 512])
        srh = k.sbuf("srh", [128, 4 * 512], BF16); srl = k.sbuf("srl", [128, 4 * 512], BF16)
        sih = k.sbuf("sih", [128, 4 * 512], BF16); sil = k.sbuf("sil", [128, 4 * 512], BF16)
        crh = k.sbuf("crh", [128, 4 * 512], BF16); crl = k.sbuf("crl", [128, 4 * 512], BF16)
        cih = k.sbuf("cih", [128, 4 * 512], BF16); cil = k.sbuf("cil", [128, 4 * 512], BF16)
        dut = k.sbuf("dut", [128, 512])
        dt2 = k.sbuf("dt2", [128, 512])
        t_relu_prev = [None]
        psY = [k.psum("psY0"), k.psum("psY1")]
        bar()
        t_lb = {}
        t_bufree = [None]
        t_lb[(0, 0)] = (k.dma("sync", bur[0][:], sp_bur[0:128, 0:512]),
                        k.dma("gpsimd", bui[0][:], sp_bui[0:128, 0:512]))
        t_split = {}
        t_ymm_all = [None]
        t_cload = [None]
        t_xadd = [None, None]
        t_duld = {}
        for tc in range(4):
            tsl = slice(tc * 512, (tc + 1) * 512)
            for c in range(4):
                base = c * 512
                nc_ = (tc, c)
                nxt = (tc, c + 1) if c < 3 else (tc + 1, 0)
                b_r = bur[c % 2]; b_i = bui[c % 2]
                tlr, tli = t_lb[nc_]
                k.vec(lambda e, base=base: e.tensor_mul(out=t1[:], in0=Tre[:, base:base + 512],
                                                        in1=Tre[:, base:base + 512]),
                      wait=[t_ymm_all[0]])
                k.vec(lambda e, base=base: e.tensor_mul(out=t2[:], in0=Tim[:, base:base + 512],
                                                        in1=Tim[:, base:base + 512]))
                k.vec(lambda e: e.tensor_add(out=t1[:], in0=t1[:], in1=t2[:]))
                k.vec(lambda e: e.reciprocal(out=nvt[:], in_=t1[:]))
                k.vec(lambda e, b_r=b_r, base=base: e.tensor_mul(
                    out=gre[:], in0=b_r[:], in1=Tre[:, base:base + 512]),
                    wait=[tlr])
                k.vec(lambda e, b_i=b_i, base=base: e.tensor_mul(
                    out=t1[:], in0=b_i[:], in1=Tim[:, base:base + 512]), wait=[tli])
                k.vec(lambda e: e.tensor_add(out=gre[:], in0=gre[:], in1=t1[:]))
                k.vec(lambda e: e.tensor_mul(out=gre[:], in0=gre[:], in1=nvt[:]))
                k.vec(lambda e, b_i=b_i, base=base: e.tensor_mul(
                    out=gim[:], in0=b_i[:], in1=Tre[:, base:base + 512]))
                t_bufree[0] = k.vec(lambda e, b_r=b_r, base=base: e.tensor_mul(
                    out=t1[:], in0=b_r[:], in1=Tim[:, base:base + 512]), sig=True)
                k.vec(lambda e: e.tensor_sub(out=gim[:], in0=gim[:], in1=t1[:]))
                k.vec(lambda e: e.tensor_mul(out=gim[:], in0=gim[:], in1=nvt[:]))
                if nxt[0] < 4:
                    nb = bur[(c + 1) % 2]; nbi_ = bui[(c + 1) % 2]
                    t_lb[nxt] = (
                        k.dma("sync", nb[:], sp_bur[nxt[1] * 128:(nxt[1] + 1) * 128,
                                                    nxt[0] * 512:(nxt[0] + 1) * 512],
                              wait=[t_bufree[0]]),
                        k.dma("gpsimd", nbi_[:], sp_bui[nxt[1] * 128:(nxt[1] + 1) * 128,
                                                        nxt[0] * 512:(nxt[0] + 1) * 512],
                              wait=[t_bufree[0]]))
                k.vec(lambda e, c=c: e.tensor_scalar(out=inr[:], in0=carr[:, c:c + 1],
                                                     scalar1=lbr[:, c:c + 1], scalar2=None,
                                                     op0=ALU.mult))
                k.vec(lambda e, c=c: e.tensor_scalar(out=t1[:, 0:1], in0=cari[:, c:c + 1],
                                                     scalar1=lbi[:, c:c + 1], scalar2=None,
                                                     op0=ALU.mult))
                k.vec(lambda e: e.tensor_sub(out=inr[:], in0=inr[:], in1=t1[:, 0:1]))
                k.vec(lambda e, c=c: e.tensor_scalar(out=ini[:], in0=carr[:, c:c + 1],
                                                     scalar1=lbi[:, c:c + 1], scalar2=None,
                                                     op0=ALU.mult))
                k.vec(lambda e, c=c: e.tensor_scalar(out=t1[:, 0:1], in0=cari[:, c:c + 1],
                                                     scalar1=lbr[:, c:c + 1], scalar2=None,
                                                     op0=ALU.mult))
                k.vec(lambda e: e.tensor_add(out=ini[:], in0=ini[:], in1=t1[:, 0:1]))
                k.vec(lambda e: e.tensor_tensor_scan(out=cre[:], data0=onet[:], data1=gre[:],
                                                     initial=inr[:], op0=ALU.mult,
                                                     op1=ALU.add))
                k.vec(lambda e: e.tensor_tensor_scan(out=cim[:], data0=onet[:], data1=gim[:],
                                                     initial=ini[:], op0=ALU.mult,
                                                     op1=ALU.add))
                k.vec(lambda e, base=base: e.tensor_mul(out=t1[:], in0=cre[:],
                                                        in1=Tre[:, base:base + 512]))
                k.vec(lambda e, base=base: e.tensor_mul(out=t2[:], in0=cim[:],
                                                        in1=Tim[:, base:base + 512]))
                tsr = k.vec(lambda e: e.tensor_sub(out=t1[:], in0=t1[:], in1=t2[:]), sig=True)
                k.vec(lambda e, c=c: e.tensor_copy(out=carr[:, c:c + 1], in_=t1[:, 511:512]))
                th1 = k.sca(lambda e, base=base: e.activation(
                    out=srh[:, base:base + 512], in_=t1[:], func=AF.Copy, bias=0.0,
                    scale=1.0), wait=[tsr, t_ymm_all[0]], sig=True)
                tl1 = k.vec(lambda e, base=base: e.scalar_tensor_tensor(
                    out=srl[:, base:base + 512], in0=t1[:], scalar=1.0,
                    in1=srh[:, base:base + 512], op0=ALU.mult, op1=ALU.subtract),
                    wait=[th1], sig=True)
                k.vec(lambda e, base=base: e.tensor_mul(out=t1[:], in0=cim[:],
                                                        in1=Tre[:, base:base + 512]))
                k.vec(lambda e, base=base: e.tensor_mul(out=t2[:], in0=cre[:],
                                                        in1=Tim[:, base:base + 512]))
                tsi = k.vec(lambda e: e.tensor_add(out=t1[:], in0=t1[:], in1=t2[:]), sig=True)
                k.vec(lambda e, c=c: e.tensor_copy(out=cari[:, c:c + 1], in_=t1[:, 511:512]))
                th2 = k.sca(lambda e, base=base: e.activation(
                    out=sih[:, base:base + 512], in_=t1[:], func=AF.Copy, bias=0.0,
                    scale=1.0), wait=[tsi], sig=True)
                tl2 = k.vec(lambda e, base=base: e.scalar_tensor_tensor(
                    out=sil[:, base:base + 512], in0=t1[:], scalar=1.0,
                    in1=sih[:, base:base + 512], op0=ALU.mult, op1=ALU.subtract),
                    wait=[th2], sig=True)
                t_split[nc_] = tl2
            for dhh in range(2):
                dsl = slice(dhh * 512, (dhh + 1) * 512)
                tcl = []
                for c in range(4):
                    csl = slice(c * 128, (c + 1) * 128)
                    tcl.append(k.dma("sync", crh[:, c * 512:(c + 1) * 512],
                                     csp_d["reh"][csl, dsl],
                                     wait=[t_xadd[0], t_xadd[1]]))
                    tcl.append(k.dma("gpsimd", crl[:, c * 512:(c + 1) * 512],
                                     csp_d["rel"][csl, dsl]))
                    tcl.append(k.dma("sync", cih[:, c * 512:(c + 1) * 512],
                                     csp_d["imh"][csl, dsl]))
                    tcl.append(k.dma("gpsimd", cil[:, c * 512:(c + 1) * 512],
                                     csp_d["iml"][csl, dsl]))
                for dh in range(dhh * 4, dhh * 4 + 4):
                    ps = psY[dh % 2]
                    dho = dh - dhh * 4
                    t_duld[dh] = k.dma("gpsimd", dut[:],
                                       sp_du[dh * 128:(dh + 1) * 128, tsl],
                                       wait=[t_xadd[0], t_xadd[1]])
                    nmm = 0
                    for c in range(4):
                        for hsrc, lsrc, sh, sl_ in ((crh, crl, srh, srl),
                                                    (cih, cil, sih, sil)):
                            lh = hsrc[:, c * 512 + dho * 128: c * 512 + dho * 128 + 128]
                            ll = lsrc[:, c * 512 + dho * 128: c * 512 + dho * 128 + 128]
                            rh = sh[:, c * 512:(c + 1) * 512]
                            rl = sl_[:, c * 512:(c + 1) * 512]
                            w0 = ([t_split[(tc, 3)], t_xadd[dh % 2]] + tcl) if nmm == 0 else []
                            k.tens(lambda e, ps=ps, lh=lh, rh=rh, first=(nmm == 0): e.matmul(
                                ps[:], lh, rh, start=first, stop=False), wait=w0)
                            k.tens(lambda e, ps=ps, lh=lh, rl=rl: e.matmul(
                                ps[:], lh, rl, start=False, stop=False))
                            last = (nmm == 7)
                            t = k.tens(lambda e, ps=ps, ll=ll, rh=rh, last=last: e.matmul(
                                ps[:], ll, rh, start=False, stop=last), sig=last)
                            nmm += 1
                    tv1 = k.vec(lambda e, ps=ps: e.tensor_add(
                        out=dt2[:], in0=dut[:], in1=ps[:]),
                        wait=[t, t_duld[dh], t_relu_prev[0]], sig=True)
                    tv2 = k.sca(lambda e: e.activation(
                        out=dt2[:], in_=dt2[:], func=AF.Relu,
                        bias=0.0, scale=1.0), wait=[tv1], sig=True)
                    t_relu_prev[0] = tv2
                    t_xadd[dh % 2] = k.vec(lambda e, dh=dh, tc=tc: e.tensor_tensor(
                        out=X[:, dh * T + tc * 512: dh * T + tc * 512 + 512],
                        in0=X[:, dh * T + tc * 512: dh * T + tc * 512 + 512],
                        in1=dt2[:], op=ALU.add), wait=[tv2], sig=True)
            t_ymm_all[0] = t_xadd[1]
        bar()
        k.free("psY1", "psY0")
        k.free("dt2", "dut", "cil", "cih", "crl", "crh",
               "sil", "sih", "srl", "srh", "cim", "cre", "gim", "gre",
               "bui1", "bui0", "bur1", "bur0", "ini", "inr", "cari", "carr", "nvt",
               "onet", "sst2", "sst1", "smi", "smr", "Tim", "Tre")

    ALL_PHASES = ("bias", "ln0", "ar1", "attn", "xq", "ffn0mm", "ffn0ew",
                  "ffn0down", "ln1", "ffn1", "ssm")
    if phases is None:
        phases = ALL_PHASES
    lambox = {}

    def run_phase(name):
        if name == "bias":
            emit_bias_prep()
        elif name == "ln0" or name == "ln1":
            emit_ln()
        elif name == "ar1":
            emit_allreduce_max([S_XG])
        elif name == "attn":
            emit_attention()
        elif name == "xq":
            emit_xq()
        elif name == "ffn0mm":
            emit_ffn0_mm()
            emit_allreduce_max([S_UG, S_GG])
            emit_scales_after_ar2()
        elif name == "ffn0ew":
            emit_ffn0_ew_B()
            emit_allreduce_max([S_QG])
            emit_scales_after_ar3()
            emit_ffn0_ew_CD()
            emit_allreduce_max([S_PG])
            emit_scales_after_ar4()
            emit_ffn0_ew_E()
        elif name == "ffn0down":
            emit_ffn0_down_prep()
            emit_ffn0_down()
        elif name == "ffn1":
            emit_ffn1()
        elif name == "ssm":
            lam = emit_ssm_prep()
            lr, li, dtv, av, th, lbr, lbi, lir, lii, rr_, ri_ = lam
            emit_ssm_bu(lbr, lbi, lir, lii, rr_, ri_)
            emit_ssm_core(lbr, lbi, lir, lii)

    for name in phases:
        run_phase(name)
    for nm, buf, dt_ in (("XH", XH, BF16), ("XL", XL, BF16)):
        if nm in k.dbg_names:
            dt = k.dout("dbg_" + nm, [128, NCH * T], dt_)
            k.dma("sync", dt[:], buf[:])
    if "panel" in k.dbg_names:
        dt = k.dout("dbg_panel", [1, 64], F32)
        k.dma("sync", dt[:], panel[:])
    if "pbc" in k.dbg_names:
        dt = k.dout("dbg_pbc", [128, 16], F32)
        k.dma("sync", dt[:], pbc[:])
    for nm, dr, shp, dt_ in (("w1q", w1q_d, [D, DFF1], BF16),
                             ("w2q", w2q_d, [DFF1, D], BF16),
                             ("spg2k", sp_g2k, [IFF0, T], I16),
                             ("spup", sp_up, [IFF0, T], F32),
                             ("spgate", sp_gate, [IFF0, T], F32),
                             ("spq", sp_q, [IFF0, T], F32),
                             ("spupk", sp_upk, [IFF0, T], I16),
                             ("spoh", sp_oh, [IFF0, T], BF16),
                             ("spol", sp_ol, [IFF0, T], BF16)):
        if nm in k.dbg_names:
            dt = k.dout("dbg_" + nm, shp, dt_)
            k.dma("sync", dt[:], dr[:])
    for c in range(NCH):
        k.dma("sync", out[c * 128:(c + 1) * 128, :], X[:, c * T:(c + 1) * T])
    bar()

    p.check()
    handles = {}
    from contextlib import ExitStack
    es = ExitStack()
    for s in SEMS:
        handles[s] = es.enter_context(nc.semaphore("sem_" + s))
    with nc.Block() as block:
        p.emit(nc, block, handles)
    es.close()
    return k



def host_inputs(inputs):
    import ml_dtypes
    bf16 = ml_dtypes.bfloat16
    C = np.ascontiguousarray
    f32 = np.float32
    g = lambda n: np.asarray(inputs[n], dtype=f32)
    halfA = np.zeros((128, 1), f32); halfA[:64] = 1.0
    halfB = np.zeros((128, 1), f32); halfB[64:] = 1.0
    shared = {
        "wqB": C(g("L0_attn_wq").T.reshape(D, 8, 128).transpose(1, 0, 2)),
        "wkB": C(g("L0_attn_wk").T.reshape(D, 8, 128).transpose(1, 0, 2)),
        "wvB": C(g("L0_attn_wv").T.reshape(D, 8, 128).transpose(1, 0, 2)),
        "woT": C(g("L0_attn_wo").T),
        "bq": g("L0_attn_bq").reshape(1, D), "bk": g("L0_attn_bk").reshape(1, D),
        "bqcol": C(g("L0_attn_bq").reshape(8, 128).T),
        "bkcol": C(g("L0_attn_bk").reshape(8, 128).T),
        "bvcol": C(g("L0_attn_bv").reshape(8, 128).T),
        "bocol": C(g("L0_attn_bo").reshape(8, 128).T),
        "b2col": C(g("L1_ffn_b2").reshape(8, 128).T),
        "bv": g("L0_attn_bv").reshape(1, D), "bo": g("L0_attn_bo").reshape(1, D),
        "relb": g("L0_attn_rel_bias"),
        "onehotT": host_bucket_onehot(),
        "identc": np.eye(128, dtype=f32).astype(bf16),
        "identf": np.eye(128, dtype=f32),
        "ones_r": np.ones((1, 128), f32),
        "onesall": np.ones((128, 128), f32),
        "ones_c": np.ones((128, 1), f32),
        "halfA_c": halfA, "halfB_c": halfB,
        "onesrow_b": np.ones((1, 512), f32).astype(bf16),
        "cstv_d": np.tile(np.array([[1e-5, np.pi / 2]], f32), (128, 1)),
        "wup": g("L0_ffn_w_up"), "wgate": g("L0_ffn_w_gate"),
        "wdown": g("L0_ffn_w_down"),
        "lam_re": C(g("L1_attn_Lambda_re").reshape(4, 128).T),
        "lam_im": C(g("L1_attn_Lambda_imag").reshape(4, 128).T),
        "stepv": C(g("L1_attn_step").reshape(-1).reshape(4, 128).T),
        "Bre": C(np.asarray(inputs["L1_attn_B"])[:, :, 0].astype(f32)),
        "Bim": C(np.asarray(inputs["L1_attn_B"])[:, :, 1].astype(f32)),
        "CreT": C(np.asarray(inputs["L1_attn_C"])[:, :, 0].T.astype(f32)),
        "CimT": C(np.asarray(inputs["L1_attn_C"])[:, :, 1].T.astype(f32)),
        "Dcol": C(g("L1_attn_D").reshape(8, 128).T),
        "Dvec": g("L1_attn_D").reshape(1, D),
        "w1T": C(g("L1_ffn_W1").T), "b1": g("L1_ffn_b1").reshape(1, DFF1),
        "b1col": C(g("L1_ffn_b1").reshape(32, 128).T),
        "w2T": C(g("L1_ffn_W2").T), "b2": g("L1_ffn_b2").reshape(1, D),
    }
    x = np.asarray(inputs["L0_attn_x"], dtype=f32)
    per_core = [dict(shared, x=C(x[b].T)) for b in range(x.shape[0])]
    return per_core


def kernel(**inputs):
    k = build_kernel(n_cores=8)
    in_maps = host_inputs(inputs)
    res = run_bass_kernel_spmd(k.nc, in_maps, core_ids=list(range(8)))
    outs = [np.ascontiguousarray(r["out"].T) for r in res.results]
    return np.stack(outs, axis=0)
